# version 08
# baseline (speedup 1.0000x reference)
import numpy as np
import ml_dtypes
from contextlib import ExitStack

import concourse.bass as bass
import concourse.mybir as mybir
from concourse.bass_utils import run_bass_kernel_spmd

F32 = mybir.dt.float32
BF16 = mybir.dt.bfloat16
AF = mybir.ActivationFunctionType
ALU = mybir.AluOpType
AX = mybir.AxisListType

S = 2048
D = 1024
NCH = 8
NE = 9
NT = 16
EXP_SHIFT = -15.0
EPS = 1e-5

ENGINES = ("sync", "gpsimd", "scalar", "vector", "tensor")


class Prog:

    SELF_FENCE = {"vector": "tV", "scalar": "tA", "gpsimd": "tG"}

    def __init__(self, nc, es):
        self.nc = nc
        self.es = es
        self.q = {e: [] for e in ENGINES}
        self.sems = {}
        self.cnt = {}
        self._lastwait = {}

    def sem(self, name):
        self.sems[name] = self.es.enter_context(self.nc.semaphore(name))
        self.cnt[name] = 0
        return name

    def wait(self, eng, sem, val):
        if val <= 0:
            return
        if sem == self.SELF_FENCE.get(eng):
            if self._lastwait.get(eng, 0) >= val:
                return
            self._lastwait[eng] = val
        h = self.sems[sem]
        self.q[eng].append(lambda e, h=h, v=val: e.wait_ge(h, v))

    def op(self, eng, build, inc=None, n=1):
        own = self.SELF_FENCE.get(eng)
        if own is not None:
            self.wait(eng, own, self.cnt[own])
            if inc is None:
                inc, n = own, 1
        if inc is None:
            self.q[eng].append(build)
            return None
        self.cnt[inc] += n
        h = self.sems[inc]
        self.q[eng].append(lambda e, b=build, h=h, k=n: b(e).then_inc(h, k))
        return self.cnt[inc]

    def dma(self, eng, out, in_, sem):
        self.cnt[sem] += 16
        h = self.sems[sem]
        self.q[eng].append(lambda e, o=out, i=in_, h=h: e.dma_start(out=o, in_=i).then_inc(h, 16))
        return self.cnt[sem]

    def c(self, sem):
        return self.cnt[sem]

    def flush_block(self):
        q = self.q
        self.q = {e: [] for e in ENGINES}
        with self.nc.Block() as blk:
            blk.sync(lambda e, fns=q["sync"]: [f(e) for f in fns])
            blk.gpsimd(lambda e, fns=q["gpsimd"]: [f(e) for f in fns])
            blk.scalar(lambda e, fns=q["scalar"]: [f(e) for f in fns])
            blk.vector(lambda e, fns=q["vector"]: [f(e) for f in fns])
            blk.tensor(lambda e, fns=q["tensor"]: [f(e) for f in fns])


class Ring:

    def __init__(self, prog, slots):
        self.p = prog
        self.slots = list(slots)
        self.n = len(self.slots)
        self.free_at = [[] for _ in range(self.n)]
        self.i = 0

    def acquire(self, eng):
        s = self.i % self.n
        self.i += 1
        for (sem, tick) in self.free_at[s]:
            self.p.wait(eng, sem, tick)
        self.free_at[s] = []
        return self.slots[s], s

    def release(self, s, sem, tick):
        self.free_at[s].append((sem, tick))


class DmaRing:

    def __init__(self, prog, slots, name):
        self.p = prog
        self.slots = list(slots)
        self.n = len(self.slots)
        self.sems = [prog.sem(f"{name}{i}") for i in range(self.n)]
        self.free_at = [[] for _ in range(self.n)]
        self.i = 0

    def load(self, eng, src):
        s = self.i % self.n
        self.i += 1
        for (sem, tick) in self.free_at[s]:
            self.p.wait(eng, sem, tick)
        self.free_at[s] = []
        slot = self.slots[s]
        cnt = self.p.dma(eng, slot[:, :], src, self.sems[s])
        return slot, s, (self.sems[s], cnt)

    def wait_loaded(self, eng, token):
        self.p.wait(eng, token[0], token[1])

    def release(self, s, sem, tick):
        self.free_at[s].append((sem, tick))


def build_nc():
    nc = bass.Bass("TRN2", target_bir_lowering=False)

    d = {}
    def din(name, shape, dt=F32):
        d[name] = nc.dram_tensor(name, shape, dt, kind="ExternalInput").ap()
    din("x", [S, D])
    din("wqr", [8, 128, 8, 128]); din("wkr", [8, 128, 8, 128])
    din("wor", [8, 128, 8, 128]); din("wv", [D, D])
    din("bqr", [128, 8]); din("bkr", [128, 8]); din("bor", [128, 8])
    din("bv", [1, D])
    din("wr1r", [2, 128, 8, 128]); din("wr2", [256, 8])
    din("br1r", [128, 2]); din("br2", [1, 8])
    din("w1r", [NE, 8, 128, 4096])
    din("w2r", [NE, 2, 4, 128, 4096])
    din("b1r", [128, NE * 32]); din("b2r", [128, NE * 8])
    din("ropec", [128, S]); din("ropes", [128, S])
    din("prm", [128, 128], BF16)
    din("identb", [128, 128], BF16)
    din("identf", [128, 128])
    din("trit", [128, 896], BF16)
    din("sel8", [8, 1024], BF16)
    y = nc.dram_tensor("y", [S, D], F32, kind="ExternalOutput").ap()
    w1bf = nc.dram_tensor("w1bf", [NE, 8, 128, 4096], BF16).ap()
    w2bf = nc.dram_tensor("w2bf", [NE, 2, 4, 128, 4096], BF16).ap()

    es = ExitStack()
    E = es.enter_context
    p = Prog(nc, es)
    dC = p.sem("dC")
    dV = p.sem("dV")
    dWo = p.sem("dWo")
    dWr = p.sem("dWr")
    dR = p.sem("dR")
    dPC = [p.sem(f"dPC{e}") for e in range(NE)]
    tT = p.sem("tT"); tV = p.sem("tV"); tA = p.sem("tA"); tG = p.sem("tG")

    identf = E(nc.sbuf_tensor("identf_s", [128, 128], F32))
    ones_b = E(nc.sbuf_tensor("ones_b", [128, 1], BF16))
    ones_f = E(nc.sbuf_tensor("ones_f", [1, 128], F32))
    bq_sb = E(nc.sbuf_tensor("bq_sb", [128, 8], F32))
    bk_sb = E(nc.sbuf_tensor("bk_sb", [128, 8], F32))
    bo_sb = E(nc.sbuf_tensor("bo_sb", [128, 8], F32))
    br1_sb = E(nc.sbuf_tensor("br1_sb", [128, 2], F32))
    br2row = E(nc.sbuf_tensor("br2row", [1, 8], F32))
    b1_sb = E(nc.sbuf_tensor("b1_sb", [128, NE * 32], F32))
    b2_sb = E(nc.sbuf_tensor("b2_sb", [128, NE * 8], F32))
    wT = E(nc.sbuf_tensor("wT", [8, S], BF16))
    cst = E(nc.sbuf_tensor("cst", [128, 4], F32))
    sel8 = E(nc.sbuf_tensor("sel8_s", [8, 1024], BF16))

    identb = E(nc.sbuf_tensor("identb_s", [128, 128], BF16))
    st_B = ExitStack()
    ctx = st_B.enter_context(nc.sbuf_tensor("ctx", [128, NCH * S], BF16))
    st_A = ExitStack()
    prm = st_A.enter_context(nc.sbuf_tensor("prm_s", [128, 128], BF16))
    trit = st_A.enter_context(nc.sbuf_tensor("trit_s", [128, 896], BF16))
    bvrow = st_A.enter_context(nc.sbuf_tensor("bvrow", [1, D], F32))
    xbf = st_A.enter_context(nc.sbuf_tensor("xbf", [128, NCH * S], BF16))
    vbuf = st_A.enter_context(nc.sbuf_tensor("vbuf", [128, NT * D], BF16))
    ropec = st_A.enter_context(nc.sbuf_tensor("ropec_s", [128, S], F32))
    ropes = st_A.enter_context(nc.sbuf_tensor("ropes_s", [128, S], F32))
    st_R1 = ExitStack()
    st_R2 = ExitStack()

    def precast_expert(e_i):
        for hh2 in range(2):
            p.dma("gpsimd", w1bf[e_i][hh2 * 4:(hh2 + 1) * 4], d["w1r"][e_i][hh2 * 4:(hh2 + 1) * 4], dPC[e_i])
            p.dma("gpsimd", w2bf[e_i][hh2], d["w2r"][e_i][hh2], dPC[e_i])

    for name, dst in (("identf", identf), ("ropec", ropec), ("ropes", ropes),
                      ("prm", prm), ("identb", identb), ("trit", trit),
                      ("bqr", bq_sb), ("bkr", bk_sb), ("bor", bo_sb),
                      ("br1r", br1_sb), ("br2", br2row), ("bv", bvrow),
                      ("sel8", sel8)):
        p.dma("sync", dst[:, :], d[name], dC)
    p.dma("sync", b1_sb[:, :], d["b1r"], dC)
    p.dma("sync", b2_sb[:, :], d["b2r"], dC)
    idf_d = p.c(dC)
    consts_d = p.c(dC)
    p.op("gpsimd", lambda e: e.memset(ones_b[:, :], 1.0), inc=tG)
    g_ones = p.c(tG)
    p.op("vector", lambda e: e.memset(ones_f[:, :], 1.0), inc=tV)
    p.op("vector", lambda e: e.memset(cst[:, 0:1], EXP_SHIFT), inc=tV)
    p.op("vector", lambda e: e.memset(cst[:, 1:2], 0.125), inc=tV)
    p.op("vector", lambda e: e.memset(cst[:, 2:3], EPS), inc=tV)
    p.op("vector", lambda e: e.memset(cst[:, 3:4], 0.5), inc=tV)
    v_ones = p.c(tV)

    with ExitStack() as ph:
        xstage = [ph.enter_context(nc.sbuf_tensor(f"xstage{i}", [128, D], F32)) for i in range(2)]
        wv_sb = ph.enter_context(nc.sbuf_tensor("wv_sb", [128, 8 * D], BF16))
        bvb = ph.enter_context(nc.sbuf_tensor("bvb", [128, D], F32))
        pX = [ph.enter_context(nc.psum_tensor(f"pX{i}", [128, D], F32)) for i in range(2)]
        pV = [ph.enter_context(nc.psum_tensor(f"pV{i}", [128, 512], F32)) for i in range(2)]
        pBC = ph.enter_context(nc.psum_tensor("pBC", [128, 512], F32))

        xring = DmaRing(p, xstage, "dX")
        px_ring = Ring(p, pX)
        a_x0 = p.c(tA)
        xbf_ev = []
        p.wait("tensor", dC, idf_d)
        for tt in range(NT):
            slot, si, tok = xring.load("sync", d["x"][tt * 128:(tt + 1) * 128, :])
            xring.wait_loaded("tensor", tok)
            pxs, pxi = px_ring.acquire("tensor")
            for k in range(NCH):
                p.op("tensor",
                     lambda e, o=pxs[:, k * 128:(k + 1) * 128],
                     i=slot[:, k * 128:(k + 1) * 128]:
                     e.transpose(o, i, identf[:, :]),
                     inc=(tT if k == NCH - 1 else None))
            tp = p.c(tT)
            xring.release(si, tT, tp)
            p.wait("scalar", tT, tp)
            out_ap = xbf.rearrange("p (c s) -> p c s", c=NCH)[:, :, tt * 128:(tt + 1) * 128]
            ev = p.op("scalar", lambda e, o=out_ap, i=pxs[:, :]:
                      e.activation(out=o, in_=i, func=AF.Copy), inc=tA)
            xbf_ev.append(ev)
            px_ring.release(pxi, tA, ev)

        p.wait("tensor", dC, consts_d)
        p.wait("tensor", tV, v_ones)
        last_bc = 0
        for ng in range(2):
            if ng == 1:
                p.wait("tensor", tV, last_bc)
            bt = p.op("tensor", lambda e, o=pBC[:, :], i=bvrow[0:1, ng * 512:(ng + 1) * 512]:
                      e.matmul(o, ones_f[:, :], i, start=True, stop=True), inc=tT)
            p.wait("vector", tT, bt)
            last_bc = p.op("vector", lambda e, o=bvb[:, ng * 512:(ng + 1) * 512], i=pBC[:, :]:
                           e.tensor_copy(o, i), inc=tV)

        for k in range(NCH):
            p.dma("gpsimd", wv_sb[:, k * D:(k + 1) * D],
                  d["wv"][k * 128:(k + 1) * 128, :], dV)
        wv_d = p.c(dV)
        precast_expert(0)
        precast_expert(1)
        p.wait("tensor", dV, wv_d)
        pv_ring = Ring(p, pV)
        gi = 0
        for tt in range(NT):
            p.wait("tensor", tA, xbf_ev[tt])
            for ng in range(2):
                slot, si = pv_ring.acquire("tensor")
                for k in range(NCH):
                    p.op("tensor",
                         lambda e, o=slot[:, :],
                         l=xbf[:, k * S + tt * 128: k * S + tt * 128 + 128],
                         r=wv_sb[:, k * D + ng * 512: k * D + ng * 512 + 512],
                         st=(k == 0), sp=(k == NCH - 1):
                         e.matmul(o, l, r, start=st, stop=sp),
                         inc=(tT if k == NCH - 1 else None))
                mm = p.c(tT)
                p.wait("vector", tT, mm)
                if gi == 0:
                    p.wait("vector", tV, last_bc)
                ev = p.op("vector",
                          lambda e, o=vbuf[:, tt * D + ng * 512: tt * D + ng * 512 + 512],
                          i=slot[:, :], b=bvb[:, ng * 512:(ng + 1) * 512]:
                          e.scalar_tensor_tensor(o, i, 1.0, b, ALU.mult, ALU.add),
                          inc=tV)
                pv_ring.release(si, tV, ev)
                gi += 1
        p.flush_block()

    with ExitStack() as ph:
        wq_sb = [ph.enter_context(nc.sbuf_tensor(f"wq{i}", [128, 1024], BF16)) for i in range(2)]
        wk_sb = [ph.enter_context(nc.sbuf_tensor(f"wk{i}", [128, 1024], BF16)) for i in range(2)]
        q_sb = ph.enter_context(nc.sbuf_tensor("q_sb", [128, S], BF16))
        k_sb = ph.enter_context(nc.sbuf_tensor("k_sb", [128, S], BF16))
        qr_sb = ph.enter_context(nc.sbuf_tensor("qr_sb", [128, S], BF16))
        kr_sb = ph.enter_context(nc.sbuf_tensor("kr_sb", [128, S], BF16))
        t1f = [ph.enter_context(nc.sbuf_tensor(f"t1f{i}", [128, 512], F32)) for i in range(2)]
        et_sb = [ph.enter_context(nc.sbuf_tensor(f"et{i}", [128, 512], BF16)) for i in range(3)]
        sums_sb = ph.enter_context(nc.sbuf_tensor("sums_sb", [1, 512], F32))
        recip_sb = ph.enter_context(nc.sbuf_tensor("recip_sb", [1, 512], F32))
        recipb = [ph.enter_context(nc.sbuf_tensor(f"recipb{i}", [128, 512], F32)) for i in range(2)]
        pS = [ph.enter_context(nc.psum_tensor(f"pS{i}", [128, 512], F32)) for i in range(2)]
        pCt = [ph.enter_context(nc.psum_tensor(f"pCt{i}", [128, 512], F32)) for i in range(3)]
        pSm = [ph.enter_context(nc.psum_tensor(f"pSm{i}", [1, 512], F32)) for i in range(2)]
        pMs = ph.enter_context(nc.psum_tensor("pMs", [128, 512], F32))

        for eng in ("tensor", "scalar", "vector", "gpsimd"):
            p.wait(eng, dC, consts_d)

        wq_ring = DmaRing(p, wq_sb, "dQ")
        wk_ring = DmaRing(p, wk_sb, "dK")
        ps_ring = Ring(p, pS)
        ct_ring = Ring(p, pCt)
        sm_ring = Ring(p, pSm)
        t1_ring = Ring(p, t1f)
        et_ring = Ring(p, et_sb)
        rb_ring = Ring(p, recipb)
        pms_free = 0
        recip_free = 0
        prev_rope_done = 0
        prev_scores_done = 0

        for j in range(NCH):
            wqs, wqi, qtok = wq_ring.load("gpsimd", d["wqr"][j].rearrange("p k c -> p (k c)"))
            wks, wki, ktok = wk_ring.load("gpsimd", d["wkr"][j].rearrange("p k c -> p (k c)"))
            if j + 2 < NE:
                precast_expert(j + 2)
            wq_ring.wait_loaded("tensor", qtok)
            wk_ring.wait_loaded("tensor", ktok)

            evac_ticks = {}
            for (nm, wsb, src, dst, bias_sb) in (
                    ("q", wqs, q_sb, qr_sb, bq_sb), ("k", wks, k_sb, kr_sb, bk_sb)):
                for g in range(4):
                    slot, si = ps_ring.acquire("tensor")
                    for k in range(NCH):
                        p.op("tensor",
                             lambda e, o=slot[:, :],
                             l=wsb[:, k * 128:(k + 1) * 128],
                             r=xbf[:, k * S + g * 512: k * S + g * 512 + 512],
                             st=(k == 0), sp=(k == NCH - 1):
                             e.matmul(o, l, r, start=st, stop=sp),
                             inc=(tT if k == NCH - 1 else None))
                    mm = p.c(tT)
                    p.wait("scalar", tT, mm)
                    if j > 0 and g == 0:
                        p.wait("scalar", tG, prev_rope_done)
                    ev = p.op("scalar",
                              lambda e, o=src[:, g * 512:(g + 1) * 512], i=slot[:, :],
                              b=bias_sb[:, j:j + 1]:
                              e.activation(out=o, in_=i, func=AF.Identity, bias=b),
                              inc=tA)
                    evac_ticks[(nm, g)] = ev
                    ps_ring.release(si, tA, ev)
                for g in range(4):
                    slot, si = ps_ring.acquire("tensor")
                    p.wait("tensor", tA, evac_ticks[(nm, g)])
                    rt = p.op("tensor",
                              lambda e, o=slot[:, :], r=src[:, g * 512:(g + 1) * 512]:
                              e.matmul(o, prm[:, :], r, start=True, stop=True),
                              inc=tT)
                    t1s, t1i = t1_ring.acquire("vector")
                    p.wait("vector", tT, rt)
                    vt = p.op("vector",
                              lambda e, o=t1s[:, :], i=slot[:, :],
                              s=ropes[:, g * 512:(g + 1) * 512]:
                              e.tensor_tensor(o, i, s, ALU.mult), inc=tV)
                    ps_ring.release(si, tV, vt)
                    if j > 0 and g == 0:
                        p.wait("gpsimd", tT, prev_scores_done)
                    p.wait("gpsimd", tA, evac_ticks[(nm, g)])
                    g1 = p.op("gpsimd",
                              lambda e, o=dst[:, g * 512:(g + 1) * 512],
                              i=src[:, g * 512:(g + 1) * 512],
                              c=ropec[:, g * 512:(g + 1) * 512]:
                              e.tensor_tensor(o, i, c, ALU.mult), inc=tG)
                    p.wait("gpsimd", tV, vt)
                    p.wait("gpsimd", tG, g1)
                    gt = p.op("gpsimd",
                              lambda e, o=dst[:, g * 512:(g + 1) * 512], t=t1s[:, :]:
                              e.tensor_tensor(o, o, t, ALU.add), inc=tG)
                    t1_ring.release(t1i, tG, gt)
            prev_rope_done = p.c(tG)
            p.wait("tensor", tG, prev_rope_done)

            def emit_av(pa):
                p.wait("tensor", tA, pa["ex"])
                p.op("tensor",
                     lambda e, o=pa["ctslot"][pa["r0"]:pa["r0"] + 64, :],
                     l=vbuf[:, pa["kt"] * D + pa["h"] * 64: pa["kt"] * D + pa["h"] * 64 + 64],
                     r=pa["ets"][:, :], st=(pa["kt"] == 0), sp=(pa["kt"] == pa["nkt"] - 1),
                     tp=(0, pa["r0"]):
                     e.matmul(o, l, r, start=st, stop=sp, tile_position=tp))
                sm = p.op("tensor",
                          lambda e, o=pa["smslot"][:, :], r=pa["ets"][:, :],
                          st=(pa["kt"] == 0), sp=(pa["kt"] == pa["nkt"] - 1):
                          e.matmul(o, ones_b[:, :], r, start=st, stop=sp),
                          inc=tT)
                et_ring.release(pa["eti"], tT, sm)
                return sm

            def emit_norm(pn):
                nonlocal recip_free, pms_free
                p.wait("tensor", tV, pn["rc"])
                p.wait("tensor", tV, pms_free)
                bt = p.op("tensor", lambda e:
                          e.matmul(pMs[:, :], ones_f[:, :], recip_sb[0:1, :],
                                   start=True, stop=True), inc=tT)
                recip_free = bt
                rslot, rbi = rb_ring.acquire("vector")
                p.wait("vector", tT, bt)
                bv_ = p.op("vector", lambda e, o=rslot[:, :]:
                           e.tensor_copy(o, pMs[:, :]), inc=tV)
                pms_free = bv_
                r0_, j_, win0_ = pn["r0"], pn["j"], pn["win0"]
                cv = p.op("vector",
                          lambda e, o=ctx[r0_:r0_ + 64, j_ * S + win0_: j_ * S + win0_ + 512],
                          i=pn["ctslot"][r0_:r0_ + 64, :], r=rslot[r0_:r0_ + 64, :]:
                          e.tensor_tensor(o, i, r, ALU.mult), inc=tV)
                rb_ring.release(rbi, tV, cv)
                ct_ring.release(pn["cti"], tV, cv)

            pend_av = None
            pend_norm = None
            for hh in range(2):
                h = 2 * j + hh
                r0 = 64 * hh
                for qh in range(2):
                    for ws in range(2):
                        win0 = qh * 1024 + ws * 512
                        nkt = qh * 8 + ws * 4 + 4
                        ctslot, cti = ct_ring.acquire("tensor")
                        smslot, smi = sm_ring.acquire("tensor")
                        for kt in range(nkt):
                            sslot, ssi = ps_ring.acquire("tensor")
                            ets, eti = et_ring.acquire("scalar")
                            off = kt * 128 - win0
                            if off >= 0:
                                p.op("tensor",
                                     lambda e, o=sslot[:, :],
                                     l=kr_sb[r0:r0 + 64, kt * 128:(kt + 1) * 128],
                                     r=qr_sb[r0:r0 + 64, win0:win0 + 512]:
                                     e.matmul(o, l, r, start=True, stop=False))
                                scol = 384 - off
                                st_t = p.op("tensor",
                                            lambda e, o=sslot[:, :],
                                            m=trit[:, scol:scol + 512]:
                                            e.matmul(o, identb[:, :], m, start=False,
                                                     stop=True), inc=tT)
                            else:
                                st_t = p.op("tensor",
                                            lambda e, o=sslot[:, :],
                                            l=kr_sb[r0:r0 + 64, kt * 128:(kt + 1) * 128],
                                            r=qr_sb[r0:r0 + 64, win0:win0 + 512]:
                                            e.matmul(o, l, r, start=True, stop=True),
                                            inc=tT)
                            p.wait("scalar", tT, st_t)
                            ex = p.op("scalar",
                                      lambda e, o=ets[:, :], i=sslot[:, :]:
                                      e.activation(out=o, in_=i, func=AF.Exp,
                                                   bias=cst[:, 0:1], scale=cst[:, 1:2]),
                                      inc=tA)
                            ps_ring.release(ssi, tA, ex)
                            if kt == 3 and pend_norm is not None:
                                emit_norm(pend_norm)
                                pend_norm = None
                            if pend_av is not None:
                                emit_av(pend_av)
                            pend_av = dict(ex=ex, ctslot=ctslot, smslot=smslot,
                                           kt=kt, nkt=nkt, ets=ets, eti=eti,
                                           h=h, r0=r0)
                        sm = emit_av(pend_av)
                        pend_av = None
                        p.wait("vector", tT, sm)
                        sc = p.op("vector", lambda e, i=smslot[:, :]:
                                  e.tensor_copy(sums_sb[:, :], i), inc=tV)
                        sm_ring.release(smi, tV, sc)
                        p.wait("vector", tT, recip_free)
                        rc = p.op("vector", lambda e: e.reciprocal(recip_sb[:, :], sums_sb[:, :]),
                                  inc=tV)
                        pend_norm = dict(rc=rc, ctslot=ctslot, cti=cti,
                                         r0=r0, j=j, win0=win0)
            emit_norm(pend_norm)
            pend_norm = None
            prev_scores_done = p.c(tT)
            wq_ring.release(wqi, tT, prev_scores_done)
            wk_ring.release(wki, tT, prev_scores_done)
        p.flush_block()
    st_A.close()

    x2 = st_R1.enter_context(nc.sbuf_tensor("x2", [128, NCH * S], F32, side="right"))
    x2bf = st_R2.enter_context(nc.sbuf_tensor("x2bf", [128, NCH * S], BF16, side="right"))
    with ExitStack() as ph:
        wo_sb = ph.enter_context(nc.sbuf_tensor("wo_sb", [128, 8 * 1024], BF16))
        stage = [ph.enter_context(nc.sbuf_tensor(f"stage{i}", [128, 4096], BF16)) for i in range(2)]
        xres = [ph.enter_context(nc.sbuf_tensor("xres0", [128, D], F32))]
        lnb = [ph.enter_context(nc.sbuf_tensor(f"lnb{i}", [128, D], F32)) for i in range(2)]
        lnb2 = [ph.enter_context(nc.sbuf_tensor("lnb20", [128, D], F32))]
        sqs = ph.enter_context(nc.sbuf_tensor("sqs", [128, D], F32))
        stt = ph.enter_context(nc.sbuf_tensor("stt", [128, 8], F32))
        pO = [ph.enter_context(nc.psum_tensor(f"pO{i}", [128, 512], F32)) for i in range(2)]
        pL = [ph.enter_context(nc.psum_tensor(f"pL{i}", [128, D], BF16)) for i in range(2)]
        pB2 = [ph.enter_context(nc.psum_tensor("pB20", [128, D], F32))]

        p.wait("scalar", dC, consts_d)
        p.wait("vector", dC, consts_d)
        p.wait("tensor", dC, consts_d)
        for k in range(8):
            p.dma("gpsimd", wo_sb[:, k * 1024:(k + 1) * 1024],
                  d["wor"][k].rearrange("p k c -> p (k c)"), dWo)
        wo_d = p.c(dWo)
        p.wait("tensor", dWo, wo_d)

        po_ring = Ring(p, pO)
        pl_ring = Ring(p, pL)
        pb_ring = Ring(p, pB2)
        xr_ring = DmaRing(p, xres, "dRx")
        ln_ring = Ring(p, lnb)
        ln2_ring = Ring(p, lnb2)
        stage_ring = Ring(p, stage)

        lnb2 = [ph.enter_context(nc.sbuf_tensor(f"lnb2x{i}", [128, D], F32)) for i in range(3)]
        ln2_ring = Ring(p, lnb2 + ln2_ring.slots)
        pB2b = ph.enter_context(nc.psum_tensor("pB21", [128, D], F32))
        pb_ring = Ring(p, pb_ring.slots + [pB2b])

        def emit_A(g, stg, ev_ticks, out_list):
            for tl in range(4):
                tok0 = g * 512 + tl * 128
                xrs, xri, xtok = xr_ring.load("sync", d["x"][tok0:tok0 + 128, :])
                pls, pli = pl_ring.acquire("tensor")
                for ds in range(8):
                    p.wait("tensor", tA, ev_ticks[ds])
                    p.op("tensor",
                         lambda e, o=pls[:, ds * 128:(ds + 1) * 128],
                         i=stg[:, ds * 512 + tl * 128: ds * 512 + tl * 128 + 128]:
                         e.transpose(o, i, identb[:, :]),
                         inc=(tT if ds == 7 else None))
                tp = p.c(tT)
                lns, lni = ln_ring.acquire("vector")
                p.wait("vector", tT, tp)
                xr_ring.wait_loaded("vector", xtok)
                lv = p.op("vector",
                          lambda e, o=lns[:, :], i=pls[:, :], r=xrs[:, :]:
                          e.scalar_tensor_tensor(o, i, 1.0, r, ALU.mult, ALU.add),
                          inc=tV)
                pl_ring.release(pli, tV, lv)
                xr_ring.release(xri, tV, lv)
                p.wait("scalar", tV, lv)
                a1 = p.op("scalar", lambda e, o=sqs[:, :], i=lns[:, :]:
                          e.activation(out=o, in_=i, func=AF.Square,
                                       accum_out=stt[:, 0:1]), inc=tA)
                v1 = p.op("vector", lambda e, i=lns[:, :]:
                          e.reduce_sum(stt[:, 1:2], i, axis=AX.X, negate=True), inc=tV)
                v2 = p.op("vector", lambda e:
                          e.tensor_scalar_mul(stt[:, 2:3], stt[:, 1:2], 1.0 / D), inc=tV)
                p.wait("vector", tV, v2)
                v3 = p.op("vector", lambda e:
                          e.tensor_tensor(stt[:, 3:4], stt[:, 2:3], stt[:, 2:3], ALU.mult),
                          inc=tV)
                p.wait("vector", tA, a1)
                v4 = p.op("vector", lambda e:
                          e.scalar_tensor_tensor(stt[:, 4:5], stt[:, 0:1], 1.0 / D,
                                                 stt[:, 3:4], ALU.mult, ALU.subtract),
                          inc=tV)
                p.wait("scalar", tV, v4)
                a2 = p.op("scalar", lambda e:
                          e.activation(out=stt[:, 5:6], in_=stt[:, 4:5], func=AF.Sqrt,
                                       bias=cst[:, 2:3]), inc=tA)
                p.wait("vector", tA, a2)
                v5 = p.op("vector", lambda e:
                          e.reciprocal(stt[:, 6:7], stt[:, 5:6]), inc=tV)
                p.wait("vector", tV, v5)
                v6 = p.op("vector", lambda e:
                          e.tensor_scalar_mul(stt[:, 7:8], stt[:, 2:3], stt[:, 6:7]),
                          inc=tV)
                ln2s, ln2i = ln2_ring.acquire("scalar")
                p.wait("scalar", tV, v6)
                a3 = p.op("scalar", lambda e, o=ln2s[:, :], i=lns[:, :]:
                          e.activation(out=o, in_=i, func=AF.Identity,
                                       scale=stt[:, 6:7], bias=stt[:, 7:8]), inc=tA)
                ln_ring.release(lni, tA, a3)
                out_list.append((tok0, ln2s, ln2i, a3))

        def emit_B(items):
            for (tok0, ln2s, ln2i, a3) in items:
                pbs, pbi = pb_ring.acquire("tensor")
                p.wait("tensor", tA, a3)
                for ds in range(8):
                    p.op("tensor",
                         lambda e, o=pbs[:, ds * 128:(ds + 1) * 128],
                         i=ln2s[:, ds * 128:(ds + 1) * 128]:
                         e.transpose(o, i, identf[:, :]),
                         inc=(tT if ds == 7 else None))
                tb = p.c(tT)
                ln2_ring.release(ln2i, tT, tb)
                p.wait("vector", tT, tb)
                x2v = x2.rearrange("p (c s) -> p c s", c=NCH)[:, :, tok0:tok0 + 128]
                vv = p.op("vector", lambda e, o=x2v, i=pbs[:, :]:
                          e.tensor_copy(o, i), inc=tV)
                pb_ring.release(pbi, tV, vv)
                p.wait("scalar", tV, vv)
                x2bv = x2bf.rearrange("p (c s) -> p c s", c=NCH)[:, :, tok0:tok0 + 128]
                p.op("scalar", lambda e, o=x2bv, i=x2v:
                     e.activation(out=o, in_=i, func=AF.Copy), inc=tA)

        pend_B = None
        for g in range(4):
            stg, sgi = stage_ring.acquire("scalar")
            ev_ticks = []
            for ds in range(8):
                slot, si = po_ring.acquire("tensor")
                for k in range(NCH):
                    p.op("tensor",
                         lambda e, o=slot[:, :],
                         l=wo_sb[:, ds * 1024 + k * 128: ds * 1024 + k * 128 + 128],
                         r=ctx[:, k * S + g * 512: k * S + g * 512 + 512],
                         st=(k == 0), sp=(k == NCH - 1):
                         e.matmul(o, l, r, start=st, stop=sp),
                         inc=(tT if k == NCH - 1 else None))
                mm = p.c(tT)
                p.wait("scalar", tT, mm)
                ev = p.op("scalar",
                          lambda e, o=stg[:, ds * 512:(ds + 1) * 512], i=slot[:, :],
                          b=bo_sb[:, ds:ds + 1]:
                          e.activation(out=o, in_=i, func=AF.Identity, bias=b), inc=tA)
                ev_ticks.append(ev)
                po_ring.release(si, tA, ev)
            if pend_B is not None:
                emit_B(pend_B)
                pend_B = None
            items = []
            emit_A(g, stg, ev_ticks, items)
            stage_ring.release(sgi, tT, p.c(tT))
            pend_B = items
        emit_B(pend_B)
        x2bf_done = p.c(tA)
        x2_done = p.c(tV)
        p.flush_block()
    st_B.close()

    with ExitStack() as ph:
        wr1_sb = ph.enter_context(nc.sbuf_tensor("wr1_sb", [128, 2048], BF16))
        wr2_sb = ph.enter_context(nc.sbuf_tensor("wr2_sb", [128, 16], BF16))
        r1bf = ph.enter_context(nc.sbuf_tensor("r1bf", [128, 2 * S], BF16))
        br2b = ph.enter_context(nc.sbuf_tensor("br2b", [128, 8], F32))
        lg = ph.enter_context(nc.sbuf_tensor("lg", [128, 8], F32))
        g8 = ph.enter_context(nc.sbuf_tensor("g8", [128, 8], F32))
        eq = ph.enter_context(nc.sbuf_tensor("eq", [128, 8], F32))
        g2 = ph.enter_context(nc.sbuf_tensor("g2", [128, 8], F32))
        msk = ph.enter_context(nc.sbuf_tensor("msk", [128, 8], F32))
        eh = ph.enter_context(nc.sbuf_tensor("eh", [128, 8], F32))
        wu = ph.enter_context(nc.sbuf_tensor("wu", [128, 8], F32))
        rst = ph.enter_context(nc.sbuf_tensor("rst", [128, 8], F32))
        wtile = ph.enter_context(nc.sbuf_tensor("wtile", [128, 128], F32))
        pR1 = [ph.enter_context(nc.psum_tensor(f"pR1{i}", [128, 512], F32)) for i in range(2)]
        pLg = [ph.enter_context(nc.psum_tensor(f"pLg{i}", [128, 8], F32)) for i in range(2)]
        pWt = ph.enter_context(nc.psum_tensor("pWt", [128, 128], F32))

        p.wait("vector", dC, consts_d)
        p.wait("scalar", dC, consts_d)
        p.wait("tensor", dC, consts_d)
        for j2 in range(2):
            p.dma("gpsimd", wr1_sb[:, j2 * 1024:(j2 + 1) * 1024],
                  d["wr1r"][j2].rearrange("p k c -> p (k c)"), dWr)
        for k2 in range(2):
            p.dma("gpsimd", wr2_sb[:, k2 * 8:(k2 + 1) * 8],
                  d["wr2"][k2 * 128:(k2 + 1) * 128, :], dWr)
        wr_d = p.c(dWr)
        p.wait("tensor", dWr, wr_d)
        bt = p.op("tensor", lambda e: e.matmul(pLg[0][:, :], ones_f[:, :], br2row[0:1, :],
                                               start=True, stop=True), inc=tT)
        p.wait("vector", tT, bt)
        bb = p.op("vector", lambda e: e.tensor_copy(br2b[:, :], pLg[0][:, :]), inc=tV)
        pr_ring = Ring(p, pR1)
        p.wait("tensor", tA, x2bf_done)
        for j2 in range(2):
            for g in range(4):
                slot, si = pr_ring.acquire("tensor")
                for k in range(NCH):
                    p.op("tensor",
                         lambda e, o=slot[:, :],
                         l=wr1_sb[:, j2 * 1024 + k * 128: j2 * 1024 + k * 128 + 128],
                         r=x2bf[:, k * S + g * 512: k * S + g * 512 + 512],
                         st=(k == 0), sp=(k == NCH - 1):
                         e.matmul(o, l, r, start=st, stop=sp),
                         inc=(tT if k == NCH - 1 else None))
                mm = p.c(tT)
                p.wait("scalar", tT, mm)
                ev = p.op("scalar",
                          lambda e, o=r1bf[:, j2 * S + g * 512: j2 * S + g * 512 + 512],
                          i=slot[:, :], b=br1_sb[:, j2:j2 + 1]:
                          e.activation(out=o, in_=i, func=AF.Relu, bias=b), inc=tA)
                pr_ring.release(si, tA, ev)
        r1_done = p.c(tA)
        p.wait("tensor", tA, r1_done)
        p.wait("tensor", tV, bb)

        lgall = ph.enter_context(nc.sbuf_tensor("lgall", [128, 128], F32))
        plg_ring = Ring(p, pLg)
        for tt in range(NT):
            lgs, lgi = plg_ring.acquire("tensor")
            for k2 in range(2):
                p.op("tensor",
                     lambda e, o=lgs[:, :],
                     l=r1bf[:, k2 * S + tt * 128: k2 * S + tt * 128 + 128],
                     r=wr2_sb[:, k2 * 8:(k2 + 1) * 8],
                     st=(k2 == 0), sp=(k2 == 1):
                     e.matmul(o, l, r, start=st, stop=sp),
                     inc=(tT if k2 == 1 else None))
            lt = p.c(tT)
            p.wait("vector", tT, lt)
            lgr = p.op("vector", lambda e, o=lgall[:, tt * 8:(tt + 1) * 8], i=lgs[:, :]:
                       e.tensor_copy(o, i), inc=tV)
            plg_ring.release(lgi, tV, lgr)

        def t16(buf):
            return buf.rearrange("p (t e) -> p t e", t=16)

        st16 = ph.enter_context(nc.sbuf_tensor("st16", [128, 96], F32))

        def s16(i):
            return st16[:, i * 16:(i + 1) * 16]

        def bc16(i):
            return bass.AP(st16, i * 16, [[96, 128], [1, 16], [0, 8]])

        p.wait("vector", tV, bb)
        br2bc = bass.AP(br2b, 0, [[8, 128], [0, 16], [1, 8]])
        p.op("vector", lambda e: e.tensor_tensor(t16(lgall), t16(lgall), br2bc, ALU.add),
             inc=tV)
        p.op("vector", lambda e: e.reduce_max(s16(0), t16(lgall), axis=AX.X), inc=tV)
        m1t = p.c(tV)
        p.op("vector", lambda e: e.tensor_tensor(t16(lgall), t16(lgall), bc16(0),
                                                 ALU.subtract), inc=tV)
        sb_t = p.c(tV)
        p.wait("scalar", tV, sb_t)
        exa = p.op("scalar", lambda e: e.activation(out=lgall[:, :], in_=lgall[:, :],
                                                    func=AF.Exp), inc=tA)
        p.wait("vector", tA, exa)
        p.op("vector", lambda e: e.reduce_sum(s16(1), t16(lgall), axis=AX.X), inc=tV)
        p.op("vector", lambda e: e.reciprocal(s16(2), s16(1)), inc=tV)
        p.op("vector", lambda e: e.tensor_tensor(t16(lgall), t16(lgall), bc16(2),
                                                 ALU.mult), inc=tV)
        p.op("vector", lambda e: e.reduce_max(s16(3), t16(lgall), axis=AX.X), inc=tV)
        p.op("vector", lambda e, o=wtile[:, :]: e.tensor_tensor(t16(o), t16(lgall),
                                                                bc16(3), ALU.is_equal),
             inc=tV)
        p.op("vector", lambda e, o=wtile[:, :]:
             e.scalar_tensor_tensor(t16(o), t16(o), -1e30, t16(lgall), ALU.mult, ALU.add),
             inc=tV)
        p.op("vector", lambda e, o=wtile[:, :]: e.reduce_max(s16(4), t16(o), axis=AX.X),
             inc=tV)
        p.op("vector", lambda e, o=wtile[:, :]: e.tensor_tensor(t16(o), t16(lgall),
                                                                bc16(4), ALU.is_ge),
             inc=tV)
        mk16 = p.c(tV)
        p.wait("scalar", tV, mk16)
        eh16 = p.op("scalar", lambda e: e.activation(out=lgall[:, :], in_=lgall[:, :],
                                                     func=AF.Exp, scale=cst[:, 3:4]),
                    inc=tA)
        p.wait("vector", tA, eh16)
        p.op("vector", lambda e, o=wtile[:, :]: e.tensor_tensor(o, o, lgall[:, :],
                                                                ALU.mult), inc=tV)
        p.op("vector", lambda e, o=wtile[:, :]: e.reduce_sum(s16(5), t16(o), axis=AX.X),
             inc=tV)
        p.op("vector", lambda e: e.reciprocal(s16(0), s16(5)), inc=tV)
        wv_t = p.op("vector", lambda e, o=wtile[:, :]:
                    e.tensor_tensor(t16(o), t16(o), bc16(0), ALU.mult), inc=tV)
        p.wait("tensor", tV, wv_t)
        for tt in range(NT):
            wt_t = p.op("tensor", lambda e, i=wtile[:, tt * 8:(tt + 1) * 8]:
                        e.transpose(pWt[0:8, 0:128], i, identf[:, :]), inc=tT)
            p.wait("vector", tT, wt_t)
            wc = p.op("vector", lambda e, o=wT[0:8, tt * 128:(tt + 1) * 128]:
                      e.tensor_copy(o, pWt[0:8, 0:128]), inc=tV)
            p.wait("tensor", tV, wc)
        p.flush_block()

    with ExitStack() as ph:
        h1bf = ph.enter_context(nc.sbuf_tensor("h1bf", [128, 16 * S], BF16))
        w1s = [ph.enter_context(nc.sbuf_tensor(f"w1s{i}", [128, 4096], BF16)) for i in range(2)]
        w2s = [ph.enter_context(nc.sbuf_tensor(f"w2s{i}", [128, 4096], BF16)) for i in range(2)]
        wb = [ph.enter_context(nc.sbuf_tensor("wb0", [128, S], BF16))] * 2
        tmp5 = [ph.enter_context(nc.sbuf_tensor("tmp50", [128, 512], F32))]
        pE1 = [ph.enter_context(nc.psum_tensor(f"pE1{i}", [128, 512], F32)) for i in range(3)]
        pE2 = [ph.enter_context(nc.psum_tensor(f"pE2{i}", [128, 512], F32)) for i in range(3)]
        pWb = ph.enter_context(nc.psum_tensor("pWb", [128, 512], F32))

        w1_ring = DmaRing(p, w1s, "dW1")
        w2_ring = DmaRing(p, w2s, "dW2")
        p1_ring = Ring(p, pE1)
        p2_ring = Ring(p, pE2)
        tmp_ring = Ring(p, tmp5)
        pwb_free = 0
        shared_evac_done = 0
        x2_add_tick = {}

        p.wait("scalar", dC, consts_d)
        p.wait("vector", dC, consts_d)
        p.wait("tensor", tA, x2bf_done)
        p.wait("vector", tV, x2_done)

        h1_consumed = 0
        for e_i in range(NE):
            if e_i >= 1:
                wb_ready = 0
                for g in range(4):
                    p.wait("tensor", tV, pwb_free)
                    bt = p.op("tensor",
                              lambda e, l=sel8[0:8, (e_i - 1) * 128: e_i * 128],
                              r=wT[0:8, g * 512:(g + 1) * 512]:
                              e.matmul(pWb[:, :], l, r, start=True, stop=True),
                              inc=tT)
                    p.wait("vector", tT, bt)
                    bv2 = p.op("vector", lambda e, o=wb[e_i % 2][:, g * 512:(g + 1) * 512]:
                               e.tensor_copy(o, pWb[:, :]), inc=tV)
                    pwb_free = bv2
                wb_ready = p.c(tV)
            for Hh in range(2):
                for ocg in range(4):
                    if Hh == 0 and ocg == 0:
                        p.wait("sync", dPC[e_i], 64)
                    slot, si, w1tok = w1_ring.load(
                        "sync", w1bf[e_i, Hh * 4 + ocg])
                    w1_ring.wait_loaded("tensor", w1tok)
                    for oi in range(4):
                        oc = ocg * 4 + oi
                        goc = Hh * 16 + oc
                        for g in range(4):
                            ps1, pi1 = p1_ring.acquire("tensor")
                            for k in range(NCH):
                                p.op("tensor",
                                     lambda e, o=ps1[:, :],
                                     l=slot[:, oi * 1024 + k * 128: oi * 1024 + k * 128 + 128],
                                     r=x2bf[:, k * S + g * 512: k * S + g * 512 + 512],
                                     st=(k == 0), sp=(k == NCH - 1):
                                     e.matmul(o, l, r, start=st, stop=sp),
                                     inc=(tT if k == NCH - 1 else None))
                            mm = p.c(tT)
                            if g == 3 and oi == 3:
                                w1_ring.release(si, tT, mm)
                            p.wait("scalar", tT, mm)
                            if oc == 0 and g == 0 and (Hh == 1 or e_i > 0):
                                p.wait("scalar", tT, h1_consumed)
                            ev = p.op("scalar",
                                      lambda e, o=h1bf[:, oc * S + g * 512: oc * S + g * 512 + 512],
                                      i=ps1[:, :], b=b1_sb[:, e_i * 32 + goc: e_i * 32 + goc + 1]:
                                      e.activation(out=o, in_=i, func=AF.Silu, bias=b), inc=tA)
                            p1_ring.release(pi1, tA, ev)
                h1_done = p.c(tA)
                for ddg in range(4):
                    slot2, si2, w2tok = w2_ring.load("sync", w2bf[e_i, Hh, ddg])
                    w2_ring.wait_loaded("tensor", w2tok)
                    if ddg == 0:
                        p.wait("tensor", tA, h1_done)
                    for di in range(2):
                      dd = ddg * 2 + di
                      for g in range(4):
                        ps2, pi2 = p2_ring.acquire("tensor")
                        for kc in range(16):
                            p.op("tensor",
                                 lambda e, o=ps2[:, :],
                                 l=slot2[:, di * 2048 + kc * 128: di * 2048 + kc * 128 + 128],
                                 r=h1bf[:, kc * S + g * 512: kc * S + g * 512 + 512],
                                 st=(kc == 0), sp=(kc == 15):
                                 e.matmul(o, l, r, start=st, stop=sp),
                                 inc=(tT if kc == 15 else None))
                        mm2 = p.c(tT)
                        if g == 3 and di == 1:
                            w2_ring.release(si2, tT, mm2)
                        x2slice = x2[:, dd * S + g * 512: dd * S + g * 512 + 512]
                        b2ap = (b2_sb[:, e_i * 8 + dd: e_i * 8 + dd + 1]
                                if Hh == 1 else 0.0)
                        p.wait("vector", tT, mm2)
                        if e_i == 0:
                            ev2 = p.op("vector",
                                       lambda e, o=x2slice, i=ps2[:, :], b=b2ap:
                                       e.scalar_tensor_tensor(o, i, b, o, ALU.add, ALU.add),
                                       inc=tV)
                            p2_ring.release(pi2, tV, ev2)
                            shared_evac_done = ev2
                        else:
                            if dd == 0 and g == 0 and Hh == 0:
                                p.wait("vector", tV, wb_ready)
                            tms, tmi = tmp_ring.acquire("vector")
                            ev2 = p.op("vector",
                                       lambda e, o=tms[:, :], i=ps2[:, :], b=b2ap,
                                       w=wb[e_i % 2][:, g * 512:(g + 1) * 512]:
                                       e.scalar_tensor_tensor(o, i, b, w, ALU.add, ALU.mult),
                                       inc=tV)
                            p2_ring.release(pi2, tV, ev2)
                            p.wait("gpsimd", tV, ev2)
                            if e_i == 1 and dd == 0 and g == 0 and Hh == 0:
                                p.wait("gpsimd", tV, shared_evac_done)
                            p.wait("gpsimd", tG, x2_add_tick.get((dd, g), 0))
                            ga = p.op("gpsimd",
                                      lambda e, o=x2slice, t=tms[:, :]:
                                      e.tensor_tensor(o, o, t, ALU.add), inc=tG)
                            x2_add_tick[(dd, g)] = ga
                            tmp_ring.release(tmi, tG, ga)
                h1_consumed = p.c(tT)
        ffn_done_g = p.c(tG)
        ffn_done_v = p.c(tV)
        p.flush_block()
    st_R2.close()

    with ExitStack() as ph:
        lnb = [ph.enter_context(nc.sbuf_tensor(f"lnc{i}", [128, D], F32)) for i in range(2)]
        ybuf = [ph.enter_context(nc.sbuf_tensor(f"ybuf{i}", [128, D], F32)) for i in range(2)]
        sqs = ph.enter_context(nc.sbuf_tensor("sqs2", [128, D], F32))
        stt = ph.enter_context(nc.sbuf_tensor("stt2", [128, 8], F32))
        pL = [ph.enter_context(nc.psum_tensor(f"pL2{i}", [128, D], F32)) for i in range(2)]
        dY = [p.sem("dY0"), p.sem("dY1")]

        pl_ring = Ring(p, pL)
        ln_ring = Ring(p, lnb)
        p.wait("scalar", dC, consts_d)
        p.wait("vector", dC, consts_d)
        p.wait("tensor", dC, consts_d)
        for tt in range(NT):
            pls, pli = pl_ring.acquire("tensor")
            for ds in range(8):
                p.op("tensor",
                     lambda e, o=pls[:, ds * 128:(ds + 1) * 128],
                     i=x2[:, ds * S + tt * 128: ds * S + tt * 128 + 128]:
                     e.transpose(o, i, identf[:, :]),
                     inc=(tT if ds == 7 else None))
            tp = p.c(tT)
            lns, lni = ln_ring.acquire("vector")
            p.wait("vector", tT, tp)
            lv = p.op("vector", lambda e, o=lns[:, :], i=pls[:, :]:
                      e.tensor_copy(o, i), inc=tV)
            pl_ring.release(pli, tV, lv)
            p.wait("scalar", tV, lv)
            a1 = p.op("scalar", lambda e, o=sqs[:, :], i=lns[:, :]:
                      e.activation(out=o, in_=i, func=AF.Square,
                                   accum_out=stt[:, 0:1]), inc=tA)
            v1 = p.op("vector", lambda e, i=lns[:, :]:
                      e.reduce_sum(stt[:, 1:2], i, axis=AX.X, negate=True), inc=tV)
            v2 = p.op("vector", lambda e:
                      e.tensor_scalar_mul(stt[:, 2:3], stt[:, 1:2], 1.0 / D), inc=tV)
            p.wait("vector", tV, v2)
            v3 = p.op("vector", lambda e:
                      e.tensor_tensor(stt[:, 3:4], stt[:, 2:3], stt[:, 2:3], ALU.mult),
                      inc=tV)
            p.wait("vector", tA, a1)
            v4 = p.op("vector", lambda e:
                      e.scalar_tensor_tensor(stt[:, 4:5], stt[:, 0:1], 1.0 / D,
                                             stt[:, 3:4], ALU.mult, ALU.subtract), inc=tV)
            p.wait("scalar", tV, v4)
            a2 = p.op("scalar", lambda e:
                      e.activation(out=stt[:, 5:6], in_=stt[:, 4:5], func=AF.Sqrt,
                                   bias=cst[:, 2:3]), inc=tA)
            p.wait("vector", tA, a2)
            v5 = p.op("vector", lambda e: e.reciprocal(stt[:, 6:7], stt[:, 5:6]), inc=tV)
            p.wait("vector", tV, v5)
            v6 = p.op("vector", lambda e:
                      e.tensor_scalar_mul(stt[:, 7:8], stt[:, 2:3], stt[:, 6:7]), inc=tV)
            p.wait("scalar", tV, v6)
            if tt >= 2:
                p.wait("scalar", dY[tt % 2], (tt // 2) * 16)
            ybs = ybuf[tt % 2]
            a3 = p.op("scalar", lambda e, o=ybs[:, :], i=lns[:, :]:
                      e.activation(out=o, in_=i, func=AF.Identity,
                                   scale=stt[:, 6:7], bias=stt[:, 7:8]), inc=tA)
            ln_ring.release(lni, tA, a3)
            p.wait("sync", tA, a3)
            p.dma("sync", y[tt * 128:(tt + 1) * 128, :], ybs[:, :], dY[tt % 2])
        p.wait("sync", dY[0], p.c(dY[0]))
        p.wait("sync", dY[1], p.c(dY[1]))
        p.flush_block()

    st_R1.close()
    es.close()
    return nc


def _host_pack(inputs):
    f32 = np.float32
    bf = ml_dtypes.bfloat16
    ca = np.ascontiguousarray
    ins = {}
    ins["wv"] = ca(inputs["L0_attn_Wv"], dtype=f32)
    for nm, key in (("wqr", "L0_attn_Wq"), ("wkr", "L0_attn_Wk"), ("wor", "L0_attn_Wo")):
        w = np.asarray(inputs[key], dtype=f32)
        ins[nm] = ca(w.reshape(8, 128, 8, 128).transpose(2, 1, 0, 3))
    ins["bqr"] = ca(np.asarray(inputs["L0_attn_bq"], dtype=f32).reshape(8, 128).T)
    ins["bkr"] = ca(np.asarray(inputs["L0_attn_bk"], dtype=f32).reshape(8, 128).T)
    ins["bor"] = ca(np.asarray(inputs["L0_attn_bo"], dtype=f32).reshape(8, 128).T)
    ins["bv"] = ca(np.asarray(inputs["L0_attn_bv"], dtype=f32).reshape(1, D))
    ins["wr1r"] = ca(np.asarray(inputs["L0_ffn_Wr1"], dtype=f32)
                     .reshape(8, 128, 2, 128).transpose(2, 1, 0, 3))
    ins["wr2"] = ca(inputs["L0_ffn_Wr2"], dtype=f32)
    ins["br1r"] = ca(np.asarray(inputs["L0_ffn_br1"], dtype=f32).reshape(2, 128).T)
    ins["br2"] = ca(np.asarray(inputs["L0_ffn_br2"], dtype=f32).reshape(1, 8))
    w1 = np.concatenate([np.asarray(inputs["L0_ffn_Ws1"], dtype=f32)[None],
                         np.asarray(inputs["L0_ffn_We1"], dtype=f32)], axis=0)
    ins["w1r"] = ca(w1.reshape(NE, 8, 128, 8, 4, 128).transpose(0, 3, 2, 4, 1, 5)
                    .reshape(NE, 8, 128, 4096))
    w2 = np.concatenate([np.asarray(inputs["L0_ffn_Ws2"], dtype=f32)[None],
                         np.asarray(inputs["L0_ffn_We2"], dtype=f32)], axis=0)
    ins["w2r"] = ca(w2.reshape(NE, 2, 16, 128, 4, 2, 128).transpose(0, 1, 4, 3, 5, 2, 6)
                    .reshape(NE, 2, 4, 128, 4096))
    b1 = np.concatenate([np.asarray(inputs["L0_ffn_bs1"], dtype=f32)[None],
                         np.asarray(inputs["L0_ffn_be1"], dtype=f32)], axis=0)
    ins["b1r"] = ca(b1.reshape(NE, 32, 128).transpose(2, 0, 1).reshape(128, NE * 32))
    b2 = np.concatenate([np.asarray(inputs["L0_ffn_bs2"], dtype=f32)[None],
                         np.asarray(inputs["L0_ffn_be2"], dtype=f32)], axis=0)
    ins["b2r"] = ca(b2.reshape(NE, 8, 128).transpose(2, 0, 1).reshape(128, NE * 8))

    hd = 64
    pos = np.arange(S, dtype=f32)
    inv_freq = np.exp(np.arange(0, hd, 2, dtype=f32) * f32(-np.log(10000.0) / hd))
    ang = pos[:, None] * inv_freq[None, :]
    sin = np.sin(ang.astype(f32)).astype(f32)
    cos = np.cos(ang.astype(f32)).astype(f32)
    pmod = np.arange(128) % 32
    ins["ropec"] = ca(cos[:, pmod].T)
    sgn = np.where((np.arange(128) % 64) < 32, -1.0, 1.0).astype(f32)
    ins["ropes"] = ca(sin[:, pmod].T * sgn[:, None])
    P = np.zeros((128, 128), dtype=f32)
    for m in range(128):
        src = m + 32 if (m % 64) < 32 else m - 32
        P[src, m] = 1.0
    ins["prm"] = ca(P.astype(bf))
    ins["identb"] = ca(np.eye(128, dtype=bf))
    ins["identf"] = ca(np.eye(128, dtype=f32))
    k_idx = np.arange(128)[:, None]
    i_idx = np.arange(896)[None, :]
    ins["trit"] = ca(np.where(i_idx < k_idx + 384, -10000.0, 0.0).astype(bf))
    sel = np.zeros((8, 8, 128), dtype=f32)
    for e in range(8):
        sel[e, e, :] = 1.0
    ins["sel8"] = ca(sel.transpose(1, 0, 2).reshape(8, 1024).astype(bf))
    return ins


def kernel(**inputs):
    ins = _host_pack(inputs)
    x = np.asarray(inputs["L0_attn_x"], dtype=np.float32)
    nc = build_nc()
    in_maps = []
    for b in range(8):
        m = dict(ins)
        m["x"] = np.ascontiguousarray(x[b])
        in_maps.append(m)
    res = run_bass_kernel_spmd(nc, in_maps, core_ids=list(range(8)))
    return np.stack([r["y"] for r in res.results], axis=0)


# version 10
# speedup vs baseline: 1.0064x; 1.0064x over previous
import numpy as np
import ml_dtypes
from contextlib import ExitStack

import concourse.bass as bass
import concourse.mybir as mybir
from concourse.bass_utils import run_bass_kernel_spmd

F32 = mybir.dt.float32
BF16 = mybir.dt.bfloat16
AF = mybir.ActivationFunctionType
ALU = mybir.AluOpType
AX = mybir.AxisListType

S = 2048
D = 1024
NCH = 8
NE = 9
NT = 16
EXP_SHIFT = -15.0
EPS = 1e-5

ENGINES = ("sync", "gpsimd", "scalar", "vector", "tensor")


class Prog:

    SELF_FENCE = {"vector": "tV", "scalar": "tA", "gpsimd": "tG"}

    def __init__(self, nc, es):
        self.nc = nc
        self.es = es
        self.q = {e: [] for e in ENGINES}
        self.sems = {}
        self.cnt = {}
        self._lastwait = {}

    def sem(self, name):
        self.sems[name] = self.es.enter_context(self.nc.semaphore(name))
        self.cnt[name] = 0
        return name

    def wait(self, eng, sem, val):
        if val <= 0:
            return
        if sem == self.SELF_FENCE.get(eng):
            if self._lastwait.get(eng, 0) >= val:
                return
            self._lastwait[eng] = val
        h = self.sems[sem]
        self.q[eng].append(lambda e, h=h, v=val: e.wait_ge(h, v))

    def op(self, eng, build, inc=None, n=1):
        own = self.SELF_FENCE.get(eng)
        if own is not None:
            self.wait(eng, own, self.cnt[own])
            if inc is None:
                inc, n = own, 1
        if inc is None:
            self.q[eng].append(build)
            return None
        self.cnt[inc] += n
        h = self.sems[inc]
        self.q[eng].append(lambda e, b=build, h=h, k=n: b(e).then_inc(h, k))
        return self.cnt[inc]

    def dma(self, eng, out, in_, sem):
        self.cnt[sem] += 16
        h = self.sems[sem]
        self.q[eng].append(lambda e, o=out, i=in_, h=h: e.dma_start(out=o, in_=i).then_inc(h, 16))
        return self.cnt[sem]

    def c(self, sem):
        return self.cnt[sem]

    def flush_block(self):
        q = self.q
        self.q = {e: [] for e in ENGINES}
        with self.nc.Block() as blk:
            blk.sync(lambda e, fns=q["sync"]: [f(e) for f in fns])
            blk.gpsimd(lambda e, fns=q["gpsimd"]: [f(e) for f in fns])
            blk.scalar(lambda e, fns=q["scalar"]: [f(e) for f in fns])
            blk.vector(lambda e, fns=q["vector"]: [f(e) for f in fns])
            blk.tensor(lambda e, fns=q["tensor"]: [f(e) for f in fns])


class Ring:

    def __init__(self, prog, slots):
        self.p = prog
        self.slots = list(slots)
        self.n = len(self.slots)
        self.free_at = [[] for _ in range(self.n)]
        self.i = 0

    def acquire(self, eng):
        s = self.i % self.n
        self.i += 1
        for (sem, tick) in self.free_at[s]:
            self.p.wait(eng, sem, tick)
        self.free_at[s] = []
        return self.slots[s], s

    def release(self, s, sem, tick):
        self.free_at[s].append((sem, tick))


class DmaRing:

    def __init__(self, prog, slots, name):
        self.p = prog
        self.slots = list(slots)
        self.n = len(self.slots)
        self.sems = [prog.sem(f"{name}{i}") for i in range(self.n)]
        self.free_at = [[] for _ in range(self.n)]
        self.i = 0

    def load(self, eng, src):
        s = self.i % self.n
        self.i += 1
        for (sem, tick) in self.free_at[s]:
            self.p.wait(eng, sem, tick)
        self.free_at[s] = []
        slot = self.slots[s]
        cnt = self.p.dma(eng, slot[:, :], src, self.sems[s])
        return slot, s, (self.sems[s], cnt)

    def wait_loaded(self, eng, token):
        self.p.wait(eng, token[0], token[1])

    def release(self, s, sem, tick):
        self.free_at[s].append((sem, tick))


def build_nc():
    nc = bass.Bass("TRN2", target_bir_lowering=False)

    d = {}
    def din(name, shape, dt=F32):
        d[name] = nc.dram_tensor(name, shape, dt, kind="ExternalInput").ap()
    din("x", [S, D])
    din("wqr", [8, 128, 8, 128]); din("wkr", [8, 128, 8, 128])
    din("wor", [8, 128, 8, 128]); din("wv", [D, D])
    din("bqr", [128, 8]); din("bkr", [128, 8]); din("bor", [128, 8])
    din("bv", [1, D])
    din("wr1r", [2, 128, 8, 128]); din("wr2", [256, 8])
    din("br1r", [128, 2]); din("br2", [1, 8])
    din("w1r", [NE, 8, 128, 4096])
    din("w2r", [NE, 2, 4, 128, 4096])
    din("b1r", [128, NE * 32]); din("b2r", [128, NE * 8])
    din("ropec", [128, S]); din("ropes", [128, S])
    din("prm", [128, 128], BF16)
    din("identb", [128, 128], BF16)
    din("identf", [128, 128])
    din("trit", [128, 896], BF16)
    din("sel8", [8, 1024], BF16)
    din("sel2", [65, 128])
    y = nc.dram_tensor("y", [S, D], F32, kind="ExternalOutput").ap()
    w1bf = nc.dram_tensor("w1bf", [NE, 8, 128, 4096], BF16).ap()
    w2bf = nc.dram_tensor("w2bf", [NE, 2, 4, 128, 4096], BF16).ap()

    es = ExitStack()
    E = es.enter_context
    p = Prog(nc, es)
    dC = p.sem("dC")
    dV = p.sem("dV")
    dWo = p.sem("dWo")
    dWr = p.sem("dWr")
    dR = p.sem("dR")
    dPC = [p.sem(f"dPC{e}") for e in range(NE)]
    tT = p.sem("tT"); tV = p.sem("tV"); tA = p.sem("tA"); tG = p.sem("tG")

    identf = E(nc.sbuf_tensor("identf_s", [128, 128], F32))
    ones_b = E(nc.sbuf_tensor("ones_b", [128, 1], BF16))
    ones_f = E(nc.sbuf_tensor("ones_f", [1, 128], F32))
    bq_sb = E(nc.sbuf_tensor("bq_sb", [128, 8], F32))
    bk_sb = E(nc.sbuf_tensor("bk_sb", [128, 8], F32))
    bo_sb = E(nc.sbuf_tensor("bo_sb", [128, 8], F32))
    br1_sb = E(nc.sbuf_tensor("br1_sb", [128, 2], F32))
    br2row = E(nc.sbuf_tensor("br2row", [1, 8], F32))
    b1_sb = E(nc.sbuf_tensor("b1_sb", [128, NE * 32], F32))
    b2_sb = E(nc.sbuf_tensor("b2_sb", [128, NE * 8], F32))
    wT = E(nc.sbuf_tensor("wT", [8, S], BF16))
    cst = E(nc.sbuf_tensor("cst", [128, 4], F32))
    sel8 = E(nc.sbuf_tensor("sel8_s", [8, 1024], BF16))
    sel2 = E(nc.sbuf_tensor("sel2_s", [65, 128], F32))

    identb = E(nc.sbuf_tensor("identb_s", [128, 128], BF16))
    st_B = ExitStack()
    ctx = st_B.enter_context(nc.sbuf_tensor("ctx", [128, NCH * S], BF16))
    st_A = ExitStack()
    prm = st_A.enter_context(nc.sbuf_tensor("prm_s", [128, 128], BF16))
    trit = st_A.enter_context(nc.sbuf_tensor("trit_s", [128, 896], BF16))
    bvrow = st_A.enter_context(nc.sbuf_tensor("bvrow", [1, D], F32))
    xbf = st_A.enter_context(nc.sbuf_tensor("xbf", [128, NCH * S], BF16))
    vbuf = st_A.enter_context(nc.sbuf_tensor("vbuf", [128, NT * D], BF16))
    ropec = st_A.enter_context(nc.sbuf_tensor("ropec_s", [128, S], F32))
    ropes = st_A.enter_context(nc.sbuf_tensor("ropes_s", [128, S], F32))
    st_R1 = ExitStack()
    st_R2 = ExitStack()

    def precast_expert(e_i):
        for hh2 in range(2):
            p.dma("gpsimd", w1bf[e_i][hh2 * 4:(hh2 + 1) * 4], d["w1r"][e_i][hh2 * 4:(hh2 + 1) * 4], dPC[e_i])
            p.dma("gpsimd", w2bf[e_i][hh2], d["w2r"][e_i][hh2], dPC[e_i])

    for name, dst in (("identf", identf), ("ropec", ropec), ("ropes", ropes),
                      ("prm", prm), ("identb", identb), ("trit", trit),
                      ("bqr", bq_sb), ("bkr", bk_sb), ("bor", bo_sb),
                      ("br1r", br1_sb), ("br2", br2row), ("bv", bvrow),
                      ("sel8", sel8), ("sel2", sel2)):
        p.dma("sync", dst[:, :], d[name], dC)
    p.dma("sync", b1_sb[:, :], d["b1r"], dC)
    p.dma("sync", b2_sb[:, :], d["b2r"], dC)
    idf_d = p.c(dC)
    consts_d = p.c(dC)
    p.op("gpsimd", lambda e: e.memset(ones_b[:, :], 1.0), inc=tG)
    g_ones = p.c(tG)
    p.op("vector", lambda e: e.memset(ones_f[:, :], 1.0), inc=tV)
    p.op("vector", lambda e: e.memset(cst[:, 0:1], EXP_SHIFT), inc=tV)
    p.op("vector", lambda e: e.memset(cst[:, 1:2], 0.125), inc=tV)
    p.op("vector", lambda e: e.memset(cst[:, 2:3], EPS), inc=tV)
    p.op("vector", lambda e: e.memset(cst[:, 3:4], 0.5), inc=tV)
    v_ones = p.c(tV)

    with ExitStack() as ph:
        xstage = [ph.enter_context(nc.sbuf_tensor(f"xstage{i}", [128, D], F32)) for i in range(2)]
        wv_sb = ph.enter_context(nc.sbuf_tensor("wv_sb", [128, 8 * D], BF16))
        bvb = ph.enter_context(nc.sbuf_tensor("bvb", [128, D], F32))
        pX = [ph.enter_context(nc.psum_tensor(f"pX{i}", [128, D], F32)) for i in range(2)]
        pV = [ph.enter_context(nc.psum_tensor(f"pV{i}", [128, 512], F32)) for i in range(2)]
        pBC = ph.enter_context(nc.psum_tensor("pBC", [128, 512], F32))

        xring = DmaRing(p, xstage, "dX")
        px_ring = Ring(p, pX)
        a_x0 = p.c(tA)
        xbf_ev = []
        p.wait("tensor", dC, idf_d)
        for tt in range(NT):
            slot, si, tok = xring.load("sync", d["x"][tt * 128:(tt + 1) * 128, :])
            xring.wait_loaded("tensor", tok)
            pxs, pxi = px_ring.acquire("tensor")
            for k in range(NCH):
                p.op("tensor",
                     lambda e, o=pxs[:, k * 128:(k + 1) * 128],
                     i=slot[:, k * 128:(k + 1) * 128]:
                     e.transpose(o, i, identf[:, :]),
                     inc=(tT if k == NCH - 1 else None))
            tp = p.c(tT)
            xring.release(si, tT, tp)
            p.wait("scalar", tT, tp)
            out_ap = xbf.rearrange("p (c s) -> p c s", c=NCH)[:, :, tt * 128:(tt + 1) * 128]
            ev = p.op("scalar", lambda e, o=out_ap, i=pxs[:, :]:
                      e.activation(out=o, in_=i, func=AF.Copy), inc=tA)
            xbf_ev.append(ev)
            px_ring.release(pxi, tA, ev)

        p.wait("tensor", dC, consts_d)
        p.wait("tensor", tV, v_ones)
        last_bc = 0
        for ng in range(2):
            if ng == 1:
                p.wait("tensor", tV, last_bc)
            bt = p.op("tensor", lambda e, o=pBC[:, :], i=bvrow[0:1, ng * 512:(ng + 1) * 512]:
                      e.matmul(o, ones_f[:, :], i, start=True, stop=True), inc=tT)
            p.wait("vector", tT, bt)
            last_bc = p.op("vector", lambda e, o=bvb[:, ng * 512:(ng + 1) * 512], i=pBC[:, :]:
                           e.tensor_copy(o, i), inc=tV)

        for k in range(NCH):
            p.dma("gpsimd", wv_sb[:, k * D:(k + 1) * D],
                  d["wv"][k * 128:(k + 1) * 128, :], dV)
        wv_d = p.c(dV)
        precast_expert(0)
        precast_expert(1)
        p.wait("tensor", dV, wv_d)
        pv_ring = Ring(p, pV)
        gi = 0
        for tt in range(NT):
            p.wait("tensor", tA, xbf_ev[tt])
            for ng in range(2):
                slot, si = pv_ring.acquire("tensor")
                for k in range(NCH):
                    p.op("tensor",
                         lambda e, o=slot[:, :],
                         l=xbf[:, k * S + tt * 128: k * S + tt * 128 + 128],
                         r=wv_sb[:, k * D + ng * 512: k * D + ng * 512 + 512],
                         st=(k == 0), sp=(k == NCH - 1):
                         e.matmul(o, l, r, start=st, stop=sp),
                         inc=(tT if k == NCH - 1 else None))
                mm = p.c(tT)
                p.wait("vector", tT, mm)
                if gi == 0:
                    p.wait("vector", tV, last_bc)
                ev = p.op("vector",
                          lambda e, o=vbuf[:, tt * D + ng * 512: tt * D + ng * 512 + 512],
                          i=slot[:, :], b=bvb[:, ng * 512:(ng + 1) * 512]:
                          e.scalar_tensor_tensor(o, i, 1.0, b, ALU.mult, ALU.add),
                          inc=tV)
                pv_ring.release(si, tV, ev)
                gi += 1
        p.flush_block()

    with ExitStack() as ph:
        wq_sb = [ph.enter_context(nc.sbuf_tensor(f"wq{i}", [128, 1024], BF16)) for i in range(2)]
        wk_sb = [ph.enter_context(nc.sbuf_tensor(f"wk{i}", [128, 1024], BF16)) for i in range(2)]
        q_sb = ph.enter_context(nc.sbuf_tensor("q_sb", [128, S], BF16))
        k_sb = ph.enter_context(nc.sbuf_tensor("k_sb", [128, S], BF16))
        qr_sb = ph.enter_context(nc.sbuf_tensor("qr_sb", [128, S], BF16))
        kr_sb = ph.enter_context(nc.sbuf_tensor("kr_sb", [128, S], BF16))
        t1f = [ph.enter_context(nc.sbuf_tensor(f"t1f{i}", [128, 512], F32)) for i in range(2)]
        et_sb = [ph.enter_context(nc.sbuf_tensor(f"et{i}", [128, 512], BF16)) for i in range(4)]
        recip_sb = ph.enter_context(nc.sbuf_tensor("recip_sb", [65, 512], F32))
        recipb = [ph.enter_context(nc.sbuf_tensor(f"recipb{i}", [128, 512], F32)) for i in range(2)]
        pS = [ph.enter_context(nc.psum_tensor(f"pS{i}", [128, 512], F32)) for i in range(4)]
        pCt = [ph.enter_context(nc.psum_tensor(f"pCt{i}", [128, 512], F32)) for i in range(2)]
        pSm = [ph.enter_context(nc.psum_tensor(f"pSm{i}", [128, 512], F32)) for i in range(2)]

        for eng in ("tensor", "scalar", "vector", "gpsimd"):
            p.wait(eng, dC, consts_d)
        rec_init = p.op("vector", lambda e: e.memset(recip_sb[:, :], 1.0), inc=tV)
        p.wait("tensor", tV, rec_init)

        wq_ring = DmaRing(p, wq_sb, "dQ")
        wk_ring = DmaRing(p, wk_sb, "dK")
        ps_ring = Ring(p, pS)
        ct_ring = Ring(p, pCt)
        sm_ring = Ring(p, pSm)
        t1_ring = Ring(p, t1f)
        et_ring = Ring(p, et_sb)
        rb_ring = Ring(p, recipb)
        prev_rope_done = 0
        prev_scores_done = 0

        for j in range(NCH):
            wqs, wqi, qtok = wq_ring.load("gpsimd", d["wqr"][j].rearrange("p k c -> p (k c)"))
            wks, wki, ktok = wk_ring.load("gpsimd", d["wkr"][j].rearrange("p k c -> p (k c)"))
            if j + 2 < NE:
                precast_expert(j + 2)
            wq_ring.wait_loaded("tensor", qtok)
            wk_ring.wait_loaded("tensor", ktok)

            evac_ticks = {}
            for (nm, wsb, src, dst, bias_sb) in (
                    ("q", wqs, q_sb, qr_sb, bq_sb), ("k", wks, k_sb, kr_sb, bk_sb)):
                for g in range(4):
                    slot, si = ps_ring.acquire("tensor")
                    for k in range(NCH):
                        p.op("tensor",
                             lambda e, o=slot[:, :],
                             l=wsb[:, k * 128:(k + 1) * 128],
                             r=xbf[:, k * S + g * 512: k * S + g * 512 + 512],
                             st=(k == 0), sp=(k == NCH - 1):
                             e.matmul(o, l, r, start=st, stop=sp),
                             inc=(tT if k == NCH - 1 else None))
                    mm = p.c(tT)
                    p.wait("scalar", tT, mm)
                    if j > 0 and g == 0:
                        p.wait("scalar", tG, prev_rope_done)
                    ev = p.op("scalar",
                              lambda e, o=src[:, g * 512:(g + 1) * 512], i=slot[:, :],
                              b=bias_sb[:, j:j + 1]:
                              e.activation(out=o, in_=i, func=AF.Identity, bias=b),
                              inc=tA)
                    evac_ticks[(nm, g)] = ev
                    ps_ring.release(si, tA, ev)
                for g in range(4):
                    slot, si = ps_ring.acquire("tensor")
                    p.wait("tensor", tA, evac_ticks[(nm, g)])
                    rt = p.op("tensor",
                              lambda e, o=slot[:, :], r=src[:, g * 512:(g + 1) * 512]:
                              e.matmul(o, prm[:, :], r, start=True, stop=True),
                              inc=tT)
                    t1s, t1i = t1_ring.acquire("vector")
                    p.wait("vector", tT, rt)
                    vt = p.op("vector",
                              lambda e, o=t1s[:, :], i=slot[:, :],
                              s=ropes[:, g * 512:(g + 1) * 512]:
                              e.tensor_tensor(o, i, s, ALU.mult), inc=tV)
                    ps_ring.release(si, tV, vt)
                    if j > 0 and g == 0:
                        p.wait("gpsimd", tT, prev_scores_done)
                    p.wait("gpsimd", tA, evac_ticks[(nm, g)])
                    g1 = p.op("gpsimd",
                              lambda e, o=dst[:, g * 512:(g + 1) * 512],
                              i=src[:, g * 512:(g + 1) * 512],
                              c=ropec[:, g * 512:(g + 1) * 512]:
                              e.tensor_tensor(o, i, c, ALU.mult), inc=tG)
                    p.wait("gpsimd", tV, vt)
                    p.wait("gpsimd", tG, g1)
                    gt = p.op("gpsimd",
                              lambda e, o=dst[:, g * 512:(g + 1) * 512], t=t1s[:, :]:
                              e.tensor_tensor(o, o, t, ALU.add), inc=tG)
                    t1_ring.release(t1i, tG, gt)
            prev_rope_done = p.c(tG)
            p.wait("tensor", tG, prev_rope_done)

            hA, hB = 2 * j, 2 * j + 1

            def emit_av(pa):
                p.wait("tensor", tA, pa["ex"])
                kt, nkt = pa["kt"], pa["nkt"]
                first = (kt == 0)
                p.op("tensor",
                     lambda e, o=pa["ctslot"][0:64, :],
                     l=vbuf[:, kt * D + hA * 64: kt * D + hA * 64 + 64],
                     r=pa["etsA"][:, :], st=first, sp=False:
                     e.matmul(o, l, r, start=st, stop=sp, tile_position=(0, 0),
                              skip_group_check=True))
                p.op("tensor",
                     lambda e, o=pa["ctslot"][64:128, :],
                     l=vbuf[:, kt * D + hB * 64: kt * D + hB * 64 + 64],
                     r=pa["etsB"][:, :], st=first, sp=(kt == nkt - 1):
                     e.matmul(o, l, r, start=st, stop=sp, tile_position=(0, 64),
                              skip_group_check=True))
                p.op("tensor",
                     lambda e, o=pa["smslot"][0:1, :], r=pa["etsA"][:, :],
                     st=first: e.matmul(o, ones_b[:, :], r, start=st, stop=False,
                                        skip_group_check=True))
                sm = p.op("tensor",
                          lambda e, o=pa["smslot"][64:65, :], r=pa["etsB"][:, :],
                          st=first, sp=(kt == nkt - 1):
                          e.matmul(o, ones_b[:, :], r, start=st, stop=sp,
                                   tile_position=(0, 64), skip_group_check=True),
                          inc=tT)
                et_ring.release(pa["etiA"], tT, sm)
                et_ring.release(pa["etiB"], tT, sm)
                return sm

            def emit_norm(pn):
                nonlocal recip_busy
                p.wait("tensor", tV, pn["rc"])
                bt = p.op("tensor", lambda e, o=pn["smslot"][:, :]:
                          e.matmul(o, sel2[0:65, 0:128], recip_sb[0:65, :],
                                   start=True, stop=True, skip_group_check=True),
                          inc=tT)
                recip_busy = bt
                rslot, rbi = rb_ring.acquire("vector")
                p.wait("vector", tT, bt)
                bv_ = p.op("vector", lambda e, o=rslot[:, :], i=pn["smslot"][:, :]:
                           e.tensor_copy(o, i), inc=tV)
                sm_ring.release(pn["smi"], tV, bv_)
                j_, win0_ = pn["j"], pn["win0"]
                cv = p.op("vector",
                          lambda e, o=ctx[:, j_ * S + win0_: j_ * S + win0_ + 512],
                          i=pn["ctslot"][:, :], r=rslot[:, :]:
                          e.tensor_tensor(o, i, r, ALU.mult), inc=tV)
                rb_ring.release(rbi, tV, cv)
                ct_ring.release(pn["cti"], tV, cv)

            pend_av = None
            pend_norm = None
            recip_busy = 0
            for qh in range(2):
                for ws in range(2):
                    win0 = qh * 1024 + ws * 512
                    nkt = qh * 8 + ws * 4 + 4
                    ctslot, cti = ct_ring.acquire("tensor")
                    smslot, smi = sm_ring.acquire("tensor")
                    for kt in range(nkt):
                        sslotA, ssiA = ps_ring.acquire("tensor")
                        sslotB, ssiB = ps_ring.acquire("tensor")
                        etsA, etiA = et_ring.acquire("scalar")
                        etsB, etiB = et_ring.acquire("scalar")
                        off = kt * 128 - win0
                        for (r0, sslot) in ((0, sslotA), (64, sslotB)):
                            last = (off < 0) and (r0 == 64)
                            p.op("tensor",
                                 lambda e, o=sslot[:, :],
                                 l=kr_sb[r0:r0 + 64, kt * 128:(kt + 1) * 128],
                                 r=qr_sb[r0:r0 + 64, win0:win0 + 512],
                                 sp=(off < 0):
                                 e.matmul(o, l, r, start=True, stop=sp),
                                 inc=(tT if last else None))
                        if off >= 0:
                            scol = 384 - off
                            for sslot in (sslotA, sslotB):
                                p.op("tensor",
                                     lambda e, o=sslot[:, :], m=trit[:, scol:scol + 512]:
                                     e.matmul(o, identb[:, :], m, start=False,
                                              stop=True),
                                     inc=(tT if sslot is sslotB else None))
                        st_t = p.c(tT)
                        p.wait("scalar", tT, st_t)
                        p.op("scalar",
                             lambda e, o=etsA[:, :], i=sslotA[:, :]:
                             e.activation(out=o, in_=i, func=AF.Exp,
                                          bias=cst[:, 0:1], scale=cst[:, 1:2]))
                        ex = p.op("scalar",
                                  lambda e, o=etsB[:, :], i=sslotB[:, :]:
                                  e.activation(out=o, in_=i, func=AF.Exp,
                                               bias=cst[:, 0:1], scale=cst[:, 1:2]),
                                  inc=tA)
                        ps_ring.release(ssiA, tA, ex)
                        ps_ring.release(ssiB, tA, ex)
                        if kt == 3 and pend_norm is not None:
                            emit_norm(pend_norm)
                            pend_norm = None
                        if pend_av is not None:
                            emit_av(pend_av)
                        pend_av = dict(ex=ex, ctslot=ctslot, smslot=smslot,
                                       kt=kt, nkt=nkt, etsA=etsA, etiA=etiA,
                                       etsB=etsB, etiB=etiB)
                    sm = emit_av(pend_av)
                    pend_av = None
                    p.wait("vector", tT, sm)
                    p.wait("vector", tT, recip_busy)
                    p.op("vector", lambda e, i=smslot[0:1, :]:
                         e.reciprocal(recip_sb[0:1, :], i), inc=tV)
                    rc = p.op("vector", lambda e, i=smslot[64:65, :]:
                             e.reciprocal(recip_sb[64:65, :], i), inc=tV)
                    pend_norm = dict(rc=rc, ctslot=ctslot, cti=cti, smslot=smslot,
                                     smi=smi, j=j, win0=win0)
            emit_norm(pend_norm)
            pend_norm = None
            prev_scores_done = p.c(tT)
            wq_ring.release(wqi, tT, prev_scores_done)
            wk_ring.release(wki, tT, prev_scores_done)
        p.flush_block()
    st_A.close()

    x2 = st_R1.enter_context(nc.sbuf_tensor("x2", [128, NCH * S], F32, side="right"))
    x2bf = st_R2.enter_context(nc.sbuf_tensor("x2bf", [128, NCH * S], BF16, side="right"))
    with ExitStack() as ph:
        wo_sb = ph.enter_context(nc.sbuf_tensor("wo_sb", [128, 8 * 1024], BF16))
        stage = [ph.enter_context(nc.sbuf_tensor(f"stage{i}", [128, 4096], BF16)) for i in range(2)]
        xres = [ph.enter_context(nc.sbuf_tensor("xres0", [128, D], F32))]
        lnb = [ph.enter_context(nc.sbuf_tensor(f"lnb{i}", [128, D], F32)) for i in range(2)]
        lnb2 = [ph.enter_context(nc.sbuf_tensor("lnb20", [128, D], F32))]
        sqs = ph.enter_context(nc.sbuf_tensor("sqs", [128, D], F32))
        stt = ph.enter_context(nc.sbuf_tensor("stt", [128, 8], F32))
        pO = [ph.enter_context(nc.psum_tensor(f"pO{i}", [128, 512], F32)) for i in range(2)]
        pL = [ph.enter_context(nc.psum_tensor(f"pL{i}", [128, D], BF16)) for i in range(2)]
        pB2 = [ph.enter_context(nc.psum_tensor("pB20", [128, D], F32))]

        p.wait("scalar", dC, consts_d)
        p.wait("vector", dC, consts_d)
        p.wait("tensor", dC, consts_d)
        for k in range(8):
            p.dma("gpsimd", wo_sb[:, k * 1024:(k + 1) * 1024],
                  d["wor"][k].rearrange("p k c -> p (k c)"), dWo)
        wo_d = p.c(dWo)
        p.wait("tensor", dWo, wo_d)

        po_ring = Ring(p, pO)
        pl_ring = Ring(p, pL)
        pb_ring = Ring(p, pB2)
        xr_ring = DmaRing(p, xres, "dRx")
        ln_ring = Ring(p, lnb)
        ln2_ring = Ring(p, lnb2)
        stage_ring = Ring(p, stage)

        lnb2 = [ph.enter_context(nc.sbuf_tensor(f"lnb2x{i}", [128, D], F32)) for i in range(3)]
        ln2_ring = Ring(p, lnb2 + ln2_ring.slots)
        pB2b = ph.enter_context(nc.psum_tensor("pB21", [128, D], F32))
        pb_ring = Ring(p, pb_ring.slots + [pB2b])

        def emit_A(g, stg, ev_ticks, out_list):
            for tl in range(4):
                tok0 = g * 512 + tl * 128
                xrs, xri, xtok = xr_ring.load("sync", d["x"][tok0:tok0 + 128, :])
                pls, pli = pl_ring.acquire("tensor")
                for ds in range(8):
                    p.wait("tensor", tA, ev_ticks[ds])
                    p.op("tensor",
                         lambda e, o=pls[:, ds * 128:(ds + 1) * 128],
                         i=stg[:, ds * 512 + tl * 128: ds * 512 + tl * 128 + 128]:
                         e.transpose(o, i, identb[:, :]),
                         inc=(tT if ds == 7 else None))
                tp = p.c(tT)
                lns, lni = ln_ring.acquire("vector")
                p.wait("vector", tT, tp)
                xr_ring.wait_loaded("vector", xtok)
                lv = p.op("vector",
                          lambda e, o=lns[:, :], i=pls[:, :], r=xrs[:, :]:
                          e.scalar_tensor_tensor(o, i, 1.0, r, ALU.mult, ALU.add),
                          inc=tV)
                pl_ring.release(pli, tV, lv)
                xr_ring.release(xri, tV, lv)
                p.wait("scalar", tV, lv)
                a1 = p.op("scalar", lambda e, o=sqs[:, :], i=lns[:, :]:
                          e.activation(out=o, in_=i, func=AF.Square,
                                       accum_out=stt[:, 0:1]), inc=tA)
                v1 = p.op("vector", lambda e, i=lns[:, :]:
                          e.reduce_sum(stt[:, 1:2], i, axis=AX.X, negate=True), inc=tV)
                v2 = p.op("vector", lambda e:
                          e.tensor_scalar_mul(stt[:, 2:3], stt[:, 1:2], 1.0 / D), inc=tV)
                p.wait("vector", tV, v2)
                v3 = p.op("vector", lambda e:
                          e.tensor_tensor(stt[:, 3:4], stt[:, 2:3], stt[:, 2:3], ALU.mult),
                          inc=tV)
                p.wait("vector", tA, a1)
                v4 = p.op("vector", lambda e:
                          e.scalar_tensor_tensor(stt[:, 4:5], stt[:, 0:1], 1.0 / D,
                                                 stt[:, 3:4], ALU.mult, ALU.subtract),
                          inc=tV)
                p.wait("scalar", tV, v4)
                a2 = p.op("scalar", lambda e:
                          e.activation(out=stt[:, 5:6], in_=stt[:, 4:5], func=AF.Sqrt,
                                       bias=cst[:, 2:3]), inc=tA)
                p.wait("vector", tA, a2)
                v5 = p.op("vector", lambda e:
                          e.reciprocal(stt[:, 6:7], stt[:, 5:6]), inc=tV)
                p.wait("vector", tV, v5)
                v6 = p.op("vector", lambda e:
                          e.tensor_scalar_mul(stt[:, 7:8], stt[:, 2:3], stt[:, 6:7]),
                          inc=tV)
                ln2s, ln2i = ln2_ring.acquire("scalar")
                p.wait("scalar", tV, v6)
                a3 = p.op("scalar", lambda e, o=ln2s[:, :], i=lns[:, :]:
                          e.activation(out=o, in_=i, func=AF.Identity,
                                       scale=stt[:, 6:7], bias=stt[:, 7:8]), inc=tA)
                ln_ring.release(lni, tA, a3)
                out_list.append((tok0, ln2s, ln2i, a3))

        def emit_B(items):
            for (tok0, ln2s, ln2i, a3) in items:
                pbs, pbi = pb_ring.acquire("tensor")
                p.wait("tensor", tA, a3)
                for ds in range(8):
                    p.op("tensor",
                         lambda e, o=pbs[:, ds * 128:(ds + 1) * 128],
                         i=ln2s[:, ds * 128:(ds + 1) * 128]:
                         e.transpose(o, i, identf[:, :]),
                         inc=(tT if ds == 7 else None))
                tb = p.c(tT)
                ln2_ring.release(ln2i, tT, tb)
                p.wait("vector", tT, tb)
                x2v = x2.rearrange("p (c s) -> p c s", c=NCH)[:, :, tok0:tok0 + 128]
                vv = p.op("vector", lambda e, o=x2v, i=pbs[:, :]:
                          e.tensor_copy(o, i), inc=tV)
                pb_ring.release(pbi, tV, vv)
                p.wait("scalar", tV, vv)
                x2bv = x2bf.rearrange("p (c s) -> p c s", c=NCH)[:, :, tok0:tok0 + 128]
                p.op("scalar", lambda e, o=x2bv, i=x2v:
                     e.activation(out=o, in_=i, func=AF.Copy), inc=tA)

        pend_B = None
        for g in range(4):
            stg, sgi = stage_ring.acquire("scalar")
            ev_ticks = []
            for ds in range(8):
                slot, si = po_ring.acquire("tensor")
                for k in range(NCH):
                    p.op("tensor",
                         lambda e, o=slot[:, :],
                         l=wo_sb[:, ds * 1024 + k * 128: ds * 1024 + k * 128 + 128],
                         r=ctx[:, k * S + g * 512: k * S + g * 512 + 512],
                         st=(k == 0), sp=(k == NCH - 1):
                         e.matmul(o, l, r, start=st, stop=sp),
                         inc=(tT if k == NCH - 1 else None))
                mm = p.c(tT)
                p.wait("scalar", tT, mm)
                ev = p.op("scalar",
                          lambda e, o=stg[:, ds * 512:(ds + 1) * 512], i=slot[:, :],
                          b=bo_sb[:, ds:ds + 1]:
                          e.activation(out=o, in_=i, func=AF.Identity, bias=b), inc=tA)
                ev_ticks.append(ev)
                po_ring.release(si, tA, ev)
            if pend_B is not None:
                emit_B(pend_B)
                pend_B = None
            items = []
            emit_A(g, stg, ev_ticks, items)
            stage_ring.release(sgi, tT, p.c(tT))
            pend_B = items
        emit_B(pend_B)
        x2bf_done = p.c(tA)
        x2_done = p.c(tV)
        p.flush_block()
    st_B.close()

    with ExitStack() as ph:
        wr1_sb = ph.enter_context(nc.sbuf_tensor("wr1_sb", [128, 2048], BF16))
        wr2_sb = ph.enter_context(nc.sbuf_tensor("wr2_sb", [128, 16], BF16))
        r1bf = ph.enter_context(nc.sbuf_tensor("r1bf", [128, 2 * S], BF16))
        br2b = ph.enter_context(nc.sbuf_tensor("br2b", [128, 8], F32))
        lg = ph.enter_context(nc.sbuf_tensor("lg", [128, 8], F32))
        g8 = ph.enter_context(nc.sbuf_tensor("g8", [128, 8], F32))
        eq = ph.enter_context(nc.sbuf_tensor("eq", [128, 8], F32))
        g2 = ph.enter_context(nc.sbuf_tensor("g2", [128, 8], F32))
        msk = ph.enter_context(nc.sbuf_tensor("msk", [128, 8], F32))
        eh = ph.enter_context(nc.sbuf_tensor("eh", [128, 8], F32))
        wu = ph.enter_context(nc.sbuf_tensor("wu", [128, 8], F32))
        rst = ph.enter_context(nc.sbuf_tensor("rst", [128, 8], F32))
        wtile = ph.enter_context(nc.sbuf_tensor("wtile", [128, 128], F32))
        pR1 = [ph.enter_context(nc.psum_tensor(f"pR1{i}", [128, 512], F32)) for i in range(2)]
        pLg = [ph.enter_context(nc.psum_tensor(f"pLg{i}", [128, 8], F32)) for i in range(2)]
        pWt = ph.enter_context(nc.psum_tensor("pWt", [128, 128], F32))

        p.wait("vector", dC, consts_d)
        p.wait("scalar", dC, consts_d)
        p.wait("tensor", dC, consts_d)
        for j2 in range(2):
            p.dma("gpsimd", wr1_sb[:, j2 * 1024:(j2 + 1) * 1024],
                  d["wr1r"][j2].rearrange("p k c -> p (k c)"), dWr)
        for k2 in range(2):
            p.dma("gpsimd", wr2_sb[:, k2 * 8:(k2 + 1) * 8],
                  d["wr2"][k2 * 128:(k2 + 1) * 128, :], dWr)
        wr_d = p.c(dWr)
        p.wait("tensor", dWr, wr_d)
        bt = p.op("tensor", lambda e: e.matmul(pLg[0][:, :], ones_f[:, :], br2row[0:1, :],
                                               start=True, stop=True), inc=tT)
        p.wait("vector", tT, bt)
        bb = p.op("vector", lambda e: e.tensor_copy(br2b[:, :], pLg[0][:, :]), inc=tV)
        pr_ring = Ring(p, pR1)
        p.wait("tensor", tA, x2bf_done)
        for j2 in range(2):
            for g in range(4):
                slot, si = pr_ring.acquire("tensor")
                for k in range(NCH):
                    p.op("tensor",
                         lambda e, o=slot[:, :],
                         l=wr1_sb[:, j2 * 1024 + k * 128: j2 * 1024 + k * 128 + 128],
                         r=x2bf[:, k * S + g * 512: k * S + g * 512 + 512],
                         st=(k == 0), sp=(k == NCH - 1):
                         e.matmul(o, l, r, start=st, stop=sp),
                         inc=(tT if k == NCH - 1 else None))
                mm = p.c(tT)
                p.wait("scalar", tT, mm)
                ev = p.op("scalar",
                          lambda e, o=r1bf[:, j2 * S + g * 512: j2 * S + g * 512 + 512],
                          i=slot[:, :], b=br1_sb[:, j2:j2 + 1]:
                          e.activation(out=o, in_=i, func=AF.Relu, bias=b), inc=tA)
                pr_ring.release(si, tA, ev)
        r1_done = p.c(tA)
        p.wait("tensor", tA, r1_done)
        p.wait("tensor", tV, bb)

        lgall = ph.enter_context(nc.sbuf_tensor("lgall", [128, 128], F32))
        plg_ring = Ring(p, pLg)
        for tt in range(NT):
            lgs, lgi = plg_ring.acquire("tensor")
            for k2 in range(2):
                p.op("tensor",
                     lambda e, o=lgs[:, :],
                     l=r1bf[:, k2 * S + tt * 128: k2 * S + tt * 128 + 128],
                     r=wr2_sb[:, k2 * 8:(k2 + 1) * 8],
                     st=(k2 == 0), sp=(k2 == 1):
                     e.matmul(o, l, r, start=st, stop=sp),
                     inc=(tT if k2 == 1 else None))
            lt = p.c(tT)
            p.wait("vector", tT, lt)
            lgr = p.op("vector", lambda e, o=lgall[:, tt * 8:(tt + 1) * 8], i=lgs[:, :]:
                       e.tensor_copy(o, i), inc=tV)
            plg_ring.release(lgi, tV, lgr)

        def t16(buf):
            return buf.rearrange("p (t e) -> p t e", t=16)

        st16 = ph.enter_context(nc.sbuf_tensor("st16", [128, 96], F32))

        def s16(i):
            return st16[:, i * 16:(i + 1) * 16]

        def bc16(i):
            return bass.AP(st16, i * 16, [[96, 128], [1, 16], [0, 8]])

        p.wait("vector", tV, bb)
        br2bc = bass.AP(br2b, 0, [[8, 128], [0, 16], [1, 8]])
        p.op("vector", lambda e: e.tensor_tensor(t16(lgall), t16(lgall), br2bc, ALU.add),
             inc=tV)
        p.op("vector", lambda e: e.reduce_max(s16(0), t16(lgall), axis=AX.X), inc=tV)
        m1t = p.c(tV)
        p.op("vector", lambda e: e.tensor_tensor(t16(lgall), t16(lgall), bc16(0),
                                                 ALU.subtract), inc=tV)
        sb_t = p.c(tV)
        p.wait("scalar", tV, sb_t)
        exa = p.op("scalar", lambda e: e.activation(out=lgall[:, :], in_=lgall[:, :],
                                                    func=AF.Exp), inc=tA)
        p.wait("vector", tA, exa)
        p.op("vector", lambda e: e.reduce_sum(s16(1), t16(lgall), axis=AX.X), inc=tV)
        p.op("vector", lambda e: e.reciprocal(s16(2), s16(1)), inc=tV)
        p.op("vector", lambda e: e.tensor_tensor(t16(lgall), t16(lgall), bc16(2),
                                                 ALU.mult), inc=tV)
        p.op("vector", lambda e: e.reduce_max(s16(3), t16(lgall), axis=AX.X), inc=tV)
        p.op("vector", lambda e, o=wtile[:, :]: e.tensor_tensor(t16(o), t16(lgall),
                                                                bc16(3), ALU.is_equal),
             inc=tV)
        p.op("vector", lambda e, o=wtile[:, :]:
             e.scalar_tensor_tensor(t16(o), t16(o), -1e30, t16(lgall), ALU.mult, ALU.add),
             inc=tV)
        p.op("vector", lambda e, o=wtile[:, :]: e.reduce_max(s16(4), t16(o), axis=AX.X),
             inc=tV)
        p.op("vector", lambda e, o=wtile[:, :]: e.tensor_tensor(t16(o), t16(lgall),
                                                                bc16(4), ALU.is_ge),
             inc=tV)
        mk16 = p.c(tV)
        p.wait("scalar", tV, mk16)
        eh16 = p.op("scalar", lambda e: e.activation(out=lgall[:, :], in_=lgall[:, :],
                                                     func=AF.Exp, scale=cst[:, 3:4]),
                    inc=tA)
        p.wait("vector", tA, eh16)
        p.op("vector", lambda e, o=wtile[:, :]: e.tensor_tensor(o, o, lgall[:, :],
                                                                ALU.mult), inc=tV)
        p.op("vector", lambda e, o=wtile[:, :]: e.reduce_sum(s16(5), t16(o), axis=AX.X),
             inc=tV)
        p.op("vector", lambda e: e.reciprocal(s16(0), s16(5)), inc=tV)
        wv_t = p.op("vector", lambda e, o=wtile[:, :]:
                    e.tensor_tensor(t16(o), t16(o), bc16(0), ALU.mult), inc=tV)
        p.wait("tensor", tV, wv_t)
        for tt in range(NT):
            wt_t = p.op("tensor", lambda e, i=wtile[:, tt * 8:(tt + 1) * 8]:
                        e.transpose(pWt[0:8, 0:128], i, identf[:, :]), inc=tT)
            p.wait("vector", tT, wt_t)
            wc = p.op("vector", lambda e, o=wT[0:8, tt * 128:(tt + 1) * 128]:
                      e.tensor_copy(o, pWt[0:8, 0:128]), inc=tV)
            p.wait("tensor", tV, wc)
        p.flush_block()

    with ExitStack() as ph:
        h1bf = ph.enter_context(nc.sbuf_tensor("h1bf", [128, 16 * S], BF16))
        w1s = [ph.enter_context(nc.sbuf_tensor(f"w1s{i}", [128, 4096], BF16)) for i in range(2)]
        w2s = [ph.enter_context(nc.sbuf_tensor(f"w2s{i}", [128, 4096], BF16)) for i in range(2)]
        wb = [ph.enter_context(nc.sbuf_tensor("wb0", [128, S], BF16))] * 2
        tmp5 = [ph.enter_context(nc.sbuf_tensor("tmp50", [128, 512], F32))]
        pE1 = [ph.enter_context(nc.psum_tensor(f"pE1{i}", [128, 512], F32)) for i in range(3)]
        pE2 = [ph.enter_context(nc.psum_tensor(f"pE2{i}", [128, 512], F32)) for i in range(3)]
        pWb = ph.enter_context(nc.psum_tensor("pWb", [128, 512], F32))

        w1_ring = DmaRing(p, w1s, "dW1")
        w2_ring = DmaRing(p, w2s, "dW2")
        p1_ring = Ring(p, pE1)
        p2_ring = Ring(p, pE2)
        tmp_ring = Ring(p, tmp5)
        pwb_free = 0
        shared_evac_done = 0
        x2_add_tick = {}

        p.wait("scalar", dC, consts_d)
        p.wait("vector", dC, consts_d)
        p.wait("tensor", tA, x2bf_done)
        p.wait("vector", tV, x2_done)

        h1_consumed = 0
        for e_i in range(NE):
            if e_i >= 1:
                wb_ready = 0
                for g in range(4):
                    p.wait("tensor", tV, pwb_free)
                    bt = p.op("tensor",
                              lambda e, l=sel8[0:8, (e_i - 1) * 128: e_i * 128],
                              r=wT[0:8, g * 512:(g + 1) * 512]:
                              e.matmul(pWb[:, :], l, r, start=True, stop=True),
                              inc=tT)
                    p.wait("vector", tT, bt)
                    bv2 = p.op("vector", lambda e, o=wb[e_i % 2][:, g * 512:(g + 1) * 512]:
                               e.tensor_copy(o, pWb[:, :]), inc=tV)
                    pwb_free = bv2
                wb_ready = p.c(tV)
            for Hh in range(2):
                for ocg in range(4):
                    if Hh == 0 and ocg == 0:
                        p.wait("sync", dPC[e_i], 64)
                    slot, si, w1tok = w1_ring.load(
                        "sync", w1bf[e_i, Hh * 4 + ocg])
                    w1_ring.wait_loaded("tensor", w1tok)
                    for oi in range(4):
                        oc = ocg * 4 + oi
                        goc = Hh * 16 + oc
                        for g in range(4):
                            ps1, pi1 = p1_ring.acquire("tensor")
                            for k in range(NCH):
                                p.op("tensor",
                                     lambda e, o=ps1[:, :],
                                     l=slot[:, oi * 1024 + k * 128: oi * 1024 + k * 128 + 128],
                                     r=x2bf[:, k * S + g * 512: k * S + g * 512 + 512],
                                     st=(k == 0), sp=(k == NCH - 1):
                                     e.matmul(o, l, r, start=st, stop=sp),
                                     inc=(tT if k == NCH - 1 else None))
                            mm = p.c(tT)
                            if g == 3 and oi == 3:
                                w1_ring.release(si, tT, mm)
                            p.wait("scalar", tT, mm)
                            if oc == 0 and g == 0 and (Hh == 1 or e_i > 0):
                                p.wait("scalar", tT, h1_consumed)
                            ev = p.op("scalar",
                                      lambda e, o=h1bf[:, oc * S + g * 512: oc * S + g * 512 + 512],
                                      i=ps1[:, :], b=b1_sb[:, e_i * 32 + goc: e_i * 32 + goc + 1]:
                                      e.activation(out=o, in_=i, func=AF.Silu, bias=b), inc=tA)
                            p1_ring.release(pi1, tA, ev)
                h1_done = p.c(tA)
                for ddg in range(4):
                    slot2, si2, w2tok = w2_ring.load("sync", w2bf[e_i, Hh, ddg])
                    w2_ring.wait_loaded("tensor", w2tok)
                    if ddg == 0:
                        p.wait("tensor", tA, h1_done)
                    for di in range(2):
                      dd = ddg * 2 + di
                      for g in range(4):
                        ps2, pi2 = p2_ring.acquire("tensor")
                        for kc in range(16):
                            p.op("tensor",
                                 lambda e, o=ps2[:, :],
                                 l=slot2[:, di * 2048 + kc * 128: di * 2048 + kc * 128 + 128],
                                 r=h1bf[:, kc * S + g * 512: kc * S + g * 512 + 512],
                                 st=(kc == 0), sp=(kc == 15):
                                 e.matmul(o, l, r, start=st, stop=sp),
                                 inc=(tT if kc == 15 else None))
                        mm2 = p.c(tT)
                        if g == 3 and di == 1:
                            w2_ring.release(si2, tT, mm2)
                        x2slice = x2[:, dd * S + g * 512: dd * S + g * 512 + 512]
                        b2ap = (b2_sb[:, e_i * 8 + dd: e_i * 8 + dd + 1]
                                if Hh == 1 else 0.0)
                        p.wait("vector", tT, mm2)
                        if e_i == 0:
                            ev2 = p.op("vector",
                                       lambda e, o=x2slice, i=ps2[:, :], b=b2ap:
                                       e.scalar_tensor_tensor(o, i, b, o, ALU.add, ALU.add),
                                       inc=tV)
                            p2_ring.release(pi2, tV, ev2)
                            shared_evac_done = ev2
                        else:
                            if dd == 0 and g == 0 and Hh == 0:
                                p.wait("vector", tV, wb_ready)
                            tms, tmi = tmp_ring.acquire("vector")
                            ev2 = p.op("vector",
                                       lambda e, o=tms[:, :], i=ps2[:, :], b=b2ap,
                                       w=wb[e_i % 2][:, g * 512:(g + 1) * 512]:
                                       e.scalar_tensor_tensor(o, i, b, w, ALU.add, ALU.mult),
                                       inc=tV)
                            p2_ring.release(pi2, tV, ev2)
                            p.wait("gpsimd", tV, ev2)
                            if e_i == 1 and dd == 0 and g == 0 and Hh == 0:
                                p.wait("gpsimd", tV, shared_evac_done)
                            p.wait("gpsimd", tG, x2_add_tick.get((dd, g), 0))
                            ga = p.op("gpsimd",
                                      lambda e, o=x2slice, t=tms[:, :]:
                                      e.tensor_tensor(o, o, t, ALU.add), inc=tG)
                            x2_add_tick[(dd, g)] = ga
                            tmp_ring.release(tmi, tG, ga)
                h1_consumed = p.c(tT)
        ffn_done_g = p.c(tG)
        ffn_done_v = p.c(tV)
        p.flush_block()
    st_R2.close()

    with ExitStack() as ph:
        lnb = [ph.enter_context(nc.sbuf_tensor(f"lnc{i}", [128, D], F32)) for i in range(2)]
        ybuf = [ph.enter_context(nc.sbuf_tensor(f"ybuf{i}", [128, D], F32)) for i in range(2)]
        sqs = ph.enter_context(nc.sbuf_tensor("sqs2", [128, D], F32))
        stt = ph.enter_context(nc.sbuf_tensor("stt2", [128, 8], F32))
        pL = [ph.enter_context(nc.psum_tensor(f"pL2{i}", [128, D], F32)) for i in range(2)]
        dY = [p.sem("dY0"), p.sem("dY1")]

        pl_ring = Ring(p, pL)
        ln_ring = Ring(p, lnb)
        p.wait("scalar", dC, consts_d)
        p.wait("vector", dC, consts_d)
        p.wait("tensor", dC, consts_d)
        for tt in range(NT):
            pls, pli = pl_ring.acquire("tensor")
            for ds in range(8):
                p.op("tensor",
                     lambda e, o=pls[:, ds * 128:(ds + 1) * 128],
                     i=x2[:, ds * S + tt * 128: ds * S + tt * 128 + 128]:
                     e.transpose(o, i, identf[:, :]),
                     inc=(tT if ds == 7 else None))
            tp = p.c(tT)
            lns, lni = ln_ring.acquire("vector")
            p.wait("vector", tT, tp)
            lv = p.op("vector", lambda e, o=lns[:, :], i=pls[:, :]:
                      e.tensor_copy(o, i), inc=tV)
            pl_ring.release(pli, tV, lv)
            p.wait("scalar", tV, lv)
            a1 = p.op("scalar", lambda e, o=sqs[:, :], i=lns[:, :]:
                      e.activation(out=o, in_=i, func=AF.Square,
                                   accum_out=stt[:, 0:1]), inc=tA)
            v1 = p.op("vector", lambda e, i=lns[:, :]:
                      e.reduce_sum(stt[:, 1:2], i, axis=AX.X, negate=True), inc=tV)
            v2 = p.op("vector", lambda e:
                      e.tensor_scalar_mul(stt[:, 2:3], stt[:, 1:2], 1.0 / D), inc=tV)
            p.wait("vector", tV, v2)
            v3 = p.op("vector", lambda e:
                      e.tensor_tensor(stt[:, 3:4], stt[:, 2:3], stt[:, 2:3], ALU.mult),
                      inc=tV)
            p.wait("vector", tA, a1)
            v4 = p.op("vector", lambda e:
                      e.scalar_tensor_tensor(stt[:, 4:5], stt[:, 0:1], 1.0 / D,
                                             stt[:, 3:4], ALU.mult, ALU.subtract), inc=tV)
            p.wait("scalar", tV, v4)
            a2 = p.op("scalar", lambda e:
                      e.activation(out=stt[:, 5:6], in_=stt[:, 4:5], func=AF.Sqrt,
                                   bias=cst[:, 2:3]), inc=tA)
            p.wait("vector", tA, a2)
            v5 = p.op("vector", lambda e: e.reciprocal(stt[:, 6:7], stt[:, 5:6]), inc=tV)
            p.wait("vector", tV, v5)
            v6 = p.op("vector", lambda e:
                      e.tensor_scalar_mul(stt[:, 7:8], stt[:, 2:3], stt[:, 6:7]), inc=tV)
            p.wait("scalar", tV, v6)
            if tt >= 2:
                p.wait("scalar", dY[tt % 2], (tt // 2) * 16)
            ybs = ybuf[tt % 2]
            a3 = p.op("scalar", lambda e, o=ybs[:, :], i=lns[:, :]:
                      e.activation(out=o, in_=i, func=AF.Identity,
                                   scale=stt[:, 6:7], bias=stt[:, 7:8]), inc=tA)
            ln_ring.release(lni, tA, a3)
            p.wait("sync", tA, a3)
            p.dma("sync", y[tt * 128:(tt + 1) * 128, :], ybs[:, :], dY[tt % 2])
        p.wait("sync", dY[0], p.c(dY[0]))
        p.wait("sync", dY[1], p.c(dY[1]))
        p.flush_block()

    st_R1.close()
    es.close()
    return nc


def _host_pack(inputs):
    f32 = np.float32
    bf = ml_dtypes.bfloat16
    ca = np.ascontiguousarray
    ins = {}
    ins["wv"] = ca(inputs["L0_attn_Wv"], dtype=f32)
    for nm, key in (("wqr", "L0_attn_Wq"), ("wkr", "L0_attn_Wk"), ("wor", "L0_attn_Wo")):
        w = np.asarray(inputs[key], dtype=f32)
        ins[nm] = ca(w.reshape(8, 128, 8, 128).transpose(2, 1, 0, 3))
    ins["bqr"] = ca(np.asarray(inputs["L0_attn_bq"], dtype=f32).reshape(8, 128).T)
    ins["bkr"] = ca(np.asarray(inputs["L0_attn_bk"], dtype=f32).reshape(8, 128).T)
    ins["bor"] = ca(np.asarray(inputs["L0_attn_bo"], dtype=f32).reshape(8, 128).T)
    ins["bv"] = ca(np.asarray(inputs["L0_attn_bv"], dtype=f32).reshape(1, D))
    ins["wr1r"] = ca(np.asarray(inputs["L0_ffn_Wr1"], dtype=f32)
                     .reshape(8, 128, 2, 128).transpose(2, 1, 0, 3))
    ins["wr2"] = ca(inputs["L0_ffn_Wr2"], dtype=f32)
    ins["br1r"] = ca(np.asarray(inputs["L0_ffn_br1"], dtype=f32).reshape(2, 128).T)
    ins["br2"] = ca(np.asarray(inputs["L0_ffn_br2"], dtype=f32).reshape(1, 8))
    w1 = np.concatenate([np.asarray(inputs["L0_ffn_Ws1"], dtype=f32)[None],
                         np.asarray(inputs["L0_ffn_We1"], dtype=f32)], axis=0)
    ins["w1r"] = ca(w1.reshape(NE, 8, 128, 8, 4, 128).transpose(0, 3, 2, 4, 1, 5)
                    .reshape(NE, 8, 128, 4096))
    w2 = np.concatenate([np.asarray(inputs["L0_ffn_Ws2"], dtype=f32)[None],
                         np.asarray(inputs["L0_ffn_We2"], dtype=f32)], axis=0)
    ins["w2r"] = ca(w2.reshape(NE, 2, 16, 128, 4, 2, 128).transpose(0, 1, 4, 3, 5, 2, 6)
                    .reshape(NE, 2, 4, 128, 4096))
    b1 = np.concatenate([np.asarray(inputs["L0_ffn_bs1"], dtype=f32)[None],
                         np.asarray(inputs["L0_ffn_be1"], dtype=f32)], axis=0)
    ins["b1r"] = ca(b1.reshape(NE, 32, 128).transpose(2, 0, 1).reshape(128, NE * 32))
    b2 = np.concatenate([np.asarray(inputs["L0_ffn_bs2"], dtype=f32)[None],
                         np.asarray(inputs["L0_ffn_be2"], dtype=f32)], axis=0)
    ins["b2r"] = ca(b2.reshape(NE, 8, 128).transpose(2, 0, 1).reshape(128, NE * 8))

    hd = 64
    pos = np.arange(S, dtype=f32)
    inv_freq = np.exp(np.arange(0, hd, 2, dtype=f32) * f32(-np.log(10000.0) / hd))
    ang = pos[:, None] * inv_freq[None, :]
    sin = np.sin(ang.astype(f32)).astype(f32)
    cos = np.cos(ang.astype(f32)).astype(f32)
    pmod = np.arange(128) % 32
    ins["ropec"] = ca(cos[:, pmod].T)
    sgn = np.where((np.arange(128) % 64) < 32, -1.0, 1.0).astype(f32)
    ins["ropes"] = ca(sin[:, pmod].T * sgn[:, None])
    P = np.zeros((128, 128), dtype=f32)
    for m in range(128):
        src = m + 32 if (m % 64) < 32 else m - 32
        P[src, m] = 1.0
    ins["prm"] = ca(P.astype(bf))
    ins["identb"] = ca(np.eye(128, dtype=bf))
    ins["identf"] = ca(np.eye(128, dtype=f32))
    k_idx = np.arange(128)[:, None]
    i_idx = np.arange(896)[None, :]
    ins["trit"] = ca(np.where(i_idx < k_idx + 384, -10000.0, 0.0).astype(bf))
    sel = np.zeros((8, 8, 128), dtype=f32)
    for e in range(8):
        sel[e, e, :] = 1.0
    ins["sel8"] = ca(sel.transpose(1, 0, 2).reshape(8, 1024).astype(bf))
    s2 = np.zeros((65, 128), dtype=f32)
    s2[0, 0:64] = 1.0
    s2[64, 64:128] = 1.0
    ins["sel2"] = ca(s2)
    return ins


def kernel(**inputs):
    ins = _host_pack(inputs)
    x = np.asarray(inputs["L0_attn_x"], dtype=np.float32)
    nc = build_nc()
    in_maps = []
    for b in range(8):
        m = dict(ins)
        m["x"] = np.ascontiguousarray(x[b])
        in_maps.append(m)
    res = run_bass_kernel_spmd(nc, in_maps, core_ids=list(range(8)))
    return np.stack([r["y"] for r in res.results], axis=0)


# version 11
# speedup vs baseline: 1.1196x; 1.1124x over previous
import numpy as np
import ml_dtypes
from contextlib import ExitStack

import concourse.bass as bass
import concourse.mybir as mybir
from concourse.bass_utils import run_bass_kernel_spmd

F32 = mybir.dt.float32
BF16 = mybir.dt.bfloat16
AF = mybir.ActivationFunctionType
ALU = mybir.AluOpType
AX = mybir.AxisListType

S = 2048
D = 1024
NCH = 8
NE = 9
NT = 16
EXP_SHIFT = -15.0
EPS = 1e-5

ENGINES = ("sync", "gpsimd", "scalar", "vector", "tensor")


class Prog:

    SELF_FENCE = {"vector": "tV", "scalar": "tA", "gpsimd": "tG"}

    def __init__(self, nc, es):
        self.nc = nc
        self.es = es
        self.q = {e: [] for e in ENGINES}
        self.sems = {}
        self.cnt = {}
        self._lastwait = {}

    def sem(self, name):
        self.sems[name] = self.es.enter_context(self.nc.semaphore(name))
        self.cnt[name] = 0
        return name

    def wait(self, eng, sem, val):
        if val <= 0:
            return
        if sem == self.SELF_FENCE.get(eng):
            if self._lastwait.get(eng, 0) >= val:
                return
            self._lastwait[eng] = val
        h = self.sems[sem]
        self.q[eng].append(lambda e, h=h, v=val: e.wait_ge(h, v))

    def op(self, eng, build, inc=None, n=1):
        own = self.SELF_FENCE.get(eng)
        if own is not None:
            self.wait(eng, own, self.cnt[own])
            if inc is None:
                inc, n = own, 1
        if inc is None:
            self.q[eng].append(build)
            return None
        self.cnt[inc] += n
        h = self.sems[inc]
        self.q[eng].append(lambda e, b=build, h=h, k=n: b(e).then_inc(h, k))
        return self.cnt[inc]

    def dma(self, eng, out, in_, sem):
        self.cnt[sem] += 16
        h = self.sems[sem]
        self.q[eng].append(lambda e, o=out, i=in_, h=h: e.dma_start(out=o, in_=i).then_inc(h, 16))
        return self.cnt[sem]

    def c(self, sem):
        return self.cnt[sem]

    def flush_block(self):
        q = self.q
        self.q = {e: [] for e in ENGINES}
        with self.nc.Block() as blk:
            blk.sync(lambda e, fns=q["sync"]: [f(e) for f in fns])
            blk.gpsimd(lambda e, fns=q["gpsimd"]: [f(e) for f in fns])
            blk.scalar(lambda e, fns=q["scalar"]: [f(e) for f in fns])
            blk.vector(lambda e, fns=q["vector"]: [f(e) for f in fns])
            blk.tensor(lambda e, fns=q["tensor"]: [f(e) for f in fns])


class Ring:

    def __init__(self, prog, slots):
        self.p = prog
        self.slots = list(slots)
        self.n = len(self.slots)
        self.free_at = [[] for _ in range(self.n)]
        self.i = 0

    def acquire(self, eng):
        s = self.i % self.n
        self.i += 1
        for (sem, tick) in self.free_at[s]:
            self.p.wait(eng, sem, tick)
        self.free_at[s] = []
        return self.slots[s], s

    def release(self, s, sem, tick):
        self.free_at[s].append((sem, tick))


class DmaRing:

    def __init__(self, prog, slots, name):
        self.p = prog
        self.slots = list(slots)
        self.n = len(self.slots)
        self.sems = [prog.sem(f"{name}{i}") for i in range(self.n)]
        self.free_at = [[] for _ in range(self.n)]
        self.i = 0

    def load(self, eng, src):
        s = self.i % self.n
        self.i += 1
        for (sem, tick) in self.free_at[s]:
            self.p.wait(eng, sem, tick)
        self.free_at[s] = []
        slot = self.slots[s]
        cnt = self.p.dma(eng, slot[:, :], src, self.sems[s])
        return slot, s, (self.sems[s], cnt)

    def wait_loaded(self, eng, token):
        self.p.wait(eng, token[0], token[1])

    def release(self, s, sem, tick):
        self.free_at[s].append((sem, tick))


def build_nc():
    nc = bass.Bass("TRN2", target_bir_lowering=False)

    d = {}
    def din(name, shape, dt=F32):
        d[name] = nc.dram_tensor(name, shape, dt, kind="ExternalInput").ap()
    din("x", [S, D])
    din("wqr", [8, 128, 8, 128]); din("wkr", [8, 128, 8, 128])
    din("wor", [8, 128, 8, 128]); din("wv", [D, D])
    din("bqr", [128, 8]); din("bkr", [128, 8]); din("bor", [128, 8])
    din("bv", [1, D])
    din("wr1r", [2, 128, 8, 128]); din("wr2", [256, 8])
    din("br1r", [128, 2]); din("br2", [1, 8])
    din("w1r", [NE, 8, 128, 4096])
    din("w2r", [NE, 2, 4, 128, 4096])
    din("b1r", [128, NE * 32]); din("b2r", [128, NE * 8])
    din("ropec", [128, S]); din("ropes", [128, S])
    din("prm", [128, 128], BF16)
    din("identb", [128, 128], BF16)
    din("identf", [128, 128])
    din("trit", [128, 896], BF16)
    din("sel8", [8, 1024], BF16)
    din("sel2", [65, 128])
    y = nc.dram_tensor("y", [S, D], F32, kind="ExternalOutput").ap()
    w1bf = nc.dram_tensor("w1bf", [NE, 8, 128, 4096], BF16).ap()
    w2bf = nc.dram_tensor("w2bf", [NE, 2, 4, 128, 4096], BF16).ap()

    es = ExitStack()
    E = es.enter_context
    p = Prog(nc, es)
    dC = p.sem("dC")
    dV = p.sem("dV")
    dWo = p.sem("dWo")
    dWr = p.sem("dWr")
    dR = p.sem("dR")
    dPC = [p.sem(f"dPC{e}") for e in range(NE)]
    tT = p.sem("tT"); tV = p.sem("tV"); tA = p.sem("tA"); tG = p.sem("tG")

    identf = E(nc.sbuf_tensor("identf_s", [128, 128], F32))
    ones_b = E(nc.sbuf_tensor("ones_b", [128, 1], BF16))
    ones_f = E(nc.sbuf_tensor("ones_f", [1, 128], F32))
    bq_sb = E(nc.sbuf_tensor("bq_sb", [128, 8], F32))
    bk_sb = E(nc.sbuf_tensor("bk_sb", [128, 8], F32))
    bo_sb = E(nc.sbuf_tensor("bo_sb", [128, 8], F32))
    br1_sb = E(nc.sbuf_tensor("br1_sb", [128, 2], F32))
    br2row = E(nc.sbuf_tensor("br2row", [1, 8], F32))
    b1_sb = E(nc.sbuf_tensor("b1_sb", [128, NE * 32], F32))
    b2_sb = E(nc.sbuf_tensor("b2_sb", [128, NE * 8], F32))
    wT = E(nc.sbuf_tensor("wT", [8, S], BF16))
    cst = E(nc.sbuf_tensor("cst", [128, 4], F32))
    sel8 = E(nc.sbuf_tensor("sel8_s", [8, 1024], BF16))
    sel2 = E(nc.sbuf_tensor("sel2_s", [65, 128], F32))

    identb = E(nc.sbuf_tensor("identb_s", [128, 128], BF16))
    st_B = ExitStack()
    ctx = st_B.enter_context(nc.sbuf_tensor("ctx", [128, NCH * S], BF16))
    st_A = ExitStack()
    prm = st_A.enter_context(nc.sbuf_tensor("prm_s", [128, 128], BF16))
    trit = st_A.enter_context(nc.sbuf_tensor("trit_s", [128, 896], BF16))
    bvrow = st_A.enter_context(nc.sbuf_tensor("bvrow", [1, D], F32))
    xbf = st_A.enter_context(nc.sbuf_tensor("xbf", [128, NCH * S], BF16))
    vbuf = st_A.enter_context(nc.sbuf_tensor("vbuf", [128, NT * D], BF16))
    ropec = st_A.enter_context(nc.sbuf_tensor("ropec_s", [128, S], F32))
    ropes = st_A.enter_context(nc.sbuf_tensor("ropes_s", [128, S], F32))
    st_R1 = ExitStack()
    st_R2 = ExitStack()

    def precast_expert(e_i):
        for hh2 in range(2):
            p.dma("gpsimd", w1bf[e_i][hh2 * 4:(hh2 + 1) * 4], d["w1r"][e_i][hh2 * 4:(hh2 + 1) * 4], dPC[e_i])
            p.dma("gpsimd", w2bf[e_i][hh2], d["w2r"][e_i][hh2], dPC[e_i])

    for name, dst in (("identf", identf), ("ropec", ropec), ("ropes", ropes),
                      ("prm", prm), ("identb", identb), ("trit", trit),
                      ("bqr", bq_sb), ("bkr", bk_sb), ("bor", bo_sb),
                      ("br1r", br1_sb), ("br2", br2row), ("bv", bvrow),
                      ("sel8", sel8), ("sel2", sel2)):
        p.dma("sync", dst[:, :], d[name], dC)
    p.dma("sync", b1_sb[:, :], d["b1r"], dC)
    p.dma("sync", b2_sb[:, :], d["b2r"], dC)
    idf_d = p.c(dC)
    consts_d = p.c(dC)
    p.op("gpsimd", lambda e: e.memset(ones_b[:, :], 1.0), inc=tG)
    g_ones = p.c(tG)
    p.op("vector", lambda e: e.memset(ones_f[:, :], 1.0), inc=tV)
    p.op("vector", lambda e: e.memset(cst[:, 0:1], EXP_SHIFT), inc=tV)
    p.op("vector", lambda e: e.memset(cst[:, 1:2], 0.125), inc=tV)
    p.op("vector", lambda e: e.memset(cst[:, 2:3], EPS), inc=tV)
    p.op("vector", lambda e: e.memset(cst[:, 3:4], 0.5), inc=tV)
    v_ones = p.c(tV)

    with ExitStack() as ph:
        xstage = [ph.enter_context(nc.sbuf_tensor(f"xstage{i}", [128, D], F32)) for i in range(2)]
        wv_sb = ph.enter_context(nc.sbuf_tensor("wv_sb", [128, 8 * D], BF16))
        bvb = ph.enter_context(nc.sbuf_tensor("bvb", [128, D], F32))
        pX = [ph.enter_context(nc.psum_tensor(f"pX{i}", [128, D], F32)) for i in range(2)]
        pV = [ph.enter_context(nc.psum_tensor(f"pV{i}", [128, 512], F32)) for i in range(2)]
        pBC = ph.enter_context(nc.psum_tensor("pBC", [128, 512], F32))

        xring = DmaRing(p, xstage, "dX")
        px_ring = Ring(p, pX)
        a_x0 = p.c(tA)
        xbf_ev = []
        p.wait("tensor", dC, idf_d)
        for tt in range(NT):
            slot, si, tok = xring.load("sync", d["x"][tt * 128:(tt + 1) * 128, :])
            xring.wait_loaded("tensor", tok)
            pxs, pxi = px_ring.acquire("tensor")
            for k in range(NCH):
                p.op("tensor",
                     lambda e, o=pxs[:, k * 128:(k + 1) * 128],
                     i=slot[:, k * 128:(k + 1) * 128]:
                     e.transpose(o, i, identf[:, :]),
                     inc=(tT if k == NCH - 1 else None))
            tp = p.c(tT)
            xring.release(si, tT, tp)
            p.wait("scalar", tT, tp)
            out_ap = xbf.rearrange("p (c s) -> p c s", c=NCH)[:, :, tt * 128:(tt + 1) * 128]
            ev = p.op("scalar", lambda e, o=out_ap, i=pxs[:, :]:
                      e.activation(out=o, in_=i, func=AF.Copy), inc=tA)
            xbf_ev.append(ev)
            px_ring.release(pxi, tA, ev)

        p.wait("tensor", dC, consts_d)
        p.wait("tensor", tV, v_ones)
        last_bc = 0
        for ng in range(2):
            if ng == 1:
                p.wait("tensor", tV, last_bc)
            bt = p.op("tensor", lambda e, o=pBC[:, :], i=bvrow[0:1, ng * 512:(ng + 1) * 512]:
                      e.matmul(o, ones_f[:, :], i, start=True, stop=True), inc=tT)
            p.wait("vector", tT, bt)
            last_bc = p.op("vector", lambda e, o=bvb[:, ng * 512:(ng + 1) * 512], i=pBC[:, :]:
                           e.tensor_copy(o, i), inc=tV)

        for k in range(NCH):
            p.dma("gpsimd", wv_sb[:, k * D:(k + 1) * D],
                  d["wv"][k * 128:(k + 1) * 128, :], dV)
        wv_d = p.c(dV)
        precast_expert(0)
        precast_expert(1)
        p.wait("tensor", dV, wv_d)
        pv_ring = Ring(p, pV)
        gi = 0
        for tt in range(NT):
            p.wait("tensor", tA, xbf_ev[tt])
            for ng in range(2):
                slot, si = pv_ring.acquire("tensor")
                for k in range(NCH):
                    p.op("tensor",
                         lambda e, o=slot[:, :],
                         l=xbf[:, k * S + tt * 128: k * S + tt * 128 + 128],
                         r=wv_sb[:, k * D + ng * 512: k * D + ng * 512 + 512],
                         st=(k == 0), sp=(k == NCH - 1):
                         e.matmul(o, l, r, start=st, stop=sp),
                         inc=(tT if k == NCH - 1 else None))
                mm = p.c(tT)
                p.wait("vector", tT, mm)
                if gi == 0:
                    p.wait("vector", tV, last_bc)
                ev = p.op("vector",
                          lambda e, o=vbuf[:, tt * D + ng * 512: tt * D + ng * 512 + 512],
                          i=slot[:, :], b=bvb[:, ng * 512:(ng + 1) * 512]:
                          e.scalar_tensor_tensor(o, i, 1.0, b, ALU.mult, ALU.add),
                          inc=tV)
                pv_ring.release(si, tV, ev)
                gi += 1
        p.flush_block()

    with ExitStack() as ph:
        wq_sb = [ph.enter_context(nc.sbuf_tensor(f"wq{i}", [128, 1024], BF16)) for i in range(2)]
        wk_sb = [ph.enter_context(nc.sbuf_tensor(f"wk{i}", [128, 1024], BF16)) for i in range(2)]
        q_sb = ph.enter_context(nc.sbuf_tensor("q_sb", [128, S], BF16))
        k_sb = ph.enter_context(nc.sbuf_tensor("k_sb", [128, S], BF16))
        qr_sb = ph.enter_context(nc.sbuf_tensor("qr_sb", [128, S], BF16))
        kr_sb = ph.enter_context(nc.sbuf_tensor("kr_sb", [128, S], BF16))
        t1f = [ph.enter_context(nc.sbuf_tensor(f"t1f{i}", [128, 512], F32)) for i in range(2)]
        et_sb = [ph.enter_context(nc.sbuf_tensor(f"et{i}", [128, 1024], BF16)) for i in range(3)]
        recip_sb = ph.enter_context(nc.sbuf_tensor("recip_sb", [65, 512], F32))
        recipb = [ph.enter_context(nc.sbuf_tensor(f"recipb{i}", [128, 512], F32)) for i in range(2)]
        pS = [ph.enter_context(nc.psum_tensor(f"pS{i}", [128, 1024], F32)) for i in range(2)]
        pCt = [ph.enter_context(nc.psum_tensor(f"pCt{i}", [128, 512], F32)) for i in range(2)]
        pSm = [ph.enter_context(nc.psum_tensor(f"pSm{i}", [128, 512], F32)) for i in range(2)]

        for eng in ("tensor", "scalar", "vector", "gpsimd"):
            p.wait(eng, dC, consts_d)
        rec_init = p.op("vector", lambda e: e.memset(recip_sb[:, :], 1.0), inc=tV)
        p.wait("tensor", tV, rec_init)

        wq_ring = DmaRing(p, wq_sb, "dQ")
        wk_ring = DmaRing(p, wk_sb, "dK")
        ps_ring = Ring(p, pS)
        ct_ring = Ring(p, pCt)
        sm_ring = Ring(p, pSm)
        t1_ring = Ring(p, t1f)
        et_ring = Ring(p, et_sb)
        rb_ring = Ring(p, recipb)
        prev_rope_done = 0
        prev_scores_done = 0

        for j in range(NCH):
            wqs, wqi, qtok = wq_ring.load("gpsimd", d["wqr"][j].rearrange("p k c -> p (k c)"))
            wks, wki, ktok = wk_ring.load("gpsimd", d["wkr"][j].rearrange("p k c -> p (k c)"))
            if j + 2 < NE:
                precast_expert(j + 2)
            wq_ring.wait_loaded("tensor", qtok)
            wk_ring.wait_loaded("tensor", ktok)

            evac_ticks = {}
            for (nm, wsb, src, dst, bias_sb) in (
                    ("q", wqs, q_sb, qr_sb, bq_sb), ("k", wks, k_sb, kr_sb, bk_sb)):
                for g in range(4):
                    slot, si = ps_ring.acquire("tensor")
                    slot = slot[:, 0:512]
                    for k in range(NCH):
                        p.op("tensor",
                             lambda e, o=slot[:, :],
                             l=wsb[:, k * 128:(k + 1) * 128],
                             r=xbf[:, k * S + g * 512: k * S + g * 512 + 512],
                             st=(k == 0), sp=(k == NCH - 1):
                             e.matmul(o, l, r, start=st, stop=sp),
                             inc=(tT if k == NCH - 1 else None))
                    mm = p.c(tT)
                    p.wait("vector", tT, mm)
                    if j > 0 and g == 0:
                        p.wait("vector", tG, prev_rope_done)
                    ev = p.op("vector",
                              lambda e, o=src[:, g * 512:(g + 1) * 512], i=slot[:, :],
                              b=bias_sb[:, j:j + 1]:
                              e.tensor_scalar_add(o, i, b),
                              inc=tV)
                    evac_ticks[(nm, g)] = ev
                    ps_ring.release(si, tV, ev)
                for g in range(4):
                    slot, si = ps_ring.acquire("tensor")
                    slot = slot[:, 0:512]
                    p.wait("tensor", tV, evac_ticks[(nm, g)])
                    rt = p.op("tensor",
                              lambda e, o=slot[:, :], r=src[:, g * 512:(g + 1) * 512]:
                              e.matmul(o, prm[:, :], r, start=True, stop=True),
                              inc=tT)
                    t1s, t1i = t1_ring.acquire("vector")
                    p.wait("vector", tT, rt)
                    vt = p.op("vector",
                              lambda e, o=t1s[:, :], i=slot[:, :],
                              s=ropes[:, g * 512:(g + 1) * 512]:
                              e.tensor_tensor(o, i, s, ALU.mult), inc=tV)
                    ps_ring.release(si, tV, vt)
                    if j > 0 and g == 0:
                        p.wait("gpsimd", tT, prev_scores_done)
                    p.wait("gpsimd", tV, evac_ticks[(nm, g)])
                    g1 = p.op("gpsimd",
                              lambda e, o=dst[:, g * 512:(g + 1) * 512],
                              i=src[:, g * 512:(g + 1) * 512],
                              c=ropec[:, g * 512:(g + 1) * 512]:
                              e.tensor_tensor(o, i, c, ALU.mult), inc=tG)
                    p.wait("gpsimd", tV, vt)
                    p.wait("gpsimd", tG, g1)
                    gt = p.op("gpsimd",
                              lambda e, o=dst[:, g * 512:(g + 1) * 512], t=t1s[:, :]:
                              e.tensor_tensor(o, o, t, ALU.add), inc=tG)
                    t1_ring.release(t1i, tG, gt)
            prev_rope_done = p.c(tG)
            p.wait("tensor", tG, prev_rope_done)

            hA, hB = 2 * j, 2 * j + 1

            def emit_av(pa):
                p.wait("tensor", tA, pa["ex"])
                kt, nkt = pa["kt"], pa["nkt"]
                first = (kt == 0)
                p.op("tensor",
                     lambda e, o=pa["ctslot"][0:64, :],
                     l=vbuf[:, kt * D + hA * 64: kt * D + hA * 64 + 64],
                     r=pa["ets"][:, 0:512], st=first, sp=False:
                     e.matmul(o, l, r, start=st, stop=sp, tile_position=(0, 0),
                              skip_group_check=True))
                p.op("tensor",
                     lambda e, o=pa["ctslot"][64:128, :],
                     l=vbuf[:, kt * D + hB * 64: kt * D + hB * 64 + 64],
                     r=pa["ets"][:, 512:1024], st=first, sp=(kt == nkt - 1):
                     e.matmul(o, l, r, start=st, stop=sp, tile_position=(0, 64),
                              skip_group_check=True))
                p.op("tensor",
                     lambda e, o=pa["smslot"][0:1, :], r=pa["ets"][:, 0:512],
                     st=first: e.matmul(o, ones_b[:, :], r, start=st, stop=False,
                                        skip_group_check=True))
                sm = p.op("tensor",
                          lambda e, o=pa["smslot"][64:65, :], r=pa["ets"][:, 512:1024],
                          st=first, sp=(kt == nkt - 1):
                          e.matmul(o, ones_b[:, :], r, start=st, stop=sp,
                                   tile_position=(0, 64), skip_group_check=True),
                          inc=tT)
                et_ring.release(pa["eti"], tT, sm)
                return sm

            def emit_norm(pn):
                nonlocal recip_busy
                p.wait("tensor", tV, pn["rc"])
                bt = p.op("tensor", lambda e, o=pn["smslot"][:, :]:
                          e.matmul(o, sel2[0:65, 0:128], recip_sb[0:65, :],
                                   start=True, stop=True, skip_group_check=True),
                          inc=tT)
                recip_busy = bt
                rslot, rbi = rb_ring.acquire("vector")
                p.wait("vector", tT, bt)
                bv_ = p.op("vector", lambda e, o=rslot[:, :], i=pn["smslot"][:, :]:
                           e.tensor_copy(o, i), inc=tV)
                sm_ring.release(pn["smi"], tV, bv_)
                j_, win0_ = pn["j"], pn["win0"]
                cv = p.op("vector",
                          lambda e, o=ctx[:, j_ * S + win0_: j_ * S + win0_ + 512],
                          i=pn["ctslot"][:, :], r=rslot[:, :]:
                          e.tensor_tensor(o, i, r, ALU.mult), inc=tV)
                rb_ring.release(rbi, tV, cv)
                ct_ring.release(pn["cti"], tV, cv)

            pend_av = []
            pend_norm = None
            recip_busy = 0
            for qh in range(2):
                for ws in range(2):
                    win0 = qh * 1024 + ws * 512
                    nkt = qh * 8 + ws * 4 + 4
                    ctslot, cti = ct_ring.acquire("tensor")
                    smslot, smi = sm_ring.acquire("tensor")
                    for kt in range(nkt):
                        sslot, ssi = ps_ring.acquire("tensor")
                        ets, eti = et_ring.acquire("scalar")
                        off = kt * 128 - win0
                        for (r0, c0) in ((0, 0), (64, 512)):
                            last = (off < 0) and (r0 == 64)
                            p.op("tensor",
                                 lambda e, o=sslot[:, c0:c0 + 512],
                                 l=kr_sb[r0:r0 + 64, kt * 128:(kt + 1) * 128],
                                 r=qr_sb[r0:r0 + 64, win0:win0 + 512],
                                 sp=(off < 0):
                                 e.matmul(o, l, r, start=True, stop=sp),
                                 inc=(tT if last else None))
                        if off >= 0:
                            scol = 384 - off
                            for c0 in (0, 512):
                                p.op("tensor",
                                     lambda e, o=sslot[:, c0:c0 + 512],
                                     m=trit[:, scol:scol + 512]:
                                     e.matmul(o, identb[:, :], m, start=False,
                                              stop=True),
                                     inc=(tT if c0 == 512 else None))
                        st_t = p.c(tT)
                        p.wait("scalar", tT, st_t)
                        ex = p.op("scalar",
                                  lambda e, o=ets[:, :], i=sslot[:, :]:
                                  e.activation(out=o, in_=i, func=AF.Exp,
                                               bias=cst[:, 0:1], scale=cst[:, 1:2]),
                                  inc=tA)
                        ps_ring.release(ssi, tA, ex)
                        if kt == 3 and pend_norm is not None:
                            emit_norm(pend_norm)
                            pend_norm = None
                        if len(pend_av) >= 2:
                            emit_av(pend_av.pop(0))
                        pend_av.append(dict(ex=ex, ctslot=ctslot, smslot=smslot,
                                            kt=kt, nkt=nkt, ets=ets, eti=eti))
                    for pa in pend_av:
                        sm = emit_av(pa)
                    pend_av = []
                    p.wait("vector", tT, sm)
                    p.wait("vector", tT, recip_busy)
                    p.op("vector", lambda e, i=smslot[0:1, :]:
                         e.reciprocal(recip_sb[0:1, :], i), inc=tV)
                    rc = p.op("vector", lambda e, i=smslot[64:65, :]:
                             e.reciprocal(recip_sb[64:65, :], i), inc=tV)
                    pend_norm = dict(rc=rc, ctslot=ctslot, cti=cti, smslot=smslot,
                                     smi=smi, j=j, win0=win0)
            emit_norm(pend_norm)
            pend_norm = None
            prev_scores_done = p.c(tT)
            wq_ring.release(wqi, tT, prev_scores_done)
            wk_ring.release(wki, tT, prev_scores_done)
        p.flush_block()
    st_A.close()

    x2 = st_R1.enter_context(nc.sbuf_tensor("x2", [128, NCH * S], F32, side="right"))
    x2bf = st_R2.enter_context(nc.sbuf_tensor("x2bf", [128, NCH * S], BF16, side="right"))
    with ExitStack() as ph:
        wo_sb = ph.enter_context(nc.sbuf_tensor("wo_sb", [128, 8 * 1024], BF16))
        stage = [ph.enter_context(nc.sbuf_tensor(f"stage{i}", [128, 4096], BF16)) for i in range(2)]
        xres = [ph.enter_context(nc.sbuf_tensor("xres0", [128, D], F32))]
        lnb = [ph.enter_context(nc.sbuf_tensor(f"lnb{i}", [128, D], F32)) for i in range(2)]
        lnb2 = [ph.enter_context(nc.sbuf_tensor("lnb20", [128, D], F32))]
        sqs = ph.enter_context(nc.sbuf_tensor("sqs", [128, D], F32))
        stt = ph.enter_context(nc.sbuf_tensor("stt", [128, 8], F32))
        pO = [ph.enter_context(nc.psum_tensor(f"pO{i}", [128, 512], F32)) for i in range(2)]
        pL = [ph.enter_context(nc.psum_tensor(f"pL{i}", [128, D], BF16)) for i in range(2)]
        pB2 = [ph.enter_context(nc.psum_tensor("pB20", [128, D], F32))]

        p.wait("scalar", dC, consts_d)
        p.wait("vector", dC, consts_d)
        p.wait("tensor", dC, consts_d)
        for k in range(8):
            p.dma("gpsimd", wo_sb[:, k * 1024:(k + 1) * 1024],
                  d["wor"][k].rearrange("p k c -> p (k c)"), dWo)
        wo_d = p.c(dWo)
        p.wait("tensor", dWo, wo_d)

        po_ring = Ring(p, pO)
        pl_ring = Ring(p, pL)
        pb_ring = Ring(p, pB2)
        xr_ring = DmaRing(p, xres, "dRx")
        ln_ring = Ring(p, lnb)
        ln2_ring = Ring(p, lnb2)
        stage_ring = Ring(p, stage)

        lnb2 = [ph.enter_context(nc.sbuf_tensor(f"lnb2x{i}", [128, D], F32)) for i in range(3)]
        ln2_ring = Ring(p, lnb2 + ln2_ring.slots)
        pB2b = ph.enter_context(nc.psum_tensor("pB21", [128, D], F32))
        pb_ring = Ring(p, pb_ring.slots + [pB2b])

        def emit_A(g, stg, ev_ticks, out_list):
            for tl in range(4):
                tok0 = g * 512 + tl * 128
                xrs, xri, xtok = xr_ring.load("sync", d["x"][tok0:tok0 + 128, :])
                pls, pli = pl_ring.acquire("tensor")
                for ds in range(8):
                    p.wait("tensor", tA, ev_ticks[ds])
                    p.op("tensor",
                         lambda e, o=pls[:, ds * 128:(ds + 1) * 128],
                         i=stg[:, ds * 512 + tl * 128: ds * 512 + tl * 128 + 128]:
                         e.transpose(o, i, identb[:, :]),
                         inc=(tT if ds == 7 else None))
                tp = p.c(tT)
                lns, lni = ln_ring.acquire("vector")
                p.wait("vector", tT, tp)
                xr_ring.wait_loaded("vector", xtok)
                lv = p.op("vector",
                          lambda e, o=lns[:, :], i=pls[:, :], r=xrs[:, :]:
                          e.scalar_tensor_tensor(o, i, 1.0, r, ALU.mult, ALU.add),
                          inc=tV)
                pl_ring.release(pli, tV, lv)
                xr_ring.release(xri, tV, lv)
                p.wait("scalar", tV, lv)
                a1 = p.op("scalar", lambda e, o=sqs[:, :], i=lns[:, :]:
                          e.activation(out=o, in_=i, func=AF.Square,
                                       accum_out=stt[:, 0:1]), inc=tA)
                v1 = p.op("vector", lambda e, i=lns[:, :]:
                          e.reduce_sum(stt[:, 1:2], i, axis=AX.X, negate=True), inc=tV)
                v2 = p.op("vector", lambda e:
                          e.tensor_scalar_mul(stt[:, 2:3], stt[:, 1:2], 1.0 / D), inc=tV)
                p.wait("vector", tV, v2)
                v3 = p.op("vector", lambda e:
                          e.tensor_tensor(stt[:, 3:4], stt[:, 2:3], stt[:, 2:3], ALU.mult),
                          inc=tV)
                p.wait("vector", tA, a1)
                v4 = p.op("vector", lambda e:
                          e.scalar_tensor_tensor(stt[:, 4:5], stt[:, 0:1], 1.0 / D,
                                                 stt[:, 3:4], ALU.mult, ALU.subtract),
                          inc=tV)
                p.wait("scalar", tV, v4)
                a2 = p.op("scalar", lambda e:
                          e.activation(out=stt[:, 5:6], in_=stt[:, 4:5], func=AF.Sqrt,
                                       bias=cst[:, 2:3]), inc=tA)
                p.wait("vector", tA, a2)
                v5 = p.op("vector", lambda e:
                          e.reciprocal(stt[:, 6:7], stt[:, 5:6]), inc=tV)
                p.wait("vector", tV, v5)
                v6 = p.op("vector", lambda e:
                          e.tensor_scalar_mul(stt[:, 7:8], stt[:, 2:3], stt[:, 6:7]),
                          inc=tV)
                ln2s, ln2i = ln2_ring.acquire("scalar")
                p.wait("scalar", tV, v6)
                a3 = p.op("scalar", lambda e, o=ln2s[:, :], i=lns[:, :]:
                          e.activation(out=o, in_=i, func=AF.Identity,
                                       scale=stt[:, 6:7], bias=stt[:, 7:8]), inc=tA)
                ln_ring.release(lni, tA, a3)
                out_list.append((tok0, ln2s, ln2i, a3))

        def emit_B(items):
            for (tok0, ln2s, ln2i, a3) in items:
                pbs, pbi = pb_ring.acquire("tensor")
                p.wait("tensor", tA, a3)
                for ds in range(8):
                    p.op("tensor",
                         lambda e, o=pbs[:, ds * 128:(ds + 1) * 128],
                         i=ln2s[:, ds * 128:(ds + 1) * 128]:
                         e.transpose(o, i, identf[:, :]),
                         inc=(tT if ds == 7 else None))
                tb = p.c(tT)
                ln2_ring.release(ln2i, tT, tb)
                p.wait("vector", tT, tb)
                x2v = x2.rearrange("p (c s) -> p c s", c=NCH)[:, :, tok0:tok0 + 128]
                vv = p.op("vector", lambda e, o=x2v, i=pbs[:, :]:
                          e.tensor_copy(o, i), inc=tV)
                pb_ring.release(pbi, tV, vv)
                p.wait("scalar", tV, vv)
                x2bv = x2bf.rearrange("p (c s) -> p c s", c=NCH)[:, :, tok0:tok0 + 128]
                p.op("scalar", lambda e, o=x2bv, i=x2v:
                     e.activation(out=o, in_=i, func=AF.Copy), inc=tA)

        pend_B = None
        for g in range(4):
            stg, sgi = stage_ring.acquire("scalar")
            ev_ticks = []
            for ds in range(8):
                slot, si = po_ring.acquire("tensor")
                for k in range(NCH):
                    p.op("tensor",
                         lambda e, o=slot[:, :],
                         l=wo_sb[:, ds * 1024 + k * 128: ds * 1024 + k * 128 + 128],
                         r=ctx[:, k * S + g * 512: k * S + g * 512 + 512],
                         st=(k == 0), sp=(k == NCH - 1):
                         e.matmul(o, l, r, start=st, stop=sp),
                         inc=(tT if k == NCH - 1 else None))
                mm = p.c(tT)
                p.wait("scalar", tT, mm)
                ev = p.op("scalar",
                          lambda e, o=stg[:, ds * 512:(ds + 1) * 512], i=slot[:, :],
                          b=bo_sb[:, ds:ds + 1]:
                          e.activation(out=o, in_=i, func=AF.Identity, bias=b), inc=tA)
                ev_ticks.append(ev)
                po_ring.release(si, tA, ev)
            if pend_B is not None:
                emit_B(pend_B)
                pend_B = None
            items = []
            emit_A(g, stg, ev_ticks, items)
            stage_ring.release(sgi, tT, p.c(tT))
            pend_B = items
        emit_B(pend_B)
        x2bf_done = p.c(tA)
        x2_done = p.c(tV)
        p.flush_block()
    st_B.close()

    with ExitStack() as ph:
        wr1_sb = ph.enter_context(nc.sbuf_tensor("wr1_sb", [128, 2048], BF16))
        wr2_sb = ph.enter_context(nc.sbuf_tensor("wr2_sb", [128, 16], BF16))
        r1bf = ph.enter_context(nc.sbuf_tensor("r1bf", [128, 2 * S], BF16))
        br2b = ph.enter_context(nc.sbuf_tensor("br2b", [128, 8], F32))
        lg = ph.enter_context(nc.sbuf_tensor("lg", [128, 8], F32))
        g8 = ph.enter_context(nc.sbuf_tensor("g8", [128, 8], F32))
        eq = ph.enter_context(nc.sbuf_tensor("eq", [128, 8], F32))
        g2 = ph.enter_context(nc.sbuf_tensor("g2", [128, 8], F32))
        msk = ph.enter_context(nc.sbuf_tensor("msk", [128, 8], F32))
        eh = ph.enter_context(nc.sbuf_tensor("eh", [128, 8], F32))
        wu = ph.enter_context(nc.sbuf_tensor("wu", [128, 8], F32))
        rst = ph.enter_context(nc.sbuf_tensor("rst", [128, 8], F32))
        wtile = ph.enter_context(nc.sbuf_tensor("wtile", [128, 128], F32))
        pR1 = [ph.enter_context(nc.psum_tensor(f"pR1{i}", [128, 512], F32)) for i in range(2)]
        pLg = [ph.enter_context(nc.psum_tensor(f"pLg{i}", [128, 8], F32)) for i in range(2)]
        pWt = ph.enter_context(nc.psum_tensor("pWt", [128, 128], F32))

        p.wait("vector", dC, consts_d)
        p.wait("scalar", dC, consts_d)
        p.wait("tensor", dC, consts_d)
        for j2 in range(2):
            p.dma("gpsimd", wr1_sb[:, j2 * 1024:(j2 + 1) * 1024],
                  d["wr1r"][j2].rearrange("p k c -> p (k c)"), dWr)
        for k2 in range(2):
            p.dma("gpsimd", wr2_sb[:, k2 * 8:(k2 + 1) * 8],
                  d["wr2"][k2 * 128:(k2 + 1) * 128, :], dWr)
        wr_d = p.c(dWr)
        p.wait("tensor", dWr, wr_d)
        bt = p.op("tensor", lambda e: e.matmul(pLg[0][:, :], ones_f[:, :], br2row[0:1, :],
                                               start=True, stop=True), inc=tT)
        p.wait("vector", tT, bt)
        bb = p.op("vector", lambda e: e.tensor_copy(br2b[:, :], pLg[0][:, :]), inc=tV)
        pr_ring = Ring(p, pR1)
        p.wait("tensor", tA, x2bf_done)
        for j2 in range(2):
            for g in range(4):
                slot, si = pr_ring.acquire("tensor")
                for k in range(NCH):
                    p.op("tensor",
                         lambda e, o=slot[:, :],
                         l=wr1_sb[:, j2 * 1024 + k * 128: j2 * 1024 + k * 128 + 128],
                         r=x2bf[:, k * S + g * 512: k * S + g * 512 + 512],
                         st=(k == 0), sp=(k == NCH - 1):
                         e.matmul(o, l, r, start=st, stop=sp),
                         inc=(tT if k == NCH - 1 else None))
                mm = p.c(tT)
                p.wait("scalar", tT, mm)
                ev = p.op("scalar",
                          lambda e, o=r1bf[:, j2 * S + g * 512: j2 * S + g * 512 + 512],
                          i=slot[:, :], b=br1_sb[:, j2:j2 + 1]:
                          e.activation(out=o, in_=i, func=AF.Relu, bias=b), inc=tA)
                pr_ring.release(si, tA, ev)
        r1_done = p.c(tA)
        p.wait("tensor", tA, r1_done)
        p.wait("tensor", tV, bb)

        lgall = ph.enter_context(nc.sbuf_tensor("lgall", [128, 128], F32))
        plg_ring = Ring(p, pLg)
        for tt in range(NT):
            lgs, lgi = plg_ring.acquire("tensor")
            for k2 in range(2):
                p.op("tensor",
                     lambda e, o=lgs[:, :],
                     l=r1bf[:, k2 * S + tt * 128: k2 * S + tt * 128 + 128],
                     r=wr2_sb[:, k2 * 8:(k2 + 1) * 8],
                     st=(k2 == 0), sp=(k2 == 1):
                     e.matmul(o, l, r, start=st, stop=sp),
                     inc=(tT if k2 == 1 else None))
            lt = p.c(tT)
            p.wait("vector", tT, lt)
            lgr = p.op("vector", lambda e, o=lgall[:, tt * 8:(tt + 1) * 8], i=lgs[:, :]:
                       e.tensor_copy(o, i), inc=tV)
            plg_ring.release(lgi, tV, lgr)

        def t16(buf):
            return buf.rearrange("p (t e) -> p t e", t=16)

        st16 = ph.enter_context(nc.sbuf_tensor("st16", [128, 96], F32))

        def s16(i):
            return st16[:, i * 16:(i + 1) * 16]

        def bc16(i):
            return bass.AP(st16, i * 16, [[96, 128], [1, 16], [0, 8]])

        p.wait("vector", tV, bb)
        br2bc = bass.AP(br2b, 0, [[8, 128], [0, 16], [1, 8]])
        p.op("vector", lambda e: e.tensor_tensor(t16(lgall), t16(lgall), br2bc, ALU.add),
             inc=tV)
        p.op("vector", lambda e: e.reduce_max(s16(0), t16(lgall), axis=AX.X), inc=tV)
        m1t = p.c(tV)
        p.op("vector", lambda e: e.tensor_tensor(t16(lgall), t16(lgall), bc16(0),
                                                 ALU.subtract), inc=tV)
        sb_t = p.c(tV)
        p.wait("scalar", tV, sb_t)
        exa = p.op("scalar", lambda e: e.activation(out=lgall[:, :], in_=lgall[:, :],
                                                    func=AF.Exp), inc=tA)
        p.wait("vector", tA, exa)
        p.op("vector", lambda e: e.reduce_sum(s16(1), t16(lgall), axis=AX.X), inc=tV)
        p.op("vector", lambda e: e.reciprocal(s16(2), s16(1)), inc=tV)
        p.op("vector", lambda e: e.tensor_tensor(t16(lgall), t16(lgall), bc16(2),
                                                 ALU.mult), inc=tV)
        p.op("vector", lambda e: e.reduce_max(s16(3), t16(lgall), axis=AX.X), inc=tV)
        p.op("vector", lambda e, o=wtile[:, :]: e.tensor_tensor(t16(o), t16(lgall),
                                                                bc16(3), ALU.is_equal),
             inc=tV)
        p.op("vector", lambda e, o=wtile[:, :]:
             e.scalar_tensor_tensor(t16(o), t16(o), -1e30, t16(lgall), ALU.mult, ALU.add),
             inc=tV)
        p.op("vector", lambda e, o=wtile[:, :]: e.reduce_max(s16(4), t16(o), axis=AX.X),
             inc=tV)
        p.op("vector", lambda e, o=wtile[:, :]: e.tensor_tensor(t16(o), t16(lgall),
                                                                bc16(4), ALU.is_ge),
             inc=tV)
        mk16 = p.c(tV)
        p.wait("scalar", tV, mk16)
        eh16 = p.op("scalar", lambda e: e.activation(out=lgall[:, :], in_=lgall[:, :],
                                                     func=AF.Exp, scale=cst[:, 3:4]),
                    inc=tA)
        p.wait("vector", tA, eh16)
        p.op("vector", lambda e, o=wtile[:, :]: e.tensor_tensor(o, o, lgall[:, :],
                                                                ALU.mult), inc=tV)
        p.op("vector", lambda e, o=wtile[:, :]: e.reduce_sum(s16(5), t16(o), axis=AX.X),
             inc=tV)
        p.op("vector", lambda e: e.reciprocal(s16(0), s16(5)), inc=tV)
        wv_t = p.op("vector", lambda e, o=wtile[:, :]:
                    e.tensor_tensor(t16(o), t16(o), bc16(0), ALU.mult), inc=tV)
        p.wait("tensor", tV, wv_t)
        for tt in range(NT):
            wt_t = p.op("tensor", lambda e, i=wtile[:, tt * 8:(tt + 1) * 8]:
                        e.transpose(pWt[0:8, 0:128], i, identf[:, :]), inc=tT)
            p.wait("vector", tT, wt_t)
            wc = p.op("vector", lambda e, o=wT[0:8, tt * 128:(tt + 1) * 128]:
                      e.tensor_copy(o, pWt[0:8, 0:128]), inc=tV)
            p.wait("tensor", tV, wc)
        p.flush_block()

    with ExitStack() as ph:
        h1bf = ph.enter_context(nc.sbuf_tensor("h1bf", [128, 16 * S], BF16))
        w1s = [ph.enter_context(nc.sbuf_tensor(f"w1s{i}", [128, 4096], BF16)) for i in range(2)]
        w2s = [ph.enter_context(nc.sbuf_tensor(f"w2s{i}", [128, 4096], BF16)) for i in range(2)]
        wb = [ph.enter_context(nc.sbuf_tensor("wb0", [128, S], BF16))] * 2
        tmp5 = [ph.enter_context(nc.sbuf_tensor("tmp50", [128, 512], F32))]
        pE1 = [ph.enter_context(nc.psum_tensor(f"pE1{i}", [128, 512], F32)) for i in range(3)]
        pE2 = [ph.enter_context(nc.psum_tensor(f"pE2{i}", [128, 512], F32)) for i in range(3)]
        pWb = ph.enter_context(nc.psum_tensor("pWb", [128, 512], F32))

        w1_ring = DmaRing(p, w1s, "dW1")
        w2_ring = DmaRing(p, w2s, "dW2")
        p1_ring = Ring(p, pE1)
        p2_ring = Ring(p, pE2)
        tmp_ring = Ring(p, tmp5)
        pwb_free = 0
        shared_evac_done = 0
        x2_add_tick = {}

        p.wait("scalar", dC, consts_d)
        p.wait("vector", dC, consts_d)
        p.wait("tensor", tA, x2bf_done)
        p.wait("vector", tV, x2_done)

        h1_consumed = 0
        for e_i in range(NE):
            if e_i >= 1:
                wb_ready = 0
                for g in range(4):
                    p.wait("tensor", tV, pwb_free)
                    bt = p.op("tensor",
                              lambda e, l=sel8[0:8, (e_i - 1) * 128: e_i * 128],
                              r=wT[0:8, g * 512:(g + 1) * 512]:
                              e.matmul(pWb[:, :], l, r, start=True, stop=True),
                              inc=tT)
                    p.wait("vector", tT, bt)
                    bv2 = p.op("vector", lambda e, o=wb[e_i % 2][:, g * 512:(g + 1) * 512]:
                               e.tensor_copy(o, pWb[:, :]), inc=tV)
                    pwb_free = bv2
                wb_ready = p.c(tV)
            for Hh in range(2):
                for ocg in range(4):
                    if Hh == 0 and ocg == 0:
                        p.wait("sync", dPC[e_i], 64)
                    slot, si, w1tok = w1_ring.load(
                        "sync", w1bf[e_i, Hh * 4 + ocg])
                    w1_ring.wait_loaded("tensor", w1tok)
                    for oi in range(4):
                        oc = ocg * 4 + oi
                        goc = Hh * 16 + oc
                        for g in range(4):
                            ps1, pi1 = p1_ring.acquire("tensor")
                            for k in range(NCH):
                                p.op("tensor",
                                     lambda e, o=ps1[:, :],
                                     l=slot[:, oi * 1024 + k * 128: oi * 1024 + k * 128 + 128],
                                     r=x2bf[:, k * S + g * 512: k * S + g * 512 + 512],
                                     st=(k == 0), sp=(k == NCH - 1):
                                     e.matmul(o, l, r, start=st, stop=sp),
                                     inc=(tT if k == NCH - 1 else None))
                            mm = p.c(tT)
                            if g == 3 and oi == 3:
                                w1_ring.release(si, tT, mm)
                            p.wait("scalar", tT, mm)
                            if oc == 0 and g == 0 and (Hh == 1 or e_i > 0):
                                p.wait("scalar", tT, h1_consumed)
                            ev = p.op("scalar",
                                      lambda e, o=h1bf[:, oc * S + g * 512: oc * S + g * 512 + 512],
                                      i=ps1[:, :], b=b1_sb[:, e_i * 32 + goc: e_i * 32 + goc + 1]:
                                      e.activation(out=o, in_=i, func=AF.Silu, bias=b), inc=tA)
                            p1_ring.release(pi1, tA, ev)
                h1_done = p.c(tA)
                for ddg in range(4):
                    slot2, si2, w2tok = w2_ring.load("sync", w2bf[e_i, Hh, ddg])
                    w2_ring.wait_loaded("tensor", w2tok)
                    if ddg == 0:
                        p.wait("tensor", tA, h1_done)
                    for di in range(2):
                      dd = ddg * 2 + di
                      for g in range(4):
                        ps2, pi2 = p2_ring.acquire("tensor")
                        for kc in range(16):
                            p.op("tensor",
                                 lambda e, o=ps2[:, :],
                                 l=slot2[:, di * 2048 + kc * 128: di * 2048 + kc * 128 + 128],
                                 r=h1bf[:, kc * S + g * 512: kc * S + g * 512 + 512],
                                 st=(kc == 0), sp=(kc == 15):
                                 e.matmul(o, l, r, start=st, stop=sp),
                                 inc=(tT if kc == 15 else None))
                        mm2 = p.c(tT)
                        if g == 3 and di == 1:
                            w2_ring.release(si2, tT, mm2)
                        x2slice = x2[:, dd * S + g * 512: dd * S + g * 512 + 512]
                        b2ap = (b2_sb[:, e_i * 8 + dd: e_i * 8 + dd + 1]
                                if Hh == 1 else 0.0)
                        p.wait("vector", tT, mm2)
                        if e_i == 0:
                            ev2 = p.op("vector",
                                       lambda e, o=x2slice, i=ps2[:, :], b=b2ap:
                                       e.scalar_tensor_tensor(o, i, b, o, ALU.add, ALU.add),
                                       inc=tV)
                            p2_ring.release(pi2, tV, ev2)
                            shared_evac_done = ev2
                        else:
                            if dd == 0 and g == 0 and Hh == 0:
                                p.wait("vector", tV, wb_ready)
                            tms, tmi = tmp_ring.acquire("vector")
                            ev2 = p.op("vector",
                                       lambda e, o=tms[:, :], i=ps2[:, :], b=b2ap,
                                       w=wb[e_i % 2][:, g * 512:(g + 1) * 512]:
                                       e.scalar_tensor_tensor(o, i, b, w, ALU.add, ALU.mult),
                                       inc=tV)
                            p2_ring.release(pi2, tV, ev2)
                            p.wait("gpsimd", tV, ev2)
                            if e_i == 1 and dd == 0 and g == 0 and Hh == 0:
                                p.wait("gpsimd", tV, shared_evac_done)
                            p.wait("gpsimd", tG, x2_add_tick.get((dd, g), 0))
                            ga = p.op("gpsimd",
                                      lambda e, o=x2slice, t=tms[:, :]:
                                      e.tensor_tensor(o, o, t, ALU.add), inc=tG)
                            x2_add_tick[(dd, g)] = ga
                            tmp_ring.release(tmi, tG, ga)
                h1_consumed = p.c(tT)
        ffn_done_g = p.c(tG)
        ffn_done_v = p.c(tV)
        p.flush_block()
    st_R2.close()

    with ExitStack() as ph:
        lnb = [ph.enter_context(nc.sbuf_tensor(f"lnc{i}", [128, D], F32)) for i in range(2)]
        ybuf = [ph.enter_context(nc.sbuf_tensor(f"ybuf{i}", [128, D], F32)) for i in range(2)]
        sqs = ph.enter_context(nc.sbuf_tensor("sqs2", [128, D], F32))
        stt = ph.enter_context(nc.sbuf_tensor("stt2", [128, 8], F32))
        pL = [ph.enter_context(nc.psum_tensor(f"pL2{i}", [128, D], F32)) for i in range(2)]
        dY = [p.sem("dY0"), p.sem("dY1")]

        pl_ring = Ring(p, pL)
        ln_ring = Ring(p, lnb)
        p.wait("scalar", dC, consts_d)
        p.wait("vector", dC, consts_d)
        p.wait("tensor", dC, consts_d)
        for tt in range(NT):
            pls, pli = pl_ring.acquire("tensor")
            for ds in range(8):
                p.op("tensor",
                     lambda e, o=pls[:, ds * 128:(ds + 1) * 128],
                     i=x2[:, ds * S + tt * 128: ds * S + tt * 128 + 128]:
                     e.transpose(o, i, identf[:, :]),
                     inc=(tT if ds == 7 else None))
            tp = p.c(tT)
            lns, lni = ln_ring.acquire("vector")
            p.wait("vector", tT, tp)
            lv = p.op("vector", lambda e, o=lns[:, :], i=pls[:, :]:
                      e.tensor_copy(o, i), inc=tV)
            pl_ring.release(pli, tV, lv)
            p.wait("scalar", tV, lv)
            a1 = p.op("scalar", lambda e, o=sqs[:, :], i=lns[:, :]:
                      e.activation(out=o, in_=i, func=AF.Square,
                                   accum_out=stt[:, 0:1]), inc=tA)
            v1 = p.op("vector", lambda e, i=lns[:, :]:
                      e.reduce_sum(stt[:, 1:2], i, axis=AX.X, negate=True), inc=tV)
            v2 = p.op("vector", lambda e:
                      e.tensor_scalar_mul(stt[:, 2:3], stt[:, 1:2], 1.0 / D), inc=tV)
            p.wait("vector", tV, v2)
            v3 = p.op("vector", lambda e:
                      e.tensor_tensor(stt[:, 3:4], stt[:, 2:3], stt[:, 2:3], ALU.mult),
                      inc=tV)
            p.wait("vector", tA, a1)
            v4 = p.op("vector", lambda e:
                      e.scalar_tensor_tensor(stt[:, 4:5], stt[:, 0:1], 1.0 / D,
                                             stt[:, 3:4], ALU.mult, ALU.subtract), inc=tV)
            p.wait("scalar", tV, v4)
            a2 = p.op("scalar", lambda e:
                      e.activation(out=stt[:, 5:6], in_=stt[:, 4:5], func=AF.Sqrt,
                                   bias=cst[:, 2:3]), inc=tA)
            p.wait("vector", tA, a2)
            v5 = p.op("vector", lambda e: e.reciprocal(stt[:, 6:7], stt[:, 5:6]), inc=tV)
            p.wait("vector", tV, v5)
            v6 = p.op("vector", lambda e:
                      e.tensor_scalar_mul(stt[:, 7:8], stt[:, 2:3], stt[:, 6:7]), inc=tV)
            p.wait("scalar", tV, v6)
            if tt >= 2:
                p.wait("scalar", dY[tt % 2], (tt // 2) * 16)
            ybs = ybuf[tt % 2]
            a3 = p.op("scalar", lambda e, o=ybs[:, :], i=lns[:, :]:
                      e.activation(out=o, in_=i, func=AF.Identity,
                                   scale=stt[:, 6:7], bias=stt[:, 7:8]), inc=tA)
            ln_ring.release(lni, tA, a3)
            p.wait("sync", tA, a3)
            p.dma("sync", y[tt * 128:(tt + 1) * 128, :], ybs[:, :], dY[tt % 2])
        p.wait("sync", dY[0], p.c(dY[0]))
        p.wait("sync", dY[1], p.c(dY[1]))
        p.flush_block()

    st_R1.close()
    es.close()
    return nc


def _host_pack(inputs):
    f32 = np.float32
    bf = ml_dtypes.bfloat16
    ca = np.ascontiguousarray
    ins = {}
    ins["wv"] = ca(inputs["L0_attn_Wv"], dtype=f32)
    for nm, key in (("wqr", "L0_attn_Wq"), ("wkr", "L0_attn_Wk"), ("wor", "L0_attn_Wo")):
        w = np.asarray(inputs[key], dtype=f32)
        ins[nm] = ca(w.reshape(8, 128, 8, 128).transpose(2, 1, 0, 3))
    ins["bqr"] = ca(np.asarray(inputs["L0_attn_bq"], dtype=f32).reshape(8, 128).T)
    ins["bkr"] = ca(np.asarray(inputs["L0_attn_bk"], dtype=f32).reshape(8, 128).T)
    ins["bor"] = ca(np.asarray(inputs["L0_attn_bo"], dtype=f32).reshape(8, 128).T)
    ins["bv"] = ca(np.asarray(inputs["L0_attn_bv"], dtype=f32).reshape(1, D))
    ins["wr1r"] = ca(np.asarray(inputs["L0_ffn_Wr1"], dtype=f32)
                     .reshape(8, 128, 2, 128).transpose(2, 1, 0, 3))
    ins["wr2"] = ca(inputs["L0_ffn_Wr2"], dtype=f32)
    ins["br1r"] = ca(np.asarray(inputs["L0_ffn_br1"], dtype=f32).reshape(2, 128).T)
    ins["br2"] = ca(np.asarray(inputs["L0_ffn_br2"], dtype=f32).reshape(1, 8))
    w1 = np.concatenate([np.asarray(inputs["L0_ffn_Ws1"], dtype=f32)[None],
                         np.asarray(inputs["L0_ffn_We1"], dtype=f32)], axis=0)
    ins["w1r"] = ca(w1.reshape(NE, 8, 128, 8, 4, 128).transpose(0, 3, 2, 4, 1, 5)
                    .reshape(NE, 8, 128, 4096))
    w2 = np.concatenate([np.asarray(inputs["L0_ffn_Ws2"], dtype=f32)[None],
                         np.asarray(inputs["L0_ffn_We2"], dtype=f32)], axis=0)
    ins["w2r"] = ca(w2.reshape(NE, 2, 16, 128, 4, 2, 128).transpose(0, 1, 4, 3, 5, 2, 6)
                    .reshape(NE, 2, 4, 128, 4096))
    b1 = np.concatenate([np.asarray(inputs["L0_ffn_bs1"], dtype=f32)[None],
                         np.asarray(inputs["L0_ffn_be1"], dtype=f32)], axis=0)
    ins["b1r"] = ca(b1.reshape(NE, 32, 128).transpose(2, 0, 1).reshape(128, NE * 32))
    b2 = np.concatenate([np.asarray(inputs["L0_ffn_bs2"], dtype=f32)[None],
                         np.asarray(inputs["L0_ffn_be2"], dtype=f32)], axis=0)
    ins["b2r"] = ca(b2.reshape(NE, 8, 128).transpose(2, 0, 1).reshape(128, NE * 8))

    hd = 64
    pos = np.arange(S, dtype=f32)
    inv_freq = np.exp(np.arange(0, hd, 2, dtype=f32) * f32(-np.log(10000.0) / hd))
    ang = pos[:, None] * inv_freq[None, :]
    sin = np.sin(ang.astype(f32)).astype(f32)
    cos = np.cos(ang.astype(f32)).astype(f32)
    pmod = np.arange(128) % 32
    ins["ropec"] = ca(cos[:, pmod].T)
    sgn = np.where((np.arange(128) % 64) < 32, -1.0, 1.0).astype(f32)
    ins["ropes"] = ca(sin[:, pmod].T * sgn[:, None])
    P = np.zeros((128, 128), dtype=f32)
    for m in range(128):
        src = m + 32 if (m % 64) < 32 else m - 32
        P[src, m] = 1.0
    ins["prm"] = ca(P.astype(bf))
    ins["identb"] = ca(np.eye(128, dtype=bf))
    ins["identf"] = ca(np.eye(128, dtype=f32))
    k_idx = np.arange(128)[:, None]
    i_idx = np.arange(896)[None, :]
    ins["trit"] = ca(np.where(i_idx < k_idx + 384, -10000.0, 0.0).astype(bf))
    sel = np.zeros((8, 8, 128), dtype=f32)
    for e in range(8):
        sel[e, e, :] = 1.0
    ins["sel8"] = ca(sel.transpose(1, 0, 2).reshape(8, 1024).astype(bf))
    s2 = np.zeros((65, 128), dtype=f32)
    s2[0, 0:64] = 1.0
    s2[64, 64:128] = 1.0
    ins["sel2"] = ca(s2)
    return ins


def kernel(**inputs):
    ins = _host_pack(inputs)
    x = np.asarray(inputs["L0_attn_x"], dtype=np.float32)
    nc = build_nc()
    in_maps = []
    for b in range(8):
        m = dict(ins)
        m["x"] = np.ascontiguousarray(x[b])
        in_maps.append(m)
    res = run_bass_kernel_spmd(nc, in_maps, core_ids=list(range(8)))
    return np.stack([r["y"] for r in res.results], axis=0)


# version 13
# speedup vs baseline: 1.1276x; 1.0072x over previous
import numpy as np
import ml_dtypes
from contextlib import ExitStack

import concourse.bass as bass
import concourse.mybir as mybir
from concourse.bass_utils import run_bass_kernel_spmd

F32 = mybir.dt.float32
BF16 = mybir.dt.bfloat16
AF = mybir.ActivationFunctionType
ALU = mybir.AluOpType
AX = mybir.AxisListType

S = 2048
D = 1024
NCH = 8
NE = 9
NT = 16
EXP_SHIFT = -15.0
EPS = 1e-5

ENGINES = ("sync", "gpsimd", "scalar", "vector", "tensor")


class Prog:

    SELF_FENCE = {"vector": "tV", "scalar": "tA", "gpsimd": "tG"}

    def __init__(self, nc, es):
        self.nc = nc
        self.es = es
        self.q = {e: [] for e in ENGINES}
        self.sems = {}
        self.cnt = {}
        self._lastwait = {}

    def sem(self, name):
        self.sems[name] = self.es.enter_context(self.nc.semaphore(name))
        self.cnt[name] = 0
        return name

    def wait(self, eng, sem, val):
        if val <= 0:
            return
        if sem == self.SELF_FENCE.get(eng):
            if self._lastwait.get(eng, 0) >= val:
                return
            self._lastwait[eng] = val
        h = self.sems[sem]
        self.q[eng].append(lambda e, h=h, v=val: e.wait_ge(h, v))

    def op(self, eng, build, inc=None, n=1):
        own = self.SELF_FENCE.get(eng)
        if own is not None:
            self.wait(eng, own, self.cnt[own])
            if inc is None:
                inc, n = own, 1
        if inc is None:
            self.q[eng].append(build)
            return None
        self.cnt[inc] += n
        h = self.sems[inc]
        self.q[eng].append(lambda e, b=build, h=h, k=n: b(e).then_inc(h, k))
        return self.cnt[inc]

    def dma(self, eng, out, in_, sem):
        self.cnt[sem] += 16
        h = self.sems[sem]
        self.q[eng].append(lambda e, o=out, i=in_, h=h: e.dma_start(out=o, in_=i).then_inc(h, 16))
        return self.cnt[sem]

    def c(self, sem):
        return self.cnt[sem]

    def flush_block(self):
        q = self.q
        self.q = {e: [] for e in ENGINES}
        with self.nc.Block() as blk:
            blk.sync(lambda e, fns=q["sync"]: [f(e) for f in fns])
            blk.gpsimd(lambda e, fns=q["gpsimd"]: [f(e) for f in fns])
            blk.scalar(lambda e, fns=q["scalar"]: [f(e) for f in fns])
            blk.vector(lambda e, fns=q["vector"]: [f(e) for f in fns])
            blk.tensor(lambda e, fns=q["tensor"]: [f(e) for f in fns])


class Ring:

    def __init__(self, prog, slots):
        self.p = prog
        self.slots = list(slots)
        self.n = len(self.slots)
        self.free_at = [[] for _ in range(self.n)]
        self.i = 0

    def acquire(self, eng):
        s = self.i % self.n
        self.i += 1
        for (sem, tick) in self.free_at[s]:
            self.p.wait(eng, sem, tick)
        self.free_at[s] = []
        return self.slots[s], s

    def release(self, s, sem, tick):
        self.free_at[s].append((sem, tick))


class DmaRing:

    def __init__(self, prog, slots, name):
        self.p = prog
        self.slots = list(slots)
        self.n = len(self.slots)
        self.sems = [prog.sem(f"{name}{i}") for i in range(self.n)]
        self.free_at = [[] for _ in range(self.n)]
        self.i = 0

    def load(self, eng, src):
        s = self.i % self.n
        self.i += 1
        for (sem, tick) in self.free_at[s]:
            self.p.wait(eng, sem, tick)
        self.free_at[s] = []
        slot = self.slots[s]
        cnt = self.p.dma(eng, slot[:, :], src, self.sems[s])
        return slot, s, (self.sems[s], cnt)

    def wait_loaded(self, eng, token):
        self.p.wait(eng, token[0], token[1])

    def release(self, s, sem, tick):
        self.free_at[s].append((sem, tick))


def build_nc():
    nc = bass.Bass("TRN2", target_bir_lowering=False)

    d = {}
    def din(name, shape, dt=F32):
        d[name] = nc.dram_tensor(name, shape, dt, kind="ExternalInput").ap()
    din("x", [S, D])
    din("wqr", [8, 128, 8, 128]); din("wkr", [8, 128, 8, 128])
    din("wor", [8, 128, 8, 128]); din("wv", [D, D])
    din("bqr", [128, 8]); din("bkr", [128, 8]); din("bor", [128, 8])
    din("bv", [1, D])
    din("wr1r", [2, 128, 8, 128]); din("wr2", [256, 8])
    din("br1r", [128, 2]); din("br2", [1, 8])
    din("w1r", [NE, 8, 128, 4096])
    din("w2r", [NE, 2, 4, 128, 4096])
    din("b1r", [128, NE * 32]); din("b2r", [128, NE * 8])
    din("ropec", [128, S]); din("ropes", [128, S])
    din("prm", [128, 128], BF16)
    din("identb", [128, 128], BF16)
    din("identf", [128, 128])
    din("trit", [128, 896], BF16)
    din("sel8", [8, 1024], BF16)
    din("sel2", [65, 128])
    y = nc.dram_tensor("y", [S, D], F32, kind="ExternalOutput").ap()
    w1bf = nc.dram_tensor("w1bf", [NE, 8, 128, 4096], BF16).ap()
    w2bf = nc.dram_tensor("w2bf", [NE, 2, 4, 128, 4096], BF16).ap()

    es = ExitStack()
    E = es.enter_context
    p = Prog(nc, es)
    dC = p.sem("dC")
    dV = p.sem("dV")
    dWo = p.sem("dWo")
    dWr = p.sem("dWr")
    dR = p.sem("dR")
    dPC = [p.sem(f"dPC{e}") for e in range(NE)]
    tT = p.sem("tT"); tV = p.sem("tV"); tA = p.sem("tA"); tG = p.sem("tG")

    identf = E(nc.sbuf_tensor("identf_s", [128, 128], F32))
    ones_b = E(nc.sbuf_tensor("ones_b", [128, 1], BF16))
    ones_f = E(nc.sbuf_tensor("ones_f", [1, 128], F32))
    bq_sb = E(nc.sbuf_tensor("bq_sb", [128, 8], F32))
    bk_sb = E(nc.sbuf_tensor("bk_sb", [128, 8], F32))
    bo_sb = E(nc.sbuf_tensor("bo_sb", [128, 8], F32))
    br1_sb = E(nc.sbuf_tensor("br1_sb", [128, 2], F32))
    br2row = E(nc.sbuf_tensor("br2row", [1, 8], F32))
    b1_sb = E(nc.sbuf_tensor("b1_sb", [128, NE * 32], F32))
    b2_sb = E(nc.sbuf_tensor("b2_sb", [128, NE * 8], F32))
    wT = E(nc.sbuf_tensor("wT", [8, S], BF16))
    cst = E(nc.sbuf_tensor("cst", [128, 4], F32))
    sel8 = E(nc.sbuf_tensor("sel8_s", [8, 1024], BF16))
    sel2 = E(nc.sbuf_tensor("sel2_s", [65, 128], F32))

    identb = E(nc.sbuf_tensor("identb_s", [128, 128], BF16))
    st_B = ExitStack()
    ctx = st_B.enter_context(nc.sbuf_tensor("ctx", [128, NCH * S], BF16))
    st_A = ExitStack()
    prm = st_A.enter_context(nc.sbuf_tensor("prm_s", [128, 128], BF16))
    trit = st_A.enter_context(nc.sbuf_tensor("trit_s", [128, 896], BF16))
    bvrow = st_A.enter_context(nc.sbuf_tensor("bvrow", [1, D], F32))
    xbf = st_A.enter_context(nc.sbuf_tensor("xbf", [128, NCH * S], BF16))
    vbuf = st_A.enter_context(nc.sbuf_tensor("vbuf", [128, NT * D], BF16))
    ropec = st_A.enter_context(nc.sbuf_tensor("ropec_s", [128, S], F32))
    ropes = st_A.enter_context(nc.sbuf_tensor("ropes_s", [128, S], F32))
    st_R1 = ExitStack()
    st_R2 = ExitStack()

    def precast_expert(e_i):
        for hh2 in range(2):
            p.dma("gpsimd", w1bf[e_i][hh2 * 4:(hh2 + 1) * 4], d["w1r"][e_i][hh2 * 4:(hh2 + 1) * 4], dPC[e_i])
            p.dma("gpsimd", w2bf[e_i][hh2], d["w2r"][e_i][hh2], dPC[e_i])

    for name, dst in (("identf", identf), ("ropec", ropec), ("ropes", ropes),
                      ("prm", prm), ("identb", identb), ("trit", trit),
                      ("bqr", bq_sb), ("bkr", bk_sb), ("bor", bo_sb),
                      ("br1r", br1_sb), ("br2", br2row), ("bv", bvrow),
                      ("sel8", sel8), ("sel2", sel2)):
        p.dma("sync", dst[:, :], d[name], dC)
    p.dma("sync", b1_sb[:, :], d["b1r"], dC)
    p.dma("sync", b2_sb[:, :], d["b2r"], dC)
    idf_d = p.c(dC)
    consts_d = p.c(dC)
    p.op("gpsimd", lambda e: e.memset(ones_b[:, :], 1.0), inc=tG)
    g_ones = p.c(tG)
    p.op("vector", lambda e: e.memset(ones_f[:, :], 1.0), inc=tV)
    p.op("vector", lambda e: e.memset(cst[:, 0:1], EXP_SHIFT), inc=tV)
    p.op("vector", lambda e: e.memset(cst[:, 1:2], 0.125), inc=tV)
    p.op("vector", lambda e: e.memset(cst[:, 2:3], EPS), inc=tV)
    p.op("vector", lambda e: e.memset(cst[:, 3:4], 0.5), inc=tV)
    v_ones = p.c(tV)

    with ExitStack() as ph:
        xstage = [ph.enter_context(nc.sbuf_tensor(f"xstage{i}", [128, D], F32)) for i in range(2)]
        wv_sb = ph.enter_context(nc.sbuf_tensor("wv_sb", [128, 8 * D], BF16))
        bvb = ph.enter_context(nc.sbuf_tensor("bvb", [128, D], F32))
        pX = [ph.enter_context(nc.psum_tensor(f"pX{i}", [128, D], F32)) for i in range(2)]
        pV = [ph.enter_context(nc.psum_tensor(f"pV{i}", [128, 512], F32)) for i in range(2)]
        pBC = ph.enter_context(nc.psum_tensor("pBC", [128, 512], F32))

        xring = DmaRing(p, xstage, "dX")
        px_ring = Ring(p, pX)
        a_x0 = p.c(tA)
        xbf_ev = []
        p.wait("tensor", dC, idf_d)
        for tt in range(NT):
            slot, si, tok = xring.load("sync", d["x"][tt * 128:(tt + 1) * 128, :])
            xring.wait_loaded("tensor", tok)
            pxs, pxi = px_ring.acquire("tensor")
            for k in range(NCH):
                p.op("tensor",
                     lambda e, o=pxs[:, k * 128:(k + 1) * 128],
                     i=slot[:, k * 128:(k + 1) * 128]:
                     e.transpose(o, i, identf[:, :]),
                     inc=(tT if k == NCH - 1 else None))
            tp = p.c(tT)
            xring.release(si, tT, tp)
            p.wait("scalar", tT, tp)
            out_ap = xbf.rearrange("p (c s) -> p c s", c=NCH)[:, :, tt * 128:(tt + 1) * 128]
            ev = p.op("scalar", lambda e, o=out_ap, i=pxs[:, :]:
                      e.activation(out=o, in_=i, func=AF.Copy), inc=tA)
            xbf_ev.append(ev)
            px_ring.release(pxi, tA, ev)

        p.wait("tensor", dC, consts_d)
        p.wait("tensor", tV, v_ones)
        last_bc = 0
        for ng in range(2):
            if ng == 1:
                p.wait("tensor", tV, last_bc)
            bt = p.op("tensor", lambda e, o=pBC[:, :], i=bvrow[0:1, ng * 512:(ng + 1) * 512]:
                      e.matmul(o, ones_f[:, :], i, start=True, stop=True), inc=tT)
            p.wait("vector", tT, bt)
            last_bc = p.op("vector", lambda e, o=bvb[:, ng * 512:(ng + 1) * 512], i=pBC[:, :]:
                           e.tensor_copy(o, i), inc=tV)

        for k in range(NCH):
            p.dma("gpsimd", wv_sb[:, k * D:(k + 1) * D],
                  d["wv"][k * 128:(k + 1) * 128, :], dV)
        wv_d = p.c(dV)
        precast_expert(0)
        precast_expert(1)
        p.wait("tensor", dV, wv_d)
        pv_ring = Ring(p, pV)
        gi = 0
        for tt in range(NT):
            p.wait("tensor", tA, xbf_ev[tt])
            for ng in range(2):
                slot, si = pv_ring.acquire("tensor")
                for k in range(NCH):
                    p.op("tensor",
                         lambda e, o=slot[:, :],
                         l=xbf[:, k * S + tt * 128: k * S + tt * 128 + 128],
                         r=wv_sb[:, k * D + ng * 512: k * D + ng * 512 + 512],
                         st=(k == 0), sp=(k == NCH - 1):
                         e.matmul(o, l, r, start=st, stop=sp),
                         inc=(tT if k == NCH - 1 else None))
                mm = p.c(tT)
                p.wait("vector", tT, mm)
                if gi == 0:
                    p.wait("vector", tV, last_bc)
                ev = p.op("vector",
                          lambda e, o=vbuf[:, tt * D + ng * 512: tt * D + ng * 512 + 512],
                          i=slot[:, :], b=bvb[:, ng * 512:(ng + 1) * 512]:
                          e.scalar_tensor_tensor(o, i, 1.0, b, ALU.mult, ALU.add),
                          inc=tV)
                pv_ring.release(si, tV, ev)
                gi += 1
        p.flush_block()

    with ExitStack() as ph:
        wq_sb = [ph.enter_context(nc.sbuf_tensor(f"wq{i}", [128, 1024], BF16)) for i in range(2)]
        wk_sb = [ph.enter_context(nc.sbuf_tensor(f"wk{i}", [128, 1024], BF16)) for i in range(2)]
        q_sb = [ph.enter_context(nc.sbuf_tensor(f"q_sb{i}", [128, S], BF16)) for i in range(2)]
        k_sb = [ph.enter_context(nc.sbuf_tensor(f"k_sb{i}", [128, S], BF16)) for i in range(2)]
        qr_sb = [ph.enter_context(nc.sbuf_tensor(f"qr_sb{i}", [128, S], BF16)) for i in range(2)]
        kr_sb = [ph.enter_context(nc.sbuf_tensor(f"kr_sb{i}", [128, S], BF16)) for i in range(2)]
        t1f = [ph.enter_context(nc.sbuf_tensor(f"t1f{i}", [128, 512], F32)) for i in range(2)]
        et_sb = [ph.enter_context(nc.sbuf_tensor(f"et{i}", [128, 1024], BF16)) for i in range(3)]
        recip_sb = ph.enter_context(nc.sbuf_tensor("recip_sb", [65, 512], F32))
        recipb = [ph.enter_context(nc.sbuf_tensor(f"recipb{i}", [128, 512], F32)) for i in range(2)]
        pS = [ph.enter_context(nc.psum_tensor(f"pS{i}", [128, 1024], F32)) for i in range(2)]
        pCt = [ph.enter_context(nc.psum_tensor(f"pCt{i}", [128, 512], F32)) for i in range(2)]
        pSm = [ph.enter_context(nc.psum_tensor(f"pSm{i}", [128, 512], F32)) for i in range(2)]

        for eng in ("tensor", "scalar", "vector", "gpsimd"):
            p.wait(eng, dC, consts_d)
        rec_init = p.op("vector", lambda e: e.memset(recip_sb[:, :], 1.0), inc=tV)
        p.wait("tensor", tV, rec_init)

        wq_ring = DmaRing(p, wq_sb, "dQ")
        wk_ring = DmaRing(p, wk_sb, "dK")
        ps_ring = Ring(p, pS)
        ct_ring = Ring(p, pCt)
        sm_ring = Ring(p, pSm)
        t1_ring = Ring(p, t1f)
        et_ring = Ring(p, et_sb)
        rb_ring = Ring(p, recipb)
        rope_done = {}
        scores_done = {}
        state = {"recip_busy": 0}

        def emit_projrope(j):
            par = j % 2
            wqs, wqi, qtok = wq_ring.load("gpsimd", d["wqr"][j].rearrange("p k c -> p (k c)"))
            wks, wki, ktok = wk_ring.load("gpsimd", d["wkr"][j].rearrange("p k c -> p (k c)"))
            if j + 2 < NE:
                precast_expert(j + 2)
            wq_ring.wait_loaded("tensor", qtok)
            wk_ring.wait_loaded("tensor", ktok)
            evac_ticks = {}
            for (nm, wsb, wri, src, dst, bias_sb) in (
                    ("q", wqs, (wq_ring, wqi), q_sb[par], qr_sb[par], bq_sb),
                    ("k", wks, (wk_ring, wki), k_sb[par], kr_sb[par], bk_sb)):
                for g in range(4):
                    slot, si = ps_ring.acquire("tensor")
                    slot = slot[:, 0:512]
                    for k in range(NCH):
                        p.op("tensor",
                             lambda e, o=slot[:, :],
                             l=wsb[:, k * 128:(k + 1) * 128],
                             r=xbf[:, k * S + g * 512: k * S + g * 512 + 512],
                             st=(k == 0), sp=(k == NCH - 1):
                             e.matmul(o, l, r, start=st, stop=sp),
                             inc=(tT if k == NCH - 1 else None))
                    mm = p.c(tT)
                    if g == 3:
                        wri[0].release(wri[1], tT, mm)
                    p.wait("scalar", tT, mm)
                    if g == 0 and j >= 2:
                        p.wait("scalar", tG, rope_done[j - 2])
                    ev = p.op("scalar",
                              lambda e, o=src[:, g * 512:(g + 1) * 512], i=slot[:, :],
                              b=bias_sb[:, j:j + 1]:
                              e.activation(out=o, in_=i, func=AF.Identity, bias=b),
                              inc=tA)
                    evac_ticks[(nm, g)] = ev
                    ps_ring.release(si, tA, ev)
                for g in range(4):
                    slot, si = ps_ring.acquire("tensor")
                    slot = slot[:, 0:512]
                    p.wait("tensor", tA, evac_ticks[(nm, g)])
                    rt = p.op("tensor",
                              lambda e, o=slot[:, :], r=src[:, g * 512:(g + 1) * 512]:
                              e.matmul(o, prm[:, :], r, start=True, stop=True),
                              inc=tT)
                    t1s, t1i = t1_ring.acquire("vector")
                    p.wait("vector", tT, rt)
                    vt = p.op("vector",
                              lambda e, o=t1s[:, :], i=slot[:, :],
                              s=ropes[:, g * 512:(g + 1) * 512]:
                              e.tensor_tensor(o, i, s, ALU.mult), inc=tV)
                    ps_ring.release(si, tV, vt)
                    if g == 0 and j >= 2:
                        p.wait("gpsimd", tT, scores_done[j - 2])
                    p.wait("gpsimd", tA, evac_ticks[(nm, g)])
                    g1 = p.op("gpsimd",
                              lambda e, o=dst[:, g * 512:(g + 1) * 512],
                              i=src[:, g * 512:(g + 1) * 512],
                              c=ropec[:, g * 512:(g + 1) * 512]:
                              e.tensor_tensor(o, i, c, ALU.mult), inc=tG)
                    p.wait("gpsimd", tV, vt)
                    p.wait("gpsimd", tG, g1)
                    gt = p.op("gpsimd",
                              lambda e, o=dst[:, g * 512:(g + 1) * 512], t=t1s[:, :]:
                              e.tensor_tensor(o, o, t, ALU.add), inc=tG)
                    t1_ring.release(t1i, tG, gt)
            rope_done[j] = p.c(tG)

        def emit_attention(j):
            par = j % 2
            hA, hB = 2 * j, 2 * j + 1
            krb, qrb = kr_sb[par], qr_sb[par]
            p.wait("tensor", tG, rope_done[j])

            def emit_av(pa):
                p.wait("tensor", tA, pa["ex"])
                kt, nkt = pa["kt"], pa["nkt"]
                first = (kt == 0)
                p.op("tensor",
                     lambda e, o=pa["ctslot"][0:64, :],
                     l=vbuf[:, kt * D + hA * 64: kt * D + hA * 64 + 64],
                     r=pa["ets"][:, 0:512], st=first, sp=False:
                     e.matmul(o, l, r, start=st, stop=sp, tile_position=(0, 0),
                              skip_group_check=True))
                p.op("tensor",
                     lambda e, o=pa["ctslot"][64:128, :],
                     l=vbuf[:, kt * D + hB * 64: kt * D + hB * 64 + 64],
                     r=pa["ets"][:, 512:1024], st=first, sp=(kt == nkt - 1):
                     e.matmul(o, l, r, start=st, stop=sp, tile_position=(0, 64),
                              skip_group_check=True))
                p.op("tensor",
                     lambda e, o=pa["smslot"][0:1, :], r=pa["ets"][:, 0:512],
                     st=first: e.matmul(o, ones_b[:, :], r, start=st, stop=False,
                                        skip_group_check=True))
                sm = p.op("tensor",
                          lambda e, o=pa["smslot"][64:65, :], r=pa["ets"][:, 512:1024],
                          st=first, sp=(kt == nkt - 1):
                          e.matmul(o, ones_b[:, :], r, start=st, stop=sp,
                                   tile_position=(0, 64), skip_group_check=True),
                          inc=tT)
                et_ring.release(pa["eti"], tT, sm)
                return sm

            def emit_norm(pn):
                p.wait("tensor", tV, pn["rc"])
                bt = p.op("tensor", lambda e, o=pn["smslot"][:, :]:
                          e.matmul(o, sel2[0:65, 0:128], recip_sb[0:65, :],
                                   start=True, stop=True, skip_group_check=True),
                          inc=tT)
                state["recip_busy"] = bt
                rslot, rbi = rb_ring.acquire("vector")
                p.wait("vector", tT, bt)
                bv_ = p.op("vector", lambda e, o=rslot[:, :], i=pn["smslot"][:, :]:
                           e.tensor_copy(o, i), inc=tV)
                sm_ring.release(pn["smi"], tV, bv_)
                j_, win0_ = pn["j"], pn["win0"]
                cv = p.op("vector",
                          lambda e, o=ctx[:, j_ * S + win0_: j_ * S + win0_ + 512],
                          i=pn["ctslot"][:, :], r=rslot[:, :]:
                          e.tensor_tensor(o, i, r, ALU.mult), inc=tV)
                rb_ring.release(rbi, tV, cv)
                ct_ring.release(pn["cti"], tV, cv)

            pend_av = []
            pend_norm = None
            sm = 0
            for qh in range(2):
                for ws in range(2):
                    win0 = qh * 1024 + ws * 512
                    nkt = qh * 8 + ws * 4 + 4
                    ctslot, cti = ct_ring.acquire("tensor")
                    smslot, smi = sm_ring.acquire("tensor")
                    for kt in range(nkt):
                        sslot, ssi = ps_ring.acquire("tensor")
                        ets, eti = et_ring.acquire("scalar")
                        off = kt * 128 - win0
                        for (r0, c0) in ((0, 0), (64, 512)):
                            last = (off < 0) and (r0 == 64)
                            p.op("tensor",
                                 lambda e, o=sslot[:, c0:c0 + 512],
                                 l=krb[r0:r0 + 64, kt * 128:(kt + 1) * 128],
                                 r=qrb[r0:r0 + 64, win0:win0 + 512],
                                 sp=(off < 0):
                                 e.matmul(o, l, r, start=True, stop=sp),
                                 inc=(tT if last else None))
                        if off >= 0:
                            scol = 384 - off
                            for c0 in (0, 512):
                                p.op("tensor",
                                     lambda e, o=sslot[:, c0:c0 + 512],
                                     m=trit[:, scol:scol + 512]:
                                     e.matmul(o, identb[:, :], m, start=False,
                                              stop=True),
                                     inc=(tT if c0 == 512 else None))
                        st_t = p.c(tT)
                        p.wait("scalar", tT, st_t)
                        ex = p.op("scalar",
                                  lambda e, o=ets[:, :], i=sslot[:, :]:
                                  e.activation(out=o, in_=i, func=AF.Exp,
                                               bias=cst[:, 0:1], scale=cst[:, 1:2]),
                                  inc=tA)
                        ps_ring.release(ssi, tA, ex)
                        if kt == 3 and pend_norm is not None:
                            emit_norm(pend_norm)
                            pend_norm = None
                        if len(pend_av) >= 2:
                            emit_av(pend_av.pop(0))
                        pend_av.append(dict(ex=ex, ctslot=ctslot, smslot=smslot,
                                            kt=kt, nkt=nkt, ets=ets, eti=eti))
                    for pa in pend_av:
                        sm = emit_av(pa)
                    pend_av = []
                    p.wait("vector", tT, sm)
                    p.wait("vector", tT, state["recip_busy"])
                    p.op("vector", lambda e, i=smslot[0:1, :]:
                         e.reciprocal(recip_sb[0:1, :], i), inc=tV)
                    rc = p.op("vector", lambda e, i=smslot[64:65, :]:
                             e.reciprocal(recip_sb[64:65, :], i), inc=tV)
                    pend_norm = dict(rc=rc, ctslot=ctslot, cti=cti, smslot=smslot,
                                     smi=smi, j=j, win0=win0)
            emit_norm(pend_norm)
            scores_done[j] = p.c(tT)

        emit_projrope(0)
        for j in range(NCH):
            if j + 1 < NCH:
                emit_projrope(j + 1)
            emit_attention(j)
        p.flush_block()
    st_A.close()

    x2 = st_R1.enter_context(nc.sbuf_tensor("x2", [128, NCH * S], F32, side="right"))
    x2bf = st_R2.enter_context(nc.sbuf_tensor("x2bf", [128, NCH * S], BF16, side="right"))
    with ExitStack() as ph:
        wo_sb = ph.enter_context(nc.sbuf_tensor("wo_sb", [128, 8 * 1024], BF16))
        stage = [ph.enter_context(nc.sbuf_tensor(f"stage{i}", [128, 4096], BF16)) for i in range(2)]
        xres = [ph.enter_context(nc.sbuf_tensor("xres0", [128, D], F32))]
        lnb = [ph.enter_context(nc.sbuf_tensor(f"lnb{i}", [128, D], F32)) for i in range(2)]
        lnb2 = [ph.enter_context(nc.sbuf_tensor("lnb20", [128, D], F32))]
        sqs = ph.enter_context(nc.sbuf_tensor("sqs", [128, D], F32))
        stt = ph.enter_context(nc.sbuf_tensor("stt", [128, 8], F32))
        pO = [ph.enter_context(nc.psum_tensor(f"pO{i}", [128, 512], F32)) for i in range(2)]
        pL = [ph.enter_context(nc.psum_tensor(f"pL{i}", [128, D], BF16)) for i in range(2)]
        pB2 = [ph.enter_context(nc.psum_tensor("pB20", [128, D], F32))]

        p.wait("scalar", dC, consts_d)
        p.wait("vector", dC, consts_d)
        p.wait("tensor", dC, consts_d)
        for k in range(8):
            p.dma("gpsimd", wo_sb[:, k * 1024:(k + 1) * 1024],
                  d["wor"][k].rearrange("p k c -> p (k c)"), dWo)
        wo_d = p.c(dWo)
        p.wait("tensor", dWo, wo_d)

        po_ring = Ring(p, pO)
        pl_ring = Ring(p, pL)
        pb_ring = Ring(p, pB2)
        xr_ring = DmaRing(p, xres, "dRx")
        ln_ring = Ring(p, lnb)
        ln2_ring = Ring(p, lnb2)
        stage_ring = Ring(p, stage)

        lnb2 = [ph.enter_context(nc.sbuf_tensor(f"lnb2x{i}", [128, D], F32)) for i in range(3)]
        ln2_ring = Ring(p, lnb2 + ln2_ring.slots)
        pB2b = ph.enter_context(nc.psum_tensor("pB21", [128, D], F32))
        pb_ring = Ring(p, pb_ring.slots + [pB2b])

        def emit_A(g, stg, ev_ticks, out_list):
            for tl in range(4):
                tok0 = g * 512 + tl * 128
                xrs, xri, xtok = xr_ring.load("sync", d["x"][tok0:tok0 + 128, :])
                pls, pli = pl_ring.acquire("tensor")
                for ds in range(8):
                    p.wait("tensor", tA, ev_ticks[ds])
                    p.op("tensor",
                         lambda e, o=pls[:, ds * 128:(ds + 1) * 128],
                         i=stg[:, ds * 512 + tl * 128: ds * 512 + tl * 128 + 128]:
                         e.transpose(o, i, identb[:, :]),
                         inc=(tT if ds == 7 else None))
                tp = p.c(tT)
                lns, lni = ln_ring.acquire("vector")
                p.wait("vector", tT, tp)
                xr_ring.wait_loaded("vector", xtok)
                lv = p.op("vector",
                          lambda e, o=lns[:, :], i=pls[:, :], r=xrs[:, :]:
                          e.scalar_tensor_tensor(o, i, 1.0, r, ALU.mult, ALU.add),
                          inc=tV)
                pl_ring.release(pli, tV, lv)
                xr_ring.release(xri, tV, lv)
                p.wait("scalar", tV, lv)
                a1 = p.op("scalar", lambda e, o=sqs[:, :], i=lns[:, :]:
                          e.activation(out=o, in_=i, func=AF.Square,
                                       accum_out=stt[:, 0:1]), inc=tA)
                v1 = p.op("vector", lambda e, i=lns[:, :]:
                          e.reduce_sum(stt[:, 1:2], i, axis=AX.X, negate=True), inc=tV)
                v2 = p.op("vector", lambda e:
                          e.tensor_scalar_mul(stt[:, 2:3], stt[:, 1:2], 1.0 / D), inc=tV)
                p.wait("vector", tV, v2)
                v3 = p.op("vector", lambda e:
                          e.tensor_tensor(stt[:, 3:4], stt[:, 2:3], stt[:, 2:3], ALU.mult),
                          inc=tV)
                p.wait("vector", tA, a1)
                v4 = p.op("vector", lambda e:
                          e.scalar_tensor_tensor(stt[:, 4:5], stt[:, 0:1], 1.0 / D,
                                                 stt[:, 3:4], ALU.mult, ALU.subtract),
                          inc=tV)
                p.wait("scalar", tV, v4)
                a2 = p.op("scalar", lambda e:
                          e.activation(out=stt[:, 5:6], in_=stt[:, 4:5], func=AF.Sqrt,
                                       bias=cst[:, 2:3]), inc=tA)
                p.wait("vector", tA, a2)
                v5 = p.op("vector", lambda e:
                          e.reciprocal(stt[:, 6:7], stt[:, 5:6]), inc=tV)
                p.wait("vector", tV, v5)
                v6 = p.op("vector", lambda e:
                          e.tensor_scalar_mul(stt[:, 7:8], stt[:, 2:3], stt[:, 6:7]),
                          inc=tV)
                ln2s, ln2i = ln2_ring.acquire("scalar")
                p.wait("scalar", tV, v6)
                a3 = p.op("scalar", lambda e, o=ln2s[:, :], i=lns[:, :]:
                          e.activation(out=o, in_=i, func=AF.Identity,
                                       scale=stt[:, 6:7], bias=stt[:, 7:8]), inc=tA)
                ln_ring.release(lni, tA, a3)
                out_list.append((tok0, ln2s, ln2i, a3))

        def emit_B(items):
            for (tok0, ln2s, ln2i, a3) in items:
                pbs, pbi = pb_ring.acquire("tensor")
                p.wait("tensor", tA, a3)
                for ds in range(8):
                    p.op("tensor",
                         lambda e, o=pbs[:, ds * 128:(ds + 1) * 128],
                         i=ln2s[:, ds * 128:(ds + 1) * 128]:
                         e.transpose(o, i, identf[:, :]),
                         inc=(tT if ds == 7 else None))
                tb = p.c(tT)
                ln2_ring.release(ln2i, tT, tb)
                p.wait("vector", tT, tb)
                x2v = x2.rearrange("p (c s) -> p c s", c=NCH)[:, :, tok0:tok0 + 128]
                vv = p.op("vector", lambda e, o=x2v, i=pbs[:, :]:
                          e.tensor_copy(o, i), inc=tV)
                pb_ring.release(pbi, tV, vv)
                p.wait("scalar", tV, vv)
                x2bv = x2bf.rearrange("p (c s) -> p c s", c=NCH)[:, :, tok0:tok0 + 128]
                p.op("scalar", lambda e, o=x2bv, i=x2v:
                     e.activation(out=o, in_=i, func=AF.Copy), inc=tA)

        pend_B = None
        for g in range(4):
            stg, sgi = stage_ring.acquire("scalar")
            ev_ticks = []
            for ds in range(8):
                slot, si = po_ring.acquire("tensor")
                for k in range(NCH):
                    p.op("tensor",
                         lambda e, o=slot[:, :],
                         l=wo_sb[:, ds * 1024 + k * 128: ds * 1024 + k * 128 + 128],
                         r=ctx[:, k * S + g * 512: k * S + g * 512 + 512],
                         st=(k == 0), sp=(k == NCH - 1):
                         e.matmul(o, l, r, start=st, stop=sp),
                         inc=(tT if k == NCH - 1 else None))
                mm = p.c(tT)
                p.wait("scalar", tT, mm)
                ev = p.op("scalar",
                          lambda e, o=stg[:, ds * 512:(ds + 1) * 512], i=slot[:, :],
                          b=bo_sb[:, ds:ds + 1]:
                          e.activation(out=o, in_=i, func=AF.Identity, bias=b), inc=tA)
                ev_ticks.append(ev)
                po_ring.release(si, tA, ev)
            if pend_B is not None:
                emit_B(pend_B)
                pend_B = None
            items = []
            emit_A(g, stg, ev_ticks, items)
            stage_ring.release(sgi, tT, p.c(tT))
            pend_B = items
        emit_B(pend_B)
        x2bf_done = p.c(tA)
        x2_done = p.c(tV)
        p.flush_block()
    st_B.close()

    with ExitStack() as ph:
        wr1_sb = ph.enter_context(nc.sbuf_tensor("wr1_sb", [128, 2048], BF16))
        wr2_sb = ph.enter_context(nc.sbuf_tensor("wr2_sb", [128, 16], BF16))
        r1bf = ph.enter_context(nc.sbuf_tensor("r1bf", [128, 2 * S], BF16))
        br2b = ph.enter_context(nc.sbuf_tensor("br2b", [128, 8], F32))
        lg = ph.enter_context(nc.sbuf_tensor("lg", [128, 8], F32))
        g8 = ph.enter_context(nc.sbuf_tensor("g8", [128, 8], F32))
        eq = ph.enter_context(nc.sbuf_tensor("eq", [128, 8], F32))
        g2 = ph.enter_context(nc.sbuf_tensor("g2", [128, 8], F32))
        msk = ph.enter_context(nc.sbuf_tensor("msk", [128, 8], F32))
        eh = ph.enter_context(nc.sbuf_tensor("eh", [128, 8], F32))
        wu = ph.enter_context(nc.sbuf_tensor("wu", [128, 8], F32))
        rst = ph.enter_context(nc.sbuf_tensor("rst", [128, 8], F32))
        wtile = ph.enter_context(nc.sbuf_tensor("wtile", [128, 128], F32))
        pR1 = [ph.enter_context(nc.psum_tensor(f"pR1{i}", [128, 512], F32)) for i in range(2)]
        pLg = [ph.enter_context(nc.psum_tensor(f"pLg{i}", [128, 8], F32)) for i in range(2)]
        pWt = ph.enter_context(nc.psum_tensor("pWt", [128, 128], F32))

        p.wait("vector", dC, consts_d)
        p.wait("scalar", dC, consts_d)
        p.wait("tensor", dC, consts_d)
        for j2 in range(2):
            p.dma("gpsimd", wr1_sb[:, j2 * 1024:(j2 + 1) * 1024],
                  d["wr1r"][j2].rearrange("p k c -> p (k c)"), dWr)
        for k2 in range(2):
            p.dma("gpsimd", wr2_sb[:, k2 * 8:(k2 + 1) * 8],
                  d["wr2"][k2 * 128:(k2 + 1) * 128, :], dWr)
        wr_d = p.c(dWr)
        p.wait("tensor", dWr, wr_d)
        bt = p.op("tensor", lambda e: e.matmul(pLg[0][:, :], ones_f[:, :], br2row[0:1, :],
                                               start=True, stop=True), inc=tT)
        p.wait("vector", tT, bt)
        bb = p.op("vector", lambda e: e.tensor_copy(br2b[:, :], pLg[0][:, :]), inc=tV)
        pr_ring = Ring(p, pR1)
        p.wait("tensor", tA, x2bf_done)
        for j2 in range(2):
            for g in range(4):
                slot, si = pr_ring.acquire("tensor")
                for k in range(NCH):
                    p.op("tensor",
                         lambda e, o=slot[:, :],
                         l=wr1_sb[:, j2 * 1024 + k * 128: j2 * 1024 + k * 128 + 128],
                         r=x2bf[:, k * S + g * 512: k * S + g * 512 + 512],
                         st=(k == 0), sp=(k == NCH - 1):
                         e.matmul(o, l, r, start=st, stop=sp),
                         inc=(tT if k == NCH - 1 else None))
                mm = p.c(tT)
                p.wait("scalar", tT, mm)
                ev = p.op("scalar",
                          lambda e, o=r1bf[:, j2 * S + g * 512: j2 * S + g * 512 + 512],
                          i=slot[:, :], b=br1_sb[:, j2:j2 + 1]:
                          e.activation(out=o, in_=i, func=AF.Relu, bias=b), inc=tA)
                pr_ring.release(si, tA, ev)
        r1_done = p.c(tA)
        p.wait("tensor", tA, r1_done)
        p.wait("tensor", tV, bb)

        lgall = ph.enter_context(nc.sbuf_tensor("lgall", [128, 128], F32))
        plg_ring = Ring(p, pLg)
        for tt in range(NT):
            lgs, lgi = plg_ring.acquire("tensor")
            for k2 in range(2):
                p.op("tensor",
                     lambda e, o=lgs[:, :],
                     l=r1bf[:, k2 * S + tt * 128: k2 * S + tt * 128 + 128],
                     r=wr2_sb[:, k2 * 8:(k2 + 1) * 8],
                     st=(k2 == 0), sp=(k2 == 1):
                     e.matmul(o, l, r, start=st, stop=sp),
                     inc=(tT if k2 == 1 else None))
            lt = p.c(tT)
            p.wait("vector", tT, lt)
            lgr = p.op("vector", lambda e, o=lgall[:, tt * 8:(tt + 1) * 8], i=lgs[:, :]:
                       e.tensor_copy(o, i), inc=tV)
            plg_ring.release(lgi, tV, lgr)

        def t16(buf):
            return buf.rearrange("p (t e) -> p t e", t=16)

        st16 = ph.enter_context(nc.sbuf_tensor("st16", [128, 96], F32))

        def s16(i):
            return st16[:, i * 16:(i + 1) * 16]

        def bc16(i):
            return bass.AP(st16, i * 16, [[96, 128], [1, 16], [0, 8]])

        p.wait("vector", tV, bb)
        br2bc = bass.AP(br2b, 0, [[8, 128], [0, 16], [1, 8]])
        p.op("vector", lambda e: e.tensor_tensor(t16(lgall), t16(lgall), br2bc, ALU.add),
             inc=tV)
        p.op("vector", lambda e: e.reduce_max(s16(0), t16(lgall), axis=AX.X), inc=tV)
        m1t = p.c(tV)
        p.op("vector", lambda e: e.tensor_tensor(t16(lgall), t16(lgall), bc16(0),
                                                 ALU.subtract), inc=tV)
        sb_t = p.c(tV)
        p.wait("scalar", tV, sb_t)
        exa = p.op("scalar", lambda e: e.activation(out=lgall[:, :], in_=lgall[:, :],
                                                    func=AF.Exp), inc=tA)
        p.wait("vector", tA, exa)
        p.op("vector", lambda e: e.reduce_sum(s16(1), t16(lgall), axis=AX.X), inc=tV)
        p.op("vector", lambda e: e.reciprocal(s16(2), s16(1)), inc=tV)
        p.op("vector", lambda e: e.tensor_tensor(t16(lgall), t16(lgall), bc16(2),
                                                 ALU.mult), inc=tV)
        p.op("vector", lambda e: e.reduce_max(s16(3), t16(lgall), axis=AX.X), inc=tV)
        p.op("vector", lambda e, o=wtile[:, :]: e.tensor_tensor(t16(o), t16(lgall),
                                                                bc16(3), ALU.is_equal),
             inc=tV)
        p.op("vector", lambda e, o=wtile[:, :]:
             e.scalar_tensor_tensor(t16(o), t16(o), -1e30, t16(lgall), ALU.mult, ALU.add),
             inc=tV)
        p.op("vector", lambda e, o=wtile[:, :]: e.reduce_max(s16(4), t16(o), axis=AX.X),
             inc=tV)
        p.op("vector", lambda e, o=wtile[:, :]: e.tensor_tensor(t16(o), t16(lgall),
                                                                bc16(4), ALU.is_ge),
             inc=tV)
        mk16 = p.c(tV)
        p.wait("scalar", tV, mk16)
        eh16 = p.op("scalar", lambda e: e.activation(out=lgall[:, :], in_=lgall[:, :],
                                                     func=AF.Exp, scale=cst[:, 3:4]),
                    inc=tA)
        p.wait("vector", tA, eh16)
        p.op("vector", lambda e, o=wtile[:, :]: e.tensor_tensor(o, o, lgall[:, :],
                                                                ALU.mult), inc=tV)
        p.op("vector", lambda e, o=wtile[:, :]: e.reduce_sum(s16(5), t16(o), axis=AX.X),
             inc=tV)
        p.op("vector", lambda e: e.reciprocal(s16(0), s16(5)), inc=tV)
        wv_t = p.op("vector", lambda e, o=wtile[:, :]:
                    e.tensor_tensor(t16(o), t16(o), bc16(0), ALU.mult), inc=tV)
        p.wait("tensor", tV, wv_t)
        for tt in range(NT):
            wt_t = p.op("tensor", lambda e, i=wtile[:, tt * 8:(tt + 1) * 8]:
                        e.transpose(pWt[0:8, 0:128], i, identf[:, :]), inc=tT)
            p.wait("vector", tT, wt_t)
            wc = p.op("vector", lambda e, o=wT[0:8, tt * 128:(tt + 1) * 128]:
                      e.tensor_copy(o, pWt[0:8, 0:128]), inc=tV)
            p.wait("tensor", tV, wc)
        p.flush_block()

    with ExitStack() as ph:
        h1bf = ph.enter_context(nc.sbuf_tensor("h1bf", [128, 16 * S], BF16))
        w1s = [ph.enter_context(nc.sbuf_tensor(f"w1s{i}", [128, 4096], BF16)) for i in range(2)]
        w2s = [ph.enter_context(nc.sbuf_tensor(f"w2s{i}", [128, 4096], BF16)) for i in range(2)]
        wb = [ph.enter_context(nc.sbuf_tensor("wb0", [128, S], BF16))] * 2
        tmp5 = [ph.enter_context(nc.sbuf_tensor("tmp50", [128, 512], F32))]
        pE1 = [ph.enter_context(nc.psum_tensor(f"pE1{i}", [128, 512], F32)) for i in range(3)]
        pE2 = [ph.enter_context(nc.psum_tensor(f"pE2{i}", [128, 512], F32)) for i in range(3)]
        pWb = ph.enter_context(nc.psum_tensor("pWb", [128, 512], F32))

        w1_ring = DmaRing(p, w1s, "dW1")
        w2_ring = DmaRing(p, w2s, "dW2")
        p1_ring = Ring(p, pE1)
        p2_ring = Ring(p, pE2)
        tmp_ring = Ring(p, tmp5)
        pwb_free = 0
        shared_evac_done = 0
        x2_add_tick = {}

        p.wait("scalar", dC, consts_d)
        p.wait("vector", dC, consts_d)
        p.wait("tensor", tA, x2bf_done)
        p.wait("vector", tV, x2_done)

        h1_consumed = 0
        for e_i in range(NE):
            if e_i >= 1:
                wb_ready = 0
                for g in range(4):
                    p.wait("tensor", tV, pwb_free)
                    bt = p.op("tensor",
                              lambda e, l=sel8[0:8, (e_i - 1) * 128: e_i * 128],
                              r=wT[0:8, g * 512:(g + 1) * 512]:
                              e.matmul(pWb[:, :], l, r, start=True, stop=True),
                              inc=tT)
                    p.wait("vector", tT, bt)
                    bv2 = p.op("vector", lambda e, o=wb[e_i % 2][:, g * 512:(g + 1) * 512]:
                               e.tensor_copy(o, pWb[:, :]), inc=tV)
                    pwb_free = bv2
                wb_ready = p.c(tV)
            for Hh in range(2):
                for ocg in range(4):
                    if Hh == 0 and ocg == 0:
                        p.wait("sync", dPC[e_i], 64)
                    slot, si, w1tok = w1_ring.load(
                        "sync", w1bf[e_i, Hh * 4 + ocg])
                    w1_ring.wait_loaded("tensor", w1tok)
                    for oi in range(4):
                        oc = ocg * 4 + oi
                        goc = Hh * 16 + oc
                        for g in range(4):
                            ps1, pi1 = p1_ring.acquire("tensor")
                            for k in range(NCH):
                                p.op("tensor",
                                     lambda e, o=ps1[:, :],
                                     l=slot[:, oi * 1024 + k * 128: oi * 1024 + k * 128 + 128],
                                     r=x2bf[:, k * S + g * 512: k * S + g * 512 + 512],
                                     st=(k == 0), sp=(k == NCH - 1):
                                     e.matmul(o, l, r, start=st, stop=sp),
                                     inc=(tT if k == NCH - 1 else None))
                            mm = p.c(tT)
                            if g == 3 and oi == 3:
                                w1_ring.release(si, tT, mm)
                            p.wait("scalar", tT, mm)
                            if oc == 0 and g == 0 and (Hh == 1 or e_i > 0):
                                p.wait("scalar", tT, h1_consumed)
                            ev = p.op("scalar",
                                      lambda e, o=h1bf[:, oc * S + g * 512: oc * S + g * 512 + 512],
                                      i=ps1[:, :], b=b1_sb[:, e_i * 32 + goc: e_i * 32 + goc + 1]:
                                      e.activation(out=o, in_=i, func=AF.Silu, bias=b), inc=tA)
                            p1_ring.release(pi1, tA, ev)
                h1_done = p.c(tA)
                for ddg in range(4):
                    slot2, si2, w2tok = w2_ring.load("sync", w2bf[e_i, Hh, ddg])
                    w2_ring.wait_loaded("tensor", w2tok)
                    if ddg == 0:
                        p.wait("tensor", tA, h1_done)
                    for di in range(2):
                      dd = ddg * 2 + di
                      for g in range(4):
                        ps2, pi2 = p2_ring.acquire("tensor")
                        for kc in range(16):
                            p.op("tensor",
                                 lambda e, o=ps2[:, :],
                                 l=slot2[:, di * 2048 + kc * 128: di * 2048 + kc * 128 + 128],
                                 r=h1bf[:, kc * S + g * 512: kc * S + g * 512 + 512],
                                 st=(kc == 0), sp=(kc == 15):
                                 e.matmul(o, l, r, start=st, stop=sp),
                                 inc=(tT if kc == 15 else None))
                        mm2 = p.c(tT)
                        if g == 3 and di == 1:
                            w2_ring.release(si2, tT, mm2)
                        x2slice = x2[:, dd * S + g * 512: dd * S + g * 512 + 512]
                        b2ap = (b2_sb[:, e_i * 8 + dd: e_i * 8 + dd + 1]
                                if Hh == 1 else 0.0)
                        p.wait("vector", tT, mm2)
                        if e_i == 0:
                            ev2 = p.op("vector",
                                       lambda e, o=x2slice, i=ps2[:, :], b=b2ap:
                                       e.scalar_tensor_tensor(o, i, b, o, ALU.add, ALU.add),
                                       inc=tV)
                            p2_ring.release(pi2, tV, ev2)
                            shared_evac_done = ev2
                        else:
                            if dd == 0 and g == 0 and Hh == 0:
                                p.wait("vector", tV, wb_ready)
                            tms, tmi = tmp_ring.acquire("vector")
                            ev2 = p.op("vector",
                                       lambda e, o=tms[:, :], i=ps2[:, :], b=b2ap,
                                       w=wb[e_i % 2][:, g * 512:(g + 1) * 512]:
                                       e.scalar_tensor_tensor(o, i, b, w, ALU.add, ALU.mult),
                                       inc=tV)
                            p2_ring.release(pi2, tV, ev2)
                            p.wait("gpsimd", tV, ev2)
                            if e_i == 1 and dd == 0 and g == 0 and Hh == 0:
                                p.wait("gpsimd", tV, shared_evac_done)
                            p.wait("gpsimd", tG, x2_add_tick.get((dd, g), 0))
                            ga = p.op("gpsimd",
                                      lambda e, o=x2slice, t=tms[:, :]:
                                      e.tensor_tensor(o, o, t, ALU.add), inc=tG)
                            x2_add_tick[(dd, g)] = ga
                            tmp_ring.release(tmi, tG, ga)
                h1_consumed = p.c(tT)
        ffn_done_g = p.c(tG)
        ffn_done_v = p.c(tV)
        p.flush_block()
    st_R2.close()

    with ExitStack() as ph:
        lnb = [ph.enter_context(nc.sbuf_tensor(f"lnc{i}", [128, D], F32)) for i in range(2)]
        ybuf = [ph.enter_context(nc.sbuf_tensor(f"ybuf{i}", [128, D], F32)) for i in range(2)]
        sqs = ph.enter_context(nc.sbuf_tensor("sqs2", [128, D], F32))
        stt = ph.enter_context(nc.sbuf_tensor("stt2", [128, 8], F32))
        pL = [ph.enter_context(nc.psum_tensor(f"pL2{i}", [128, D], F32)) for i in range(2)]
        dY = [p.sem("dY0"), p.sem("dY1")]

        pl_ring = Ring(p, pL)
        ln_ring = Ring(p, lnb)
        p.wait("scalar", dC, consts_d)
        p.wait("vector", dC, consts_d)
        p.wait("tensor", dC, consts_d)
        for tt in range(NT):
            pls, pli = pl_ring.acquire("tensor")
            for ds in range(8):
                p.op("tensor",
                     lambda e, o=pls[:, ds * 128:(ds + 1) * 128],
                     i=x2[:, ds * S + tt * 128: ds * S + tt * 128 + 128]:
                     e.transpose(o, i, identf[:, :]),
                     inc=(tT if ds == 7 else None))
            tp = p.c(tT)
            lns, lni = ln_ring.acquire("vector")
            p.wait("vector", tT, tp)
            lv = p.op("vector", lambda e, o=lns[:, :], i=pls[:, :]:
                      e.tensor_copy(o, i), inc=tV)
            pl_ring.release(pli, tV, lv)
            p.wait("scalar", tV, lv)
            a1 = p.op("scalar", lambda e, o=sqs[:, :], i=lns[:, :]:
                      e.activation(out=o, in_=i, func=AF.Square,
                                   accum_out=stt[:, 0:1]), inc=tA)
            v1 = p.op("vector", lambda e, i=lns[:, :]:
                      e.reduce_sum(stt[:, 1:2], i, axis=AX.X, negate=True), inc=tV)
            v2 = p.op("vector", lambda e:
                      e.tensor_scalar_mul(stt[:, 2:3], stt[:, 1:2], 1.0 / D), inc=tV)
            p.wait("vector", tV, v2)
            v3 = p.op("vector", lambda e:
                      e.tensor_tensor(stt[:, 3:4], stt[:, 2:3], stt[:, 2:3], ALU.mult),
                      inc=tV)
            p.wait("vector", tA, a1)
            v4 = p.op("vector", lambda e:
                      e.scalar_tensor_tensor(stt[:, 4:5], stt[:, 0:1], 1.0 / D,
                                             stt[:, 3:4], ALU.mult, ALU.subtract), inc=tV)
            p.wait("scalar", tV, v4)
            a2 = p.op("scalar", lambda e:
                      e.activation(out=stt[:, 5:6], in_=stt[:, 4:5], func=AF.Sqrt,
                                   bias=cst[:, 2:3]), inc=tA)
            p.wait("vector", tA, a2)
            v5 = p.op("vector", lambda e: e.reciprocal(stt[:, 6:7], stt[:, 5:6]), inc=tV)
            p.wait("vector", tV, v5)
            v6 = p.op("vector", lambda e:
                      e.tensor_scalar_mul(stt[:, 7:8], stt[:, 2:3], stt[:, 6:7]), inc=tV)
            p.wait("scalar", tV, v6)
            if tt >= 2:
                p.wait("scalar", dY[tt % 2], (tt // 2) * 16)
            ybs = ybuf[tt % 2]
            a3 = p.op("scalar", lambda e, o=ybs[:, :], i=lns[:, :]:
                      e.activation(out=o, in_=i, func=AF.Identity,
                                   scale=stt[:, 6:7], bias=stt[:, 7:8]), inc=tA)
            ln_ring.release(lni, tA, a3)
            p.wait("sync", tA, a3)
            p.dma("sync", y[tt * 128:(tt + 1) * 128, :], ybs[:, :], dY[tt % 2])
        p.wait("sync", dY[0], p.c(dY[0]))
        p.wait("sync", dY[1], p.c(dY[1]))
        p.flush_block()

    st_R1.close()
    es.close()
    return nc


def _host_pack(inputs):
    f32 = np.float32
    bf = ml_dtypes.bfloat16
    ca = np.ascontiguousarray
    ins = {}
    ins["wv"] = ca(inputs["L0_attn_Wv"], dtype=f32)
    for nm, key in (("wqr", "L0_attn_Wq"), ("wkr", "L0_attn_Wk"), ("wor", "L0_attn_Wo")):
        w = np.asarray(inputs[key], dtype=f32)
        ins[nm] = ca(w.reshape(8, 128, 8, 128).transpose(2, 1, 0, 3))
    ins["bqr"] = ca(np.asarray(inputs["L0_attn_bq"], dtype=f32).reshape(8, 128).T)
    ins["bkr"] = ca(np.asarray(inputs["L0_attn_bk"], dtype=f32).reshape(8, 128).T)
    ins["bor"] = ca(np.asarray(inputs["L0_attn_bo"], dtype=f32).reshape(8, 128).T)
    ins["bv"] = ca(np.asarray(inputs["L0_attn_bv"], dtype=f32).reshape(1, D))
    ins["wr1r"] = ca(np.asarray(inputs["L0_ffn_Wr1"], dtype=f32)
                     .reshape(8, 128, 2, 128).transpose(2, 1, 0, 3))
    ins["wr2"] = ca(inputs["L0_ffn_Wr2"], dtype=f32)
    ins["br1r"] = ca(np.asarray(inputs["L0_ffn_br1"], dtype=f32).reshape(2, 128).T)
    ins["br2"] = ca(np.asarray(inputs["L0_ffn_br2"], dtype=f32).reshape(1, 8))
    w1 = np.concatenate([np.asarray(inputs["L0_ffn_Ws1"], dtype=f32)[None],
                         np.asarray(inputs["L0_ffn_We1"], dtype=f32)], axis=0)
    ins["w1r"] = ca(w1.reshape(NE, 8, 128, 8, 4, 128).transpose(0, 3, 2, 4, 1, 5)
                    .reshape(NE, 8, 128, 4096))
    w2 = np.concatenate([np.asarray(inputs["L0_ffn_Ws2"], dtype=f32)[None],
                         np.asarray(inputs["L0_ffn_We2"], dtype=f32)], axis=0)
    ins["w2r"] = ca(w2.reshape(NE, 2, 16, 128, 4, 2, 128).transpose(0, 1, 4, 3, 5, 2, 6)
                    .reshape(NE, 2, 4, 128, 4096))
    b1 = np.concatenate([np.asarray(inputs["L0_ffn_bs1"], dtype=f32)[None],
                         np.asarray(inputs["L0_ffn_be1"], dtype=f32)], axis=0)
    ins["b1r"] = ca(b1.reshape(NE, 32, 128).transpose(2, 0, 1).reshape(128, NE * 32))
    b2 = np.concatenate([np.asarray(inputs["L0_ffn_bs2"], dtype=f32)[None],
                         np.asarray(inputs["L0_ffn_be2"], dtype=f32)], axis=0)
    ins["b2r"] = ca(b2.reshape(NE, 8, 128).transpose(2, 0, 1).reshape(128, NE * 8))

    hd = 64
    pos = np.arange(S, dtype=f32)
    inv_freq = np.exp(np.arange(0, hd, 2, dtype=f32) * f32(-np.log(10000.0) / hd))
    ang = pos[:, None] * inv_freq[None, :]
    sin = np.sin(ang.astype(f32)).astype(f32)
    cos = np.cos(ang.astype(f32)).astype(f32)
    pmod = np.arange(128) % 32
    ins["ropec"] = ca(cos[:, pmod].T)
    sgn = np.where((np.arange(128) % 64) < 32, -1.0, 1.0).astype(f32)
    ins["ropes"] = ca(sin[:, pmod].T * sgn[:, None])
    P = np.zeros((128, 128), dtype=f32)
    for m in range(128):
        src = m + 32 if (m % 64) < 32 else m - 32
        P[src, m] = 1.0
    ins["prm"] = ca(P.astype(bf))
    ins["identb"] = ca(np.eye(128, dtype=bf))
    ins["identf"] = ca(np.eye(128, dtype=f32))
    k_idx = np.arange(128)[:, None]
    i_idx = np.arange(896)[None, :]
    ins["trit"] = ca(np.where(i_idx < k_idx + 384, -10000.0, 0.0).astype(bf))
    sel = np.zeros((8, 8, 128), dtype=f32)
    for e in range(8):
        sel[e, e, :] = 1.0
    ins["sel8"] = ca(sel.transpose(1, 0, 2).reshape(8, 1024).astype(bf))
    s2 = np.zeros((65, 128), dtype=f32)
    s2[0, 0:64] = 1.0
    s2[64, 64:128] = 1.0
    ins["sel2"] = ca(s2)
    return ins


def kernel(**inputs):
    ins = _host_pack(inputs)
    x = np.asarray(inputs["L0_attn_x"], dtype=np.float32)
    nc = build_nc()
    in_maps = []
    for b in range(8):
        m = dict(ins)
        m["x"] = np.ascontiguousarray(x[b])
        in_maps.append(m)
    res = run_bass_kernel_spmd(nc, in_maps, core_ids=list(range(8)))
    return np.stack([r["y"] for r in res.results], axis=0)
